# version 22
# baseline (speedup 1.0000x reference)
import numpy as np
from contextlib import ExitStack

import concourse.bass as bass
import concourse.mybir as mybir
from concourse.bass_utils import run_bass_kernel_spmd

F32 = mybir.dt.float32
BF16 = mybir.dt.bfloat16
I16 = mybir.dt.int16
U16 = mybir.dt.uint16
OP = mybir.AluOpType
AF = mybir.ActivationFunctionType
AX = mybir.AxisListType

N_CORES = 8
SEQ = 4096
D = 1024
H = 16
IF = 3072
NT = SEQ // 128
NCH = SEQ // 512
DC = D // 128
IC = IF // 128
QMAX = 32767.0
EPS = 1e-5

ENGS = ("tensor", "vector", "scalar", "gpsimd", "sync")


class Prog:

    def __init__(self, nc, es):
        self.nc = nc
        self.es = es
        self.eng = {n: dict(sem=es.enter_context(nc.semaphore(f"es_{n}")),
                            cnt=0, q=[]) for n in ENGS}
        self.know = {e: {} for e in ENGS}
        self.buf = {}
        self.streams = {}

    def _sem(self, kind, name):
        return self.eng[name]["sem"] if kind == "eng" else self.streams[name]["sem"]

    def _wait(self, consumer, kind, name, val):
        if kind == "eng" and name == consumer and val > self.eng[name]["cnt"]:
            return
        k = self.know[consumer]
        if k.get((kind, name), 0) >= val:
            return
        k[(kind, name)] = val
        self.eng[consumer]["q"].append(("wait", self._sem(kind, name), val))

    def _deps(self, consumer, reads, writes):
        for key in reads:
            b = self.buf.get(key)
            if b:
                for (kind, name), cnt in b["w"].items():
                    self._wait(consumer, kind, name, cnt)
        for key in writes:
            b = self.buf.get(key)
            if b:
                for (kind, name), cnt in b["w"].items():
                    self._wait(consumer, kind, name, cnt)
                for (kind, name), cnt in b["r"].items():
                    self._wait(consumer, kind, name, cnt)

    def _record(self, kind, pname, reads, writes, cnt):
        for key in writes:
            self.buf[key] = dict(w={(kind, pname): cnt}, r={})
        for key in reads:
            b = self.buf.setdefault(key, dict(w={}, r={}))
            b["r"][(kind, pname)] = cnt

    def op(self, ename, fn, reads=(), writes=(), quiet=False):
        e = self.eng[ename]
        self._deps(ename, reads, writes)
        if quiet:
            e["q"].append(("raw", lambda h, fn=fn: fn(h)))
            e.setdefault("pending", False)
            e["pending"] = True
            self._record("eng", ename, reads, writes, e["cnt"] + 1)
            return
        e["cnt"] += 1
        e["pending"] = False
        e["q"].append(("op", fn, e["sem"], 1))
        self._record("eng", ename, reads, writes, e["cnt"])

    def flush_quiet(self):
        for ename, e in self.eng.items():
            if e.get("pending"):
                e["cnt"] += 1
                e["pending"] = False
                e["q"].append(("op", lambda h: h.nop(), e["sem"], 1))

    def _stream(self, name):
        if name not in self.streams:
            self.streams[name] = dict(
                sem=self.es.enter_context(self.nc.semaphore(f"ds_{name}")), cnt=0)
        return self.streams[name]

    def dma(self, qname, stream, fn, reads=(), writes=()):
        s = self._stream(stream)
        if s["cnt"] > 0:
            self._wait(qname, "stream", stream, s["cnt"])
        self._deps(qname, reads, writes)
        s["cnt"] += 16
        self.eng[qname]["q"].append(("op", fn, s["sem"], 16))
        self._record("stream", stream, reads, writes, s["cnt"])

    def cc(self, stream, fn, reads=(), writes=()):
        s = self._stream(stream)
        if s["cnt"] > 0:
            self._wait("gpsimd", "stream", stream, s["cnt"])
        self._deps("gpsimd", reads, writes)
        s["cnt"] += 1
        self.eng["gpsimd"]["q"].append(("op", fn, s["sem"], 1))
        self._record("stream", stream, reads, writes, s["cnt"])

    def barrier(self, exclude=()):
        self.flush_quiet()
        targets = [("eng", n, self.eng[n]["cnt"]) for n in ENGS
                   if self.eng[n]["cnt"] > 0]
        targets += [("stream", sn, s["cnt"]) for sn, s in self.streams.items()
                    if s["cnt"] > 0 and not any(sn.startswith(e) for e in exclude)]
        for consumer in ENGS:
            for kind, name, cnt in targets:
                if kind == "eng" and name == consumer:
                    continue
                self._wait(consumer, kind, name, cnt)

    def replay(self):
        with self.nc.Block() as block:
            def mk(name):
                q = self.eng[name]["q"]
                def body(h):
                    for it in q:
                        if it[0] == "wait":
                            h.wait_ge(it[1], it[2])
                        elif it[0] == "raw":
                            it[1](h)
                        else:
                            it[1](h).then_inc(it[2], it[3])
                return body
            block.tensor(mk("tensor"))
            block.vector(mk("vector"))
            block.scalar(mk("scalar"))
            block.gpsimd(mk("gpsimd"))
            block.sync(mk("sync"))


def _keys(prefix, off, length, gran):
    return [f"{prefix}{i}" for i in range(off // gran, (off + length - 1) // gran + 1)]


def build_nc(stage="full", debug=()):
    nc = bass.Bass("TRN2", target_bir_lowering=False)
    es = ExitStack()

    x_h = nc.dram_tensor("x", [SEQ, D], F32, kind="ExternalInput").ap()
    w_attn_h = [nc.dram_tensor(n, [D, D], F32, kind="ExternalInput").ap()
                for n in ("wq", "wk", "wv", "wo")]
    b_attn_h = [nc.dram_tensor(n, [D], F32, kind="ExternalInput").ap()
                for n in ("bq", "bk", "bv", "bo")]
    stdp_h = nc.dram_tensor("stdp", [H], F32, kind="ExternalInput").ap()
    wg_h = nc.dram_tensor("wg", [IF, D], F32, kind="ExternalInput").ap()
    wu_h = nc.dram_tensor("wu", [IF, D], F32, kind="ExternalInput").ap()
    wd_h = nc.dram_tensor("wd", [D, IF], F32, kind="ExternalInput").ap()
    out_h = nc.dram_tensor("out", [SEQ, D], F32, kind="ExternalOutput").ap()

    s_wattn = nc.dram_tensor("s_wattn", [4 * D, D], BF16).ap()
    s_ln1q = nc.dram_tensor("s_ln1q", [SEQ, D], BF16).ap()
    s_qT = nc.dram_tensor("s_qT", [D, SEQ], BF16).ap()
    s_ln2 = nc.dram_tensor("s_ln2", [SEQ, D], BF16).ap()
    s_ln2T = nc.dram_tensor("s_ln2T", [D, SEQ], BF16).ap()
    s_x1 = nc.dram_tensor("s_x1", [SEQ, D], F32).ap()
    s_wu = nc.dram_tensor("s_wu", [IF, D], BF16).ap()
    s_wg = nc.dram_tensor("s_wg", [IF, D], BF16).ap()
    s_wd = nc.dram_tensor("s_wd", [D, IF], BF16).ap()
    s_up = nc.dram_tensor("s_up", [IC, NCH, 128, 512], BF16).ap()
    s_gate = nc.dram_tensor("s_gate", [IC, NCH, 128, 512], BF16).ap()
    s_g3 = nc.dram_tensor("s_g3", [IC, NCH, 128, 512], BF16).ap()
    s_p = nc.dram_tensor("s_p", [IC, NCH, 128, 512], BF16).ap()
    s_o2T = nc.dram_tensor("s_o2T", [D, SEQ], BF16).ap()
    s_part = [nc.dram_tensor(f"s_part{j}", [128, 1], F32).ap() for j in range(2)]
    cc_in = nc.dram_tensor("cc_in", [1, 4], F32).ap()
    cc_out = nc.dram_tensor("cc_out", [1, 4], F32).ap()

    dbg = {}
    def dbg_decl(nm, shp, dt=BF16):
        dbg[nm] = nc.dram_tensor(nm, shp, dt, kind="ExternalOutput").ap()
    if "ln1q" in debug: dbg_decl("d_ln1q", [SEQ, D])
    if "qT" in debug: dbg_decl("d_qT", [D, SEQ])
    if "xqT" in debug:
        dbg_decl("d_xqT", [128, SEQ])
        dbg_decl("d_wupT", [128, 3072])
    if "z0" in debug: dbg_decl("d_z0", [128, 512], F32)
    if "kv" in debug: dbg_decl("d_kv", [128, 512], F32)
    if "ksum" in debug: dbg_decl("d_ksum", [128, 8], F32)
    if "x1" in debug: dbg_decl("d_x1", [SEQ, D], F32)
    if "ln2" in debug: dbg_decl("d_ln2", [SEQ, D])
    if "attnT" in debug: dbg_decl("d_attnT", [D, SEQ])
    if "scal" in debug:
        dbg_decl("d_scal", [1, 24], F32)
        dbg_decl("d_scal2", [1, 12], F32)
    if "upg" in debug:
        dbg_decl("d_up", [IC, NCH, 128, 512])
        dbg_decl("d_gate", [IC, NCH, 128, 512])
    if "g3p" in debug:
        dbg_decl("d_g3", [IC, NCH, 128, 512])
        dbg_decl("d_p", [IC, NCH, 128, 512])
    if "wdq" in debug:
        dbg_decl("d_wu", [IF, D])
        dbg_decl("d_wd", [D, IF])
    if "o2T" in debug: dbg_decl("d_o2T", [D, SEQ])

    sb = lambda name, shape, dt: es.enter_context(nc.sbuf_tensor(name, shape, dt))

    A1 = sb("A1", [128, 8 * SEQ], BF16)
    A2 = sb("A2", [128, 8 * SEQ], BF16)
    W1 = sb("W1", [128, 24 * 1024], BF16)
    stats = sb("stats", [128, 16], F32)
    consts = sb("consts", [128, 2], F32)
    scales = sb("scales", [128, 12], F32)
    mroll = sb("mroll", [128, 8], F32)
    small = sb("small", [128, 8], F32)
    kv_sb = sb("kv_sb", [128, 512], BF16)
    ksumbd = sb("ksumbd", [128, 8 * 16], BF16)
    e16f = sb("e16f", [16, 8 * 128], BF16)
    rden = sb("rden", [16, 512], F32)
    rdenb = sb("rdenb", [16, 512], BF16)
    onesE = sb("onesE", [128, 512], BF16)
    onesf = sb("onesf", [1, 128], F32)
    brow4 = sb("brow4", [128, D], BF16)
    stdp_s = sb("stdp_s", [16, 1], F32)
    stdp_sig = sb("stdp_sig", [16, 1], F32)
    sc11 = sb("sc11", [1, 24], F32)
    mxcol = sb("mxcol", [128, 420], F32)
    prow = sb("prow", [1, 256], F32)
    idf = sb("idf", [128, 128], F32)
    W1x = sb("W1x", [128, 8768], BF16)
    i16c = [W1x[:, 0:2048].bitcast(I16), W1x[:, 2048:4096].bitcast(I16)]
    sstg4 = [W1x[:, 4096 + 512 * i:4096 + 512 * (i + 1)] for i in range(4)]

    pb = [es.enter_context(nc.psum_tensor(f"pb{i}", [128, 512], F32))
          for i in range(8)]
    PS = [f"ps{i}" for i in range(8)]

    p = Prog(nc, es)

    def vf32(arena, off, n):
        return arena[:, off:off + 2 * n].bitcast(F32)
    def vbf(arena, off, n):
        return arena[:, off:off + n]
    def vi16(arena, off, n):
        return arena[:, off:off + n].bitcast(I16)
    def vu16(arena, off, n):
        return arena[:, off:off + n].bitcast(U16)

    def w1k(off, length=128):
        return _keys("W1_", off, length, 1024)
    def a1k(c):
        return [f"A1c{c}"]
    def a2k(off, length):
        return _keys("A2c", off, length, 4096)

    for _t in range(2):
        def f_pf(h, _t=_t):
            return h.dma_start(out=vf32(A2, _t * 2048, 1024),
                               in_=x_h[_t * 128:(_t + 1) * 128, :])
        p.dma("sync", f"xin{_t}", f_pf, writes=[f"xin{_t}"])
    p.op("vector", lambda h: h.memset(consts[:, 0:1], EPS), writes=["c_eps"])
    p.op("vector", lambda h: h.memset(sc11[:, :], 0.0), writes=["sc11z"])
    p.op("vector", lambda h: h.memset(onesE[:, :], 1.0), writes=["onesE"])
    p.op("vector", lambda h: h.memset(onesf[:, :], 1.0), writes=["onesf"])
    p.op("vector", lambda h: h.memset(brow4[:, :], 0.0), writes=["brow"])
    e16i = A1[0:16, 0:2048].bitcast(mybir.dt.int32)
    def f_idi(h):
        return h.iota(out=idf[:, :].bitcast(mybir.dt.int32),
                      pattern=[[1, 128]], base=0, channel_multiplier=-1)
    p.op("gpsimd", f_idi, writes=["idfi"])
    p.op("vector", lambda h: h.tensor_scalar(
        out=idf[:, :], in0=idf[:, :].bitcast(mybir.dt.int32), scalar1=0.0,
        scalar2=None, op0=OP.is_equal), reads=["idfi"], writes=["idf"])
    def f_ei(h):
        return h.iota(out=e16i, pattern=[[2, 8], [1, 2], [0, 64]],
                      base=0, channel_multiplier=-1)
    p.op("gpsimd", f_ei, writes=["e16i", "A1c0"])
    p.op("vector", lambda h: h.tensor_scalar(
        out=e16f[:, :], in0=e16i, scalar1=0.0,
        scalar2=None, op0=OP.is_equal), reads=["e16i", "A1c0"],
        writes=["e16", "A1c0"])
    for i in range(4):
        def f_wc(h, i=i):
            return h.dma_start(out=s_wattn[i * D:(i + 1) * D, :], in_=w_attn_h[i])
        p.dma("gpsimd", "wcast", f_wc, writes=[f"s_wattn{i}"])
        def f_bc(h, i=i):
            return h.dma_start(out=brow4[32 * i:32 * i + 1, :],
                               in_=b_attn_h[i].rearrange("(a d) -> a d", a=1))
        p.dma("gpsimd", "bload", f_bc, reads=["brow"], writes=["brow"])
    p.dma("gpsimd", "bload", lambda h: h.dma_start(
        out=stdp_s[:, :], in_=stdp_h.rearrange("(h a) -> h a", a=1)), writes=["stdp_s"])
    p.op("scalar", lambda h: h.activation(out=small[0:16, 0:1], in_=stdp_s[:, :],
                                          func=AF.Exp, bias=0.0, scale=-1.0),
         reads=["stdp_s"], writes=["sp_e"])
    p.op("vector", lambda h: h.tensor_scalar(
        out=small[0:16, 1:2], in0=small[0:16, 0:1], scalar1=1.0, scalar2=None,
        op0=OP.add), reads=["sp_e"], writes=["sp_e1"])
    p.op("vector", lambda h: h.reciprocal(out=stdp_sig[:, :], in_=small[0:16, 1:2]),
         reads=["sp_e1"], writes=["stdp_sig"])
    for wi in range(3):
        for c in range(DC):
            def f_wT(h, wi=wi, c=c):
                return h.dma_start(
                    out=W1[:, wi * 8192 + c * 1024: wi * 8192 + (c + 1) * 1024],
                    in_=s_wattn[wi * D:(wi + 1) * D, c * 128:(c + 1) * 128],
                    transpose=True)
            p.dma("scalar", "xbar", f_wT, reads=[f"s_wattn{wi}"],
                  writes=w1k(wi * 8192 + c * 1024, 1024))

    xin = [vf32(A2, 0, 1024), vf32(A2, 2048, 1024)]
    wrk = [vf32(A2, 4096, 1024), vf32(A2, 6144, 1024)]
    wrk2 = [vf32(A2, 8192 + 1024 * i, 512) for i in range(4)]
    bstg = [vbf(A2, 12288, 1024), vbf(A2, 13312, 1024)]
    ktile = [vbf(A2, 14336, 1024), vbf(A2, 15360, 1024)]
    vtile = [vbf(A2, 16384, 1024), vbf(A2, 17408, 1024)]
    qstg = [vbf(A2, 18432, 512), vbf(A2, 18944, 512)]
    wprep = [vf32(A2, 19456, 3072), vf32(A2, 25600, 3072)]
    wdqv = [vbf(W1x, 0, 3072), vbf(W1x, 0, 3072)]
    wi16v = vi16(W1x, 3072, 3072)
    ewrk = [vf32(W1x, 6720, 512), vf32(W1x, 7744, 512)]
    blkmx = vf32(W1x, 6144, 96)
    blks = vf32(W1x, 6336, 96)
    blki = vf32(W1x, 6528, 96)

    wsteps = []
    def mk_wstep(src, dst, dkey, it0, nrt, ncol, sid, buf=None, bkey="wprep"):
        if buf is None:
            buf = wprep
        def step():
            W = nrt * ncol
            nblk = W // 32
            for j in range(nrt):
                def f_ld(h, j=j):
                    return h.dma_start(
                        out=buf[sid][:, j * ncol:(j + 1) * ncol],
                        in_=src[(it0 + j) * 128:(it0 + j + 1) * 128, 0:ncol])
                p.dma("sync", f"wpl{sid}{j}", f_ld, writes=[f"{bkey}{sid}"])
            def f_red(h):
                return h.tensor_reduce(
                    out=blkmx[:, 0:nblk],
                    in_=buf[sid][:, 0:W].rearrange("p (b i) -> p b i", i=32),
                    axis=AX.X, op=OP.max, apply_absolute_value=True)
            p.op("vector", f_red, reads=[f"{bkey}{sid}"], writes=["blkmx"])
            def f_s(h):
                return h.tensor_scalar(
                    out=blks[:, 0:nblk], in0=blkmx[:, 0:nblk], scalar1=1.0 / 7.0,
                    scalar2=1e-12, op0=OP.mult, op1=OP.max)
            p.op("vector", f_s, reads=["blkmx"], writes=["blks"])
            p.op("vector", lambda h: h.reciprocal(out=blki[:, 0:nblk],
                                                  in_=blks[:, 0:nblk]),
                 reads=["blks"], writes=["blki"])
            def f_rnd(h):
                return h.tensor_tensor(
                    out=wi16v[:, 0:W].rearrange("p (b i) -> p b i", i=32),
                    in0=buf[sid][:, 0:W].rearrange("p (b i) -> p b i", i=32),
                    in1=blki[:, 0:nblk].unsqueeze(2).broadcast_to([128, nblk, 32]),
                    op=OP.mult)
            p.op("vector", f_rnd, reads=[f"{bkey}{sid}", "blki"], writes=["wi16"])
            def f_deq(h):
                return h.tensor_tensor(
                    out=wdqv[sid][:, 0:W].rearrange("p (b i) -> p b i", i=32),
                    in0=wi16v[:, 0:W].rearrange("p (b i) -> p b i", i=32),
                    in1=blks[:, 0:nblk].unsqueeze(2).broadcast_to([128, nblk, 32]),
                    op=OP.mult)
            p.op("vector", f_deq, reads=["wi16", "blks"], writes=["wdq"])
            for j in range(nrt):
                def f_st(h, j=j):
                    return h.dma_start(
                        out=dst[(it0 + j) * 128:(it0 + j + 1) * 128, 0:ncol],
                        in_=wdqv[sid][:, j * ncol:(j + 1) * ncol])
                p.dma("scalar", f"wps{sid}{j}", f_st, reads=["wdq"],
                      writes=[f"{dkey}_{it0 + j}_0"])
        return step
    ws_u = [mk_wstep(wu_h, s_wu, "s_wu", 2 * i, 2, D, i % 2)
            for i in range(12)]
    ws_g = [mk_wstep(wg_h, s_wg, "s_wg", 2 * i, 2, D, i % 2)
            for i in range(12)]
    ws_d = [mk_wstep(wd_h, s_wd, "s_wd", i, 1, IF, i % 2)
            for i in range(8)]
    wi_ = {"u": 0, "g": 0, "d": 0, "a": 0}
    def mk_drip(fam, lst):
        def drip(n):
            for _ in range(n):
                if wi_[fam] < len(lst):
                    lst[wi_[fam]]()
                    wi_[fam] += 1
        return drip
    drip_u = mk_drip("u", ws_u)
    drip_g = mk_drip("g", ws_g)
    drip_d = mk_drip("d", ws_d)
    drip_a = mk_drip("a", ws_u + ws_g + ws_d)

    def ln_stats(src_ap, src_keys, par, sink_ap, sink_key):
        c0, c1, c2, c3, c4, c5, c6 = (8 * par + j for j in range(7))
        def f_s0(h):
            return h.activation(out=sink_ap, in_=src_ap, func=AF.Identity,
                                bias=0.0, scale=1.0,
                                accum_out=stats[:, c0:c0 + 1])
        p.op("scalar", f_s0, reads=src_keys, writes=[f"st{c0}", sink_key])
        def f_s1(h):
            return h.activation(out=sink_ap, in_=src_ap, func=AF.Square,
                                bias=0.0, scale=1.0,
                                accum_out=stats[:, c1:c1 + 1])
        p.op("scalar", f_s1, reads=src_keys, writes=[f"st{c1}", sink_key])
        def f_m(h):
            return h.tensor_scalar(out=stats[:, c2:c2 + 1],
                                   in0=stats[:, c0:c0 + 1], scalar1=1.0 / D,
                                   scalar2=None, op0=OP.mult)
        p.op("vector", f_m, reads=[f"st{c0}"], writes=[f"st{c2}"])
        def f_m2(h):
            return h.tensor_tensor(out=stats[:, c3:c3 + 1],
                                   in0=stats[:, c2:c2 + 1],
                                   in1=stats[:, c2:c2 + 1], op=OP.mult)
        p.op("vector", f_m2, reads=[f"st{c2}"], writes=[f"st{c3}"])
        def f_v(h):
            return h.scalar_tensor_tensor(
                out=stats[:, c4:c4 + 1], in0=stats[:, c1:c1 + 1],
                scalar=1.0 / D, in1=stats[:, c3:c3 + 1],
                op0=OP.mult, op1=OP.subtract)
        p.op("vector", f_v, reads=[f"st{c1}", f"st{c3}"], writes=[f"st{c4}"])
        def f_l(h):
            return h.activation(out=stats[:, c5:c5 + 1], in_=stats[:, c4:c4 + 1],
                                func=AF.Ln, bias=consts[:, 0:1], scale=1.0)
        p.op("scalar", f_l, reads=[f"st{c4}", "c_eps"], writes=[f"st{c5}"])
        def f_e(h):
            return h.activation(out=stats[:, c6:c6 + 1], in_=stats[:, c5:c5 + 1],
                                func=AF.Exp, bias=0.0, scale=-0.5)
        p.op("scalar", f_e, reads=[f"st{c5}"], writes=[f"st{c6}"])
        return c2, c6

    for t in range(NT):
        par = t % 2
        if t >= 2:
            def f_x(h, t=t, par=par):
                return h.dma_start(out=xin[par], in_=x_h[t * 128:(t + 1) * 128, :])
            p.dma("sync", f"xin{par}", f_x, writes=[f"xin{par}"])
        cm, cr = ln_stats(xin[par], [f"xin{par}"], par, wrk[par], f"wrk{par}")
        def f_ln(h, par=par, cm=cm, cr=cr):
            return h.tensor_scalar(out=bstg[par], in0=xin[par],
                                   scalar1=stats[:, cm:cm + 1],
                                   scalar2=stats[:, cr:cr + 1],
                                   op0=OP.subtract, op1=OP.mult)
        p.op("vector", f_ln, reads=[f"xin{par}", f"st{8*par+2}", f"st{8*par+6}"],
             writes=[f"bstg{par}"])
        def f_l1(h, t=t, par=par):
            return h.dma_start(out=s_ln1q[t * 128:(t + 1) * 128, :],
                               in_=bstg[par])
        p.dma("gpsimd", f"ln1st{par}", f_l1, reads=[f"bstg{par}"],
              writes=[f"s_ln1q_t{t}"])
        drip_a(1)

    ln1_done = [f"s_ln1q_t{t}" for t in range(NT)]
    for c in range(DC):
        def f_lT(h, c=c):
            return h.dma_start(out=A1[:, c * SEQ:(c + 1) * SEQ],
                               in_=s_ln1q[:, c * 128:(c + 1) * 128],
                               transpose=True)
        p.dma("scalar", "xbar", f_lT, reads=ln1_done, writes=[f"A1c{c}"])

    def elu1p_evac(bank, out_ap, out_keys, wb):
        def f_r(h, bank=bank, wb=wb):
            return h.activation(out=wrk2[wb], in_=pb[bank][:, :], func=AF.Relu,
                                bias=0.0, scale=1.0)
        p.op("scalar", f_r, reads=[], writes=[f"er{wb}", PS[bank]])
        def f_m(h, bank=bank, wb=wb):
            return h.tensor_tensor(out=wrk2[(wb + 2) % 4], in0=pb[bank][:, :],
                                   in1=wrk2[wb], op=OP.subtract)
        p.op("vector", f_m, reads=[f"er{wb}"], writes=[f"em{wb}", PS[bank]])
        def f_e(h, wb=wb):
            return h.activation(out=ewrk[wb % 2], in_=wrk2[(wb + 2) % 4],
                                func=AF.Exp, bias=0.0, scale=1.0)
        p.op("scalar", f_e, reads=[f"em{wb}"], writes=[f"ee{wb}"])
        def f_o(h, wb=wb, out_ap=out_ap):
            return h.tensor_tensor(out=out_ap, in0=ewrk[wb % 2],
                                   in1=wrk2[wb], op=OP.add)
        p.op("vector", f_o, reads=[f"ee{wb}", f"er{wb}"], writes=out_keys)

    for co in range(DC):
        for nch in range(NCH):
            bank = (co * NCH + nch) % 2
            for dc in range(DC):
                def f_mm(h, co=co, nch=nch, dc=dc, bank=bank):
                    return h.matmul(
                        pb[bank][:, :],
                        W1[:, dc * 1024 + co * 128: dc * 1024 + co * 128 + 128],
                        A1[:, dc * SEQ + nch * 512: dc * SEQ + (nch + 1) * 512],
                        start=(dc == 0), stop=False)
                p.op("tensor", f_mm, reads=w1k(dc * 1024 + co * 128) + a1k(dc),
                     writes=[PS[bank]], quiet=True)
            def f_bq(h, co=co, bank=bank):
                return h.matmul(pb[bank][:, :],
                                brow4[0:1, co * 128: co * 128 + 128],
                                onesE[0:1, :], start=False, stop=True)
            p.op("tensor", f_bq, reads=["brow", "onesE"], writes=[PS[bank]])
            if co == 0 and nch == 0 and "d_z0" in dbg:
                zdump = vf32(A2, 26816, 512)
                def f_zd(h, bank=bank, zdump=zdump):
                    return h.activation(out=zdump, in_=pb[bank][:, :],
                                        func=AF.Identity, bias=0.0, scale=1.0)
                p.op("scalar", f_zd, reads=[], writes=["zdump", PS[bank]])
                p.dma("gpsimd", "dbgz", (lambda zdump=zdump: lambda h: h.dma_start(
                    out=dbg["d_z0"], in_=zdump))(), reads=["zdump"], writes=["dz0"])
            elu1p_evac(bank, qstg[bank], [f"qstg{bank}"], bank)
            def f_qs(h, co=co, nch=nch, bank=bank):
                return h.dma_start(
                    out=s_qT[co * 128:(co + 1) * 128, nch * 512:(nch + 1) * 512],
                    in_=qstg[bank])
            p.dma("sync", f"qsp{bank}", f_qs, reads=[f"qstg{bank}"],
                  writes=[f"s_qT_{co}_{nch}"])

    first_kv = [True, True]

    def emit_kvsum(t):
        par = t % 2
        for hh in range(H):
            rlo = 64 * (hh % 2)
            clo = (hh // 2) * 64
            st = (t == 0 and hh < 2)
            def f_kv(h, par=par, hh=hh, rlo=rlo, clo=clo, st=st,
                     last=(t == NT - 1 and hh == H - 1)):
                return h.matmul(pb[6][rlo:rlo + 64, clo:clo + 64],
                                ktile[par][:, hh * 64:(hh + 1) * 64],
                                vtile[par][:, hh * 64:(hh + 1) * 64],
                                start=st, stop=last, skip_group_check=True)
            p.op("tensor", f_kv, reads=[f"ktile{par}", f"vtile{par}"],
                 writes=["ps6"], quiet=(hh != H - 1))
        for pr in range(8):
            st = (t == 0 and pr == 0)
            def f_ks(h, par=par, pr=pr, st=st, last=(t == NT - 1 and pr == 7)):
                return h.matmul(pb[7][:, pr:pr + 1],
                                ktile[par][:, pr * 128:(pr + 1) * 128],
                                onesE[:, 0:1], start=st, stop=last,
                                skip_group_check=True)
            p.op("tensor", f_ks, reads=[f"ktile{par}", "onesE"], writes=["ps7"],
                 quiet=(pr != 7))

    for t in range(NT):
        par = t % 2
        for wsel, lab in ((1, "k"), (2, "v")):
            for oc in range(2):
                bank = 2 + (wsel - 1) * 2 + oc
                for dc in range(DC):
                    def f_mm(h, wsel=wsel, t=t, oc=oc, dc=dc, bank=bank):
                        return h.matmul(
                            pb[bank][:, :],
                            A1[:, dc * SEQ + t * 128: dc * SEQ + t * 128 + 128],
                            W1[:, wsel * 8192 + dc * 1024 + oc * 512:
                                  wsel * 8192 + dc * 1024 + (oc + 1) * 512],
                            start=(dc == 0), stop=False)
                    p.op("tensor", f_mm,
                         reads=a1k(dc) + w1k(wsel * 8192 + dc * 1024, 1024),
                         writes=[PS[bank]], quiet=True)
                def f_bb(h, wsel=wsel, oc=oc, bank=bank):
                    return h.matmul(
                        pb[bank][:, :], onesE[32 * wsel:32 * wsel + 1, 0:128],
                        brow4[32 * wsel:32 * wsel + 1, oc * 512:(oc + 1) * 512],
                        start=False, stop=True)
                p.op("tensor", f_bb, reads=["onesE", "brow"], writes=[PS[bank]])
                if lab == "k":
                    elu1p_evac(bank, ktile[par][:, oc * 512:(oc + 1) * 512],
                               [f"ktile{par}"], oc)
                else:
                    def f_v(h, par=par, oc=oc, bank=bank):
                        return h.activation(
                            out=vtile[par][:, oc * 512:(oc + 1) * 512],
                            in_=pb[bank][:, :], func=AF.Identity, bias=0.0,
                            scale=1.0)
                    p.op("scalar", f_v, reads=[], writes=[f"vtile{par}", PS[bank]])
        if t >= 1:
            emit_kvsum(t - 1)
    emit_kvsum(NT - 1)
    drip_a(99)

    if stage == "A":
        if "ln1q" in dbg or True:
            pass
        _dumps_A(p, dbg, pb, s_ln1q, s_qT, s_wu, s_wd, debug)
        p.barrier()
        p.replay()
        return nc, dbg

    p.op("vector", lambda h: h.tensor_copy(out=kv_sb[:, :], in_=pb[6][:, :]),
         reads=[], writes=["kv_sb", "ps6"])
    p.op("vector", lambda h: h.memset(ksumbd[:, :], 0.0), writes=["ksumbd"])
    for c in range(DC):
        def f_k0(h, c=c):
            return h.tensor_copy(
                out=ksumbd[0:64, c * 16 + 2 * c: c * 16 + 2 * c + 1],
                in_=pb[7][0:64, c:c + 1])
        p.op("vector", f_k0, reads=["ksumbd"], writes=["ksumbd", "ps7"])
        def f_k1(h, c=c):
            return h.tensor_copy(
                out=ksumbd[64:128, c * 16 + 2 * c + 1: c * 16 + 2 * c + 2],
                in_=pb[7][64:128, c:c + 1])
        p.op("vector", f_k1, reads=["ksumbd"], writes=["ksumbd", "ps7"])

    p.barrier()

    zB = [vf32(A1, 0, 1024), vf32(A1, 2048, 1024)]
    x1B = [vf32(A1, 4096, 1024), vf32(A1, 6144, 1024)]
    xinB = [vf32(A1, 8192, 1024), vf32(A1, 10240, 1024)]
    bccB = [vf32(A1, 12288, 512), vf32(A1, 19456, 512)]
    sinkB = [vf32(A1, 13312, 1024), vf32(A1, 15360, 1024)]
    bstgB = [vbf(A1, 17408, 1024), vbf(A1, 18432, 1024)]

    for c in range(DC):
        q = "sync" if c % 2 == 0 else "scalar"
        def f_ql(h, c=c):
            return h.dma_start(out=A2[:, c * SEQ:(c + 1) * SEQ],
                               in_=s_qT[c * 128:(c + 1) * 128, :])
        p.dma(q, f"qld{c % 2}", f_ql,
              reads=[f"s_qT_{c}_{n}" for n in range(NCH)], writes=[f"A2c{c}"])
    for c in range(DC):
        def f_wo(h, c=c):
            return h.dma_start(out=W1[:, c * 1024:(c + 1) * 1024],
                               in_=s_wattn[3 * D:4 * D, c * 128:(c + 1) * 128],
                               transpose=True)
        p.dma("scalar", "xbar", f_wo, reads=["s_wattn3"],
              writes=w1k(c * 1024, 1024))

    for c in range(3, DC):
        def f_u(h, c=c):
            return h.dma_start(out=W1[:, c * 3072:(c + 1) * 3072],
                               in_=s_wu[:, c * 128:(c + 1) * 128],
                               transpose=True)
        p.dma("scalar", "xbar", f_u,
              reads=[f"s_wu_{it}_0" for it in range(IF // 128)],
              writes=w1k(c * 3072, 3072))

    for nch in range(NCH):
        dbank = 4 + nch % 2
        for c in range(DC):
            def f_dn(h, c=c, nch=nch, dbank=dbank):
                return h.matmul(
                    pb[dbank][0:16, :], ksumbd[:, c * 16:(c + 1) * 16],
                    A2[:, c * SEQ + nch * 512: c * SEQ + (nch + 1) * 512],
                    start=(c == 0), stop=(c == DC - 1))
            p.op("tensor", f_dn, reads=["ksumbd", f"A2c{c}"], writes=[PS[dbank]],
                 quiet=(c != DC - 1))
        for pr in range(8):
            bank = 2 + pr % 2
            for rlo in (0, 64):
                def f_nm(h, pr=pr, rlo=rlo, nch=nch, bank=bank):
                    return h.matmul(
                        pb[bank][rlo:rlo + 64, :],
                        kv_sb[rlo:rlo + 64, pr * 64:(pr + 1) * 64],
                        A2[rlo:rlo + 64,
                           pr * SEQ + nch * 512: pr * SEQ + (nch + 1) * 512],
                        start=True, stop=True, skip_group_check=True)
                p.op("tensor", f_nm, reads=["kv_sb", f"A2c{pr}"],
                     writes=[PS[bank]], quiet=(rlo == 0))
            if pr == 0:
                def f_r1(h, dbank=dbank):
                    return h.tensor_scalar(
                        out=rden[:, :], in0=pb[dbank][0:16, :], scalar1=1e-6,
                        scalar2=None, op0=OP.add)
                p.op("vector", f_r1, reads=[], writes=["rden", PS[dbank]])
                p.op("vector", lambda h: h.reciprocal(out=rden[:, :],
                                                      in_=rden[:, :]),
                     reads=["rden"], writes=["rden"])
                def f_r3(h):
                    return h.tensor_scalar(out=rdenb[:, :], in0=rden[:, :],
                                           scalar1=stdp_sig[:, :], scalar2=None,
                                           op0=OP.mult)
                p.op("vector", f_r3, reads=["rden", "stdp_sig"],
                     writes=["rdenb"])
            bbank = 6 + pr % 2
            def f_bc(h, pr=pr, bbank=bbank):
                return h.matmul(pb[bbank][:, :], e16f[:, pr * 128:(pr + 1) * 128],
                                rdenb[:, :], start=True, stop=True)
            p.op("tensor", f_bc, reads=["e16", "rdenb"], writes=[PS[bbank]])
            def f_bcc(h, pr=pr, bbank=bbank):
                return h.activation(out=bccB[pr % 2], in_=pb[bbank][:, :],
                                    func=AF.Identity, bias=0.0, scale=1.0)
            p.op("scalar", f_bcc, reads=[], writes=[f"bcc{pr % 2}", PS[bbank]])
            def f_at(h, pr=pr, nch=nch, bank=bank):
                return h.tensor_tensor(
                    out=A2[:, pr * SEQ + nch * 512: pr * SEQ + (nch + 1) * 512],
                    in0=pb[bank][:, :], in1=bccB[pr % 2], op=OP.mult)
            p.op("vector", f_at, reads=[f"bcc{pr % 2}"],
                 writes=[f"A2c{pr}", PS[bank]])

    p.op("vector", lambda h: h.memset(mroll[:, 0:1], 0.0), writes=["m0roll"])

    def b_stage1(t):
        par = t % 2
        for oc in range(2):
            bank = (t * 2 + oc) % 4
            for fc in range(DC):
                def f_wo(h, t=t, oc=oc, fc=fc, bank=bank):
                    return h.matmul(
                        pb[bank][:, :],
                        A2[:, fc * SEQ + t * 128: fc * SEQ + t * 128 + 128],
                        W1[:, fc * 1024 + oc * 512: fc * 1024 + (oc + 1) * 512],
                        start=(fc == 0), stop=False)
                p.op("tensor", f_wo, reads=[f"A2c{fc}"] + w1k(fc * 1024, 1024),
                     writes=[PS[bank]], quiet=True)
            def f_bo(h, oc=oc, bank=bank):
                return h.matmul(pb[bank][:, :], onesE[96:97, 0:128],
                                brow4[96:97, oc * 512:(oc + 1) * 512],
                                start=False, stop=True, tile_position=(96, 0))
            p.op("tensor", f_bo, reads=["onesE", "brow"], writes=[PS[bank]])
            def f_z(h, par=par, oc=oc, bank=bank):
                return h.activation(out=zB[par][:, oc * 512:(oc + 1) * 512],
                                    in_=pb[bank][:, :], func=AF.Identity,
                                    bias=0.0, scale=1.0)
            p.op("scalar", f_z, reads=[], writes=[f"z{par}", PS[bank]])
        def f_x(h, t=t, par=par):
            return h.dma_start(out=xinB[par], in_=x_h[t * 128:(t + 1) * 128, :])
        p.dma("sync", f"xin{par}", f_x, writes=[f"xinB{par}"])

    def b_stage2(t):
        par = t % 2
        cm, cr = ln_stats(zB[par], [f"z{par}"], par, sinkB[par], f"sink{par}")
        def f_lna(h, par=par, cm=cm, cr=cr):
            return h.tensor_scalar(out=x1B[par], in0=zB[par],
                                   scalar1=stats[:, cm:cm + 1],
                                   scalar2=stats[:, cr:cr + 1],
                                   op0=OP.subtract, op1=OP.mult)
        p.op("vector", f_lna, reads=[f"z{par}", f"st{8*par+2}", f"st{8*par+6}"],
             writes=[f"x1w{par}"])
        def f_x1(h, par=par):
            return h.tensor_tensor(out=x1B[par], in0=x1B[par], in1=xinB[par],
                                   op=OP.add)
        p.op("vector", f_x1, reads=[f"x1w{par}", f"xinB{par}"],
             writes=[f"x1w{par}"])
        def f_xs(h, t=t, par=par):
            return h.dma_start(out=s_x1[t * 128:(t + 1) * 128, :], in_=x1B[par])
        p.dma("gpsimd", f"x1st{par}", f_xs, reads=[f"x1w{par}"],
              writes=[f"s_x1_t{t}"])
        cm2, cr2 = ln_stats(x1B[par], [f"x1w{par}"], par, sinkB[par],
                            f"sink{par}")
        def f_l2(h, par=par, cm2=cm2, cr2=cr2):
            return h.tensor_scalar(out=xinB[par], in0=x1B[par],
                                   scalar1=stats[:, cm2:cm2 + 1],
                                   scalar2=stats[:, cr2:cr2 + 1],
                                   op0=OP.subtract, op1=OP.mult)
        p.op("vector", f_l2, reads=[f"x1w{par}", f"st{8*par+2}", f"st{8*par+6}"],
             writes=[f"ln2f{par}", f"xinB{par}"])
        def f_m0(h, t=t, par=par):
            return h.tensor_reduce(out=mxcol[:, 384 + t:385 + t],
                                   in_=xinB[par], axis=AX.X,
                                   op=OP.max, apply_absolute_value=True)
        p.op("vector", f_m0, reads=[f"ln2f{par}", f"xinB{par}"],
             writes=[f"m0c{t}"])
        def f_l2s(h, t=t, par=par):
            return h.dma_start(out=s_ln2[t * 128:(t + 1) * 128, :],
                               in_=xinB[par])
        p.dma("gpsimd", f"ln2st{par}", f_l2s, reads=[f"ln2f{par}", f"xinB{par}"],
              writes=[f"s_ln2_t{t}"])

    for t in range(NT + 1):
        if t < NT:
            b_stage1(t)
        if t >= 1:
            b_stage2(t - 1)
    def f_m0f(h):
        return h.tensor_reduce(out=mroll[:, 0:1], in_=mxcol[:, 384:416],
                               axis=AX.X, op=OP.max)
    p.op("vector", f_m0f, reads=[f"m0c{i}" for i in range(NT)],
         writes=["m0roll"])

    _dumps_B(p, dbg, s_x1, s_ln2, s_qT, debug, A2, mroll, sc11)
    if stage == "B":
        p.barrier()
        p.replay()
        return nc, dbg

    def cross_core_max(cols, rkeys, tag):
        n = len(cols)
        for j, col in enumerate(cols):
            def f_d1(h, j=j, col=col):
                return h.dma_start(out=s_part[j], in_=mroll[:, col:col + 1])
            p.dma("gpsimd", f"ccp{j}", f_d1, reads=rkeys, writes=[f"s_part{j}"])
            def f_d2(h, j=j):
                return h.dma_start(out=prow[0:1, j * 128:(j + 1) * 128],
                                   in_=s_part[j].rearrange("(p a) b -> a (p b)", a=1))
            p.dma("gpsimd", f"ccp{j}", f_d2, reads=[f"s_part{j}"],
                  writes=[f"prow{j}"])
            def f_g(h, j=j):
                return h.tensor_reduce(out=sc11[0:1, j:j + 1],
                                       in_=prow[0:1, j * 128:(j + 1) * 128],
                                       axis=AX.X, op=OP.max)
            p.op("vector", f_g, reads=[f"prow{j}"], writes=[f"sc11_{j}"])
        def f_s(h, n=n):
            return h.dma_start(out=cc_in[0:1, 0:n], in_=sc11[0:1, 0:n])
        p.dma("gpsimd", "ccio", f_s, reads=[f"sc11_{j}" for j in range(n)],
              writes=["cc_in"])
        def f_c(h, n=n):
            return h.collective_compute("AllReduce", OP.max,
                                        replica_groups=[list(range(N_CORES))],
                                        ins=[cc_in[0:1, 0:n]],
                                        outs=[cc_out[0:1, 0:n]])
        p.cc(f"cc_{tag}", f_c, reads=["cc_in"], writes=["cc_out"])
        def f_l(h, n=n):
            return h.dma_start(out=sc11[0:1, 8:8 + n], in_=cc_out[0:1, 0:n])
        p.dma("gpsimd", "ccio", f_l, reads=["cc_out"], writes=["sc11g"])

    def bcast_scales(srcs, dsts, reads):
        for c, dpos in zip(srcs, dsts):
            def f_b(h, c=c):
                return h.matmul(pb[5][:, 0:1], onesf[0:1, :],
                                sc11[0:1, c:c + 1], start=True, stop=True)
            p.op("tensor", f_b, reads=list(reads) + ["onesf"], writes=["ps5"])
            def f_cp(h, dpos=dpos):
                return h.tensor_copy(out=scales[:, dpos:dpos + 1],
                                     in_=pb[5][:, 0:1])
            p.op("vector", f_cp, reads=[], writes=[f"scl{dpos}", "ps5"])

    if stage == "P2a0":
        p.barrier(); p.replay(); return nc, dbg
    cross_core_max([0], ["m0roll"], "ar0")
    if stage == "P2a":
        p.barrier(); p.replay(); return nc, dbg

    p.barrier(exclude=("cc_", "ccio", "ccp"))
    p.op("vector", lambda h: h.tensor_scalar(
        out=sc11[0:1, 12:13], in0=sc11[0:1, 8:9], scalar1=1.0 / QMAX,
        scalar2=1e-12, op0=OP.mult, op1=OP.max), reads=["sc11g"],
        writes=["v_s0"])
    p.op("vector", lambda h: h.reciprocal(out=sc11[0:1, 13:14],
                                          in_=sc11[0:1, 12:13]),
         reads=["v_s0"], writes=["v_is0"])
    bcast_scales([12, 13], [0, 1], ["v_s0", "v_is0"])
    wu_done = [f"s_wu_{it}_0" for it in range(IF // 128)]
    wg_done = [f"s_wg_{it}_0" for it in range(IF // 128)]
    for c in range(3):
        def f_u(h, c=c):
            return h.dma_start(out=W1[:, c * 3072:(c + 1) * 3072],
                               in_=s_wu[:, c * 128:(c + 1) * 128],
                               transpose=True)
        p.dma("scalar", "xbar", f_u, reads=wu_done,
              writes=w1k(c * 3072, 3072))

    ln2_done = [f"s_ln2_t{t}" for t in range(NT)]
    STG = 24576
    for c in range(DC):
        par = c % 2
        def f_t(h, c=c, par=par):
            return h.dma_start(
                out=A2[:, STG + par * 4096: STG + (par + 1) * 4096],
                in_=s_ln2[:, c * 128:(c + 1) * 128], transpose=True)
        p.dma("scalar", "xbar", f_t, reads=ln2_done,
              writes=[f"A2c{6 + par}"])
        for piece in range(2):
            def f_r(h, par=par, piece=piece):
                return h.tensor_scalar(
                    out=i16c[par][:, :],
                    in0=A2[:, STG + par * 4096 + piece * 2048:
                              STG + par * 4096 + (piece + 1) * 2048],
                    scalar1=scales[:, 1:2], scalar2=None, op0=OP.mult)
            p.op("vector", f_r, reads=[f"A2c{6 + par}", "scl1"],
                 writes=[f"i16c{par}"])
            def f_q(h, c=c, par=par, piece=piece):
                return h.tensor_scalar(
                    out=A1[:, c * SEQ + piece * 2048:
                              c * SEQ + (piece + 1) * 2048],
                    in0=i16c[par][:, :], scalar1=scales[:, 0:1], scalar2=None,
                    op0=OP.mult)
            p.op("vector", f_q, reads=[f"i16c{par}", "scl0"], writes=[f"A1c{c}"])

    for c in range(DC):
        def f_g(h, c=c):
            return h.dma_start(out=A2[:, c * 3072:(c + 1) * 3072],
                               in_=s_wg[:, c * 128:(c + 1) * 128],
                               transpose=True)
        p.dma("scalar", "xbar", f_g, reads=wg_done, writes=a2k(c * 3072, 3072))

    if stage == "P2b":
        p.barrier(); p.replay(); return nc, dbg
    IC_LIM = 2 if stage in ("P2d", "P2e", "P2f", "P2g") else IC
    for wsel in range(2):
        for ich in range(IC_LIM):
            for nch in range(NCH):
                bank = (ich * NCH + nch + wsel * 2) % 4
                for dc in range(DC):
                    if wsel == 0:
                        lhs = W1[:, dc * 3072 + ich * 128:
                                    dc * 3072 + ich * 128 + 128]
                        lk = w1k(dc * 3072 + ich * 128)
                    else:
                        lhs = A2[:, dc * 3072 + ich * 128:
                                    dc * 3072 + ich * 128 + 128]
                        lk = a2k(dc * 3072 + ich * 128, 128)
                    def f_mm(h, lhs=lhs, dc=dc, nch=nch, bank=bank):
                        return h.matmul(
                            pb[bank][:, :], lhs,
                            A1[:, dc * SEQ + nch * 512:
                                  dc * SEQ + (nch + 1) * 512],
                            start=(dc == 0), stop=(dc == DC - 1))
                    p.op("tensor", f_mm, reads=lk + a1k(dc), writes=[PS[bank]],
                         quiet=(dc != DC - 1))
                gidx = wsel * 192 + ich * NCH + nch
                if stage not in ("P2e", "P2g"):
                    def f_mx(h, bank=bank, gidx=gidx):
                        return h.tensor_reduce(out=mxcol[:, gidx:gidx + 1],
                                               in_=pb[bank][:, :], axis=AX.X,
                                               op=OP.max, apply_absolute_value=True)
                    p.op("vector", f_mx, reads=[], writes=[f"mxc{gidx}", PS[bank]])
                sidx = (ich * NCH + nch + wsel) % 4
                if stage not in ("P2e", "P2f"):
                    def f_c(h, bank=bank, sidx=sidx):
                        return h.activation(out=sstg4[sidx][:, :],
                                            in_=pb[bank][:, :], func=AF.Identity,
                                            bias=0.0, scale=1.0)
                    p.op("scalar", f_c, reads=[], writes=[f"sstg{sidx}", PS[bank]])
                dst = s_up if wsel == 0 else s_gate
                dk = "up" if wsel == 0 else "gate"
                if stage not in ("P2c", "P2d", "P2e", "P2f", "P2g"):
                    def f_sp(h, dst=dst, ich=ich, nch=nch, sidx=sidx):
                        return h.dma_start(out=dst[ich, nch], in_=sstg4[sidx][:, :])
                    p.dma("sync", f"sp{sidx}", f_sp, reads=[f"sstg{sidx}"],
                          writes=[f"{dk}_{ich}_{nch}"])

    def f_m1f(h):
        return h.tensor_reduce(out=mroll[:, 1:2], in_=mxcol[:, 0:192], axis=AX.X,
                               op=OP.max)
    p.op("vector", f_m1f, reads=[f"mxc{i}" for i in range(192)],
         writes=["m12roll"])
    def f_m2f(h):
        return h.tensor_reduce(out=mroll[:, 2:3], in_=mxcol[:, 192:384], axis=AX.X,
                               op=OP.max)
    p.op("vector", f_m2f, reads=[f"mxc{i}" for i in range(192, 384)],
         writes=["m12roll"])
    if stage in ("P2c", "P2d", "P2e", "P2f", "P2g"):
        p.barrier(); p.replay(); return nc, dbg
    _dumps_P2(p, dbg, s_up, s_gate, debug)
    if "xqT" in debug:
        p.dma("gpsimd", "dbgx", lambda h: h.dma_start(
            out=dbg["d_xqT"], in_=A1[:, 0:SEQ]), reads=["A1c0"],
            writes=["dbgx1"])
        p.dma("gpsimd", "dbgw", lambda h: h.dma_start(
            out=dbg["d_wupT"], in_=W1[:, 0:3072]),
            reads=[f"W1_{i}" for i in range(3)], writes=["dbgw1"])
    if "scal" in debug and stage == "P2":
        p.dma("gpsimd", "dbgs", lambda h: h.dma_start(
            out=dbg["d_scal"], in_=sc11[0:1, 0:24]), reads=["v_is0"],
            writes=["dbgs1"])
        p.dma("gpsimd", "dbgs2", lambda h: h.dma_start(
            out=dbg["d_scal2"][0:1, 0:2], in_=scales[0:1, 0:2]), reads=["scl0", "scl1"],
            writes=["dbgs2"])
    if stage == "P2":
        p.barrier()
        p.replay()
        return nc, dbg

    cross_core_max([1, 2], ["m12roll"], "ar1")
    p.barrier(exclude=("cc_", "ccio", "ccp"))
    p.op("vector", lambda h: h.tensor_scalar(
        out=sc11[0:1, 14:15], in0=sc11[0:1, 8:9], scalar1=1.0 / QMAX,
        scalar2=1e-12, op0=OP.mult, op1=OP.max), reads=["sc11g"],
        writes=["v_s1"])
    p.op("vector", lambda h: h.reciprocal(out=sc11[0:1, 15:16],
                                          in_=sc11[0:1, 14:15]),
         reads=["v_s1"], writes=["v_is1"])
    p.op("vector", lambda h: h.tensor_scalar(
        out=sc11[0:1, 16:17], in0=sc11[0:1, 9:10], scalar1=1.0 / QMAX,
        scalar2=1e-12, op0=OP.mult, op1=OP.max), reads=["sc11g"],
        writes=["v_s2"])
    p.op("vector", lambda h: h.reciprocal(out=sc11[0:1, 17:18],
                                          in_=sc11[0:1, 16:17]),
         reads=["v_s2"], writes=["v_is2"])
    bcast_scales([14, 15, 16, 17], [2, 3, 4, 5],
                 ["v_s1", "v_is1", "v_s2", "v_is2"])

    g_in = [vbf(A1, 0, 2048), vbf(A1, 2048, 2048),
            vbf(A2, 0, 2048), vbf(A2, 2048, 2048)]
    g_out = [vbf(A1, 4096, 2048), vbf(A1, 6144, 2048),
             vbf(A2, 4096, 2048), vbf(A2, 6144, 2048)]
    sigfv = [vf32(A1, 8192, 2048), vf32(A1, 12288, 2048),
             vf32(A2, 8192, 2048), vf32(A2, 12288, 2048)]
    u_in = [vbf(A1, 16384, 2048), vbf(A1, 18432, 2048),
            vbf(A2, 16384, 2048), vbf(A2, 18432, 2048)]
    i16A = [vi16(A1, 20480, 2048), vi16(A1, 22528, 2048),
            vi16(A2, 20480, 2048), vi16(A2, 22528, 2048)]
    i16B = [vi16(A1, 24576, 2048), vi16(A1, 26624, 2048),
            vi16(A2, 24576, 2048), vi16(A2, 26624, 2048)]
    u16A = [vu16(A1, 28672, 2048), vu16(A1, 30720, 2048),
            vu16(A2, 28672, 2048), vu16(A2, 30720, 2048)]

    wd_done = [f"s_wd_{it}_0" for it in range(DC)]
    for c in range(IC):
        def f_d(h, c=c):
            return h.dma_start(out=W1[:, c * 1024:(c + 1) * 1024],
                               in_=s_wd[:, c * 128:(c + 1) * 128],
                               transpose=True)
        p.dma("scalar", "xbar", f_d, reads=wd_done, writes=w1k(c * 1024, 1024))

    p.op("vector", lambda h: h.memset(mroll[:, 3:4], 0.0), writes=["m3roll"])
    MT = (IC * NCH) // 4
    for mt in range(MT):
        par = mt % 4
        ich, nch4 = divmod(mt, 2)
        def f_ld(h, ich=ich, nch4=nch4, par=par):
            return h.dma_start(out=g_in[par],
                               in_=s_gate[ich, nch4 * 4:(nch4 + 1) * 4].transpose([1, 0, 2]))
        p.dma("sync", f"p3i{par}", f_ld,
              reads=[f"gate_{ich}_{nch4 * 4 + j}" for j in range(4)],
              writes=[f"g_in{par}"])
        def f_r2(h, par=par):
            return h.activation(out=i16A[par], in_=g_in[par], func=AF.Identity,
                                bias=0.0, scale=scales[:, 5:6])
        p.op("scalar", f_r2, reads=[f"g_in{par}", "scl5"], writes=[f"i16A{par}"])
        def f_sg(h, par=par):
            return h.activation(out=sigfv[par], in_=i16A[par], func=AF.Sigmoid,
                                bias=0.0, scale=scales[:, 4:5])
        p.op("scalar", f_sg, reads=[f"i16A{par}", "scl4"], writes=[f"sigf{par}"])
        def f_sq(h, par=par):
            return h.tensor_scalar(out=u16A[par], in0=sigfv[par],
                                   scalar1=65536.0, scalar2=None, op0=OP.mult)
        p.op("vector", f_sq, reads=[f"sigf{par}"], writes=[f"u16A{par}"])
        def f_g3(h, par=par):
            return h.tensor_tensor(out=g_out[par], in0=i16A[par],
                                   in1=u16A[par], op=OP.mult)
        p.op("vector", f_g3, reads=[f"i16A{par}", f"u16A{par}"],
             writes=[f"g_out{par}"])
        def f_mx(h, par=par):
            return h.tensor_reduce(out=small[:, 5:6], in_=g_out[par], axis=AX.X,
                                   op=OP.max, apply_absolute_value=True)
        p.op("vector", f_mx, reads=[f"g_out{par}"], writes=["m3t"])
        p.op("vector", lambda h: h.tensor_tensor(
            out=mroll[:, 3:4], in0=mroll[:, 3:4], in1=small[:, 5:6], op=OP.max),
            reads=["m3t", "m3roll"], writes=["m3roll"])
        def f_st(h, ich=ich, nch4=nch4, par=par):
            return h.dma_start(out=s_g3[ich, nch4 * 4:(nch4 + 1) * 4].transpose([1, 0, 2]),
                               in_=g_out[par])
        p.dma("gpsimd", f"p3o{par}", f_st, reads=[f"g_out{par}"],
              writes=[f"g3_{ich}_{nch4}"])

    _dumps_P3(p, dbg, s_g3, s_p, debug)
    if stage == "P3":
        p.barrier()
        p.replay()
        return nc, dbg

    cross_core_max([3], ["m3roll"], "ar2")
    p.op("vector", lambda h: h.scalar_tensor_tensor(
        out=sc11[0:1, 18:19], in0=sc11[0:1, 8:9], scalar=1.0 / 65536.0,
        in1=sc11[0:1, 16:17], op0=OP.mult, op1=OP.mult),
        reads=["sc11g", "v_s2"], writes=["v_m3t"])
    p.op("vector", lambda h: h.tensor_scalar(
        out=sc11[0:1, 19:20], in0=sc11[0:1, 18:19], scalar1=1.0 / QMAX,
        scalar2=1e-12, op0=OP.mult, op1=OP.max), reads=["v_m3t"],
        writes=["v_s3"])
    p.op("vector", lambda h: h.reciprocal(out=sc11[0:1, 20:21],
                                          in_=sc11[0:1, 19:20]),
         reads=["v_s3"], writes=["v_is3"])
    p.op("vector", lambda h: h.scalar_tensor_tensor(
        out=sc11[0:1, 21:22], in0=sc11[0:1, 20:21], scalar=1.0 / 65536.0,
        in1=sc11[0:1, 16:17], op0=OP.mult, op1=OP.mult),
        reads=["v_is3", "v_s2"], writes=["v_c43"])
    bcast_scales([21], [6], ["v_c43"])

    p.op("vector", lambda h: h.memset(mroll[:, 4:5], 0.0), writes=["m4roll"])
    for mt in range(MT):
        par = mt % 4
        ich, nch4 = divmod(mt, 2)
        def f_lg(h, ich=ich, nch4=nch4, par=par):
            return h.dma_start(out=g_in[par],
                               in_=s_g3[ich, nch4 * 4:(nch4 + 1) * 4].transpose([1, 0, 2]))
        p.dma("sync", f"p4g{par}", f_lg, reads=[f"g3_{ich}_{nch4}"],
              writes=[f"g_in{par}"])
        def f_lu(h, ich=ich, nch4=nch4, par=par):
            return h.dma_start(out=u_in[par],
                               in_=s_up[ich, nch4 * 4:(nch4 + 1) * 4].transpose([1, 0, 2]))
        p.dma("scalar", f"p4u{par}", f_lu,
              reads=[f"up_{ich}_{nch4 * 4 + j}" for j in range(4)],
              writes=[f"u_in{par}"])
        def f_a(h, par=par):
            return h.activation(out=i16A[par], in_=g_in[par], func=AF.Identity,
                                bias=0.0, scale=scales[:, 6:7])
        p.op("scalar", f_a, reads=[f"g_in{par}", "scl6"], writes=[f"i16A{par}"])
        def f_b(h, par=par):
            return h.activation(out=i16B[par], in_=u_in[par], func=AF.Identity,
                                bias=0.0, scale=scales[:, 3:4])
        p.op("scalar", f_b, reads=[f"u_in{par}", "scl3"], writes=[f"i16B{par}"])
        def f_p(h, par=par):
            return h.tensor_tensor(out=g_out[par], in0=i16A[par],
                                   in1=i16B[par], op=OP.mult)
        p.op("vector", f_p, reads=[f"i16A{par}", f"i16B{par}"],
             writes=[f"g_out{par}"])
        def f_mx(h, par=par):
            return h.tensor_reduce(out=small[:, 6:7], in_=g_out[par], axis=AX.X,
                                   op=OP.max, apply_absolute_value=True)
        p.op("vector", f_mx, reads=[f"g_out{par}"], writes=["m4t"])
        p.op("vector", lambda h: h.tensor_tensor(
            out=mroll[:, 4:5], in0=mroll[:, 4:5], in1=small[:, 6:7], op=OP.max),
            reads=["m4t", "m4roll"], writes=["m4roll"])
        def f_st(h, ich=ich, nch4=nch4, par=par):
            return h.dma_start(out=s_p[ich, nch4 * 4:(nch4 + 1) * 4].transpose([1, 0, 2]),
                               in_=g_out[par])
        p.dma("gpsimd", f"p4o{par}", f_st, reads=[f"g_out{par}"],
              writes=[f"p_{ich}_{nch4}"])

    if "g3p" in debug:
        p.dma("gpsimd", "dbgp", lambda h: h.dma_start(out=dbg["d_p"], in_=s_p),
              reads=[f"p_{i}_{n}" for i in range(IC) for n in range(2)],
              writes=["dbg10"])
    if stage == "P4":
        p.barrier()
        p.replay()
        return nc, dbg

    cross_core_max([4], ["m4roll"], "ar3")
    p.op("vector", lambda h: h.tensor_scalar(
        out=sc11[0:1, 22:23], in0=sc11[0:1, 19:20], scalar1=sc11[0:1, 14:15],
        scalar2=None, op0=OP.mult), reads=["v_s3", "v_s1"], writes=["v_sg4"])
    p.op("vector", lambda h: h.tensor_scalar(
        out=sc11[0:1, 10:11], in0=sc11[0:1, 8:9], scalar1=sc11[0:1, 22:23],
        scalar2=None, op0=OP.mult), reads=["sc11g", "v_sg4"], writes=["v_m4t"])
    p.op("vector", lambda h: h.tensor_scalar(
        out=sc11[0:1, 11:12], in0=sc11[0:1, 10:11], scalar1=1.0 / QMAX,
        scalar2=1e-12, op0=OP.mult, op1=OP.max), reads=["v_m4t"],
        writes=["v_s4"])
    p.op("vector", lambda h: h.reciprocal(out=sc11[0:1, 12:13],
                                          in_=sc11[0:1, 11:12]),
         reads=["v_s4"], writes=["v_is4"])
    p.op("vector", lambda h: h.tensor_scalar(
        out=sc11[0:1, 13:14], in0=sc11[0:1, 22:23], scalar1=sc11[0:1, 12:13],
        scalar2=None, op0=OP.mult), reads=["v_sg4", "v_is4"], writes=["v_c54"])
    bcast_scales([13, 11], [7, 8], ["v_c54", "v_s4"])

    p.barrier(exclude=("cc_", "ccio", "ccp"))
    o2t = [vbf(A1, 0, 1024), vbf(A1, 1024, 1024)]
    x1l = [vf32(A1, 2048, 1024), vf32(A1, 4096, 1024)]
    resl = [vf32(A1, 6144, 1024), vf32(A1, 8192, 1024)]
    sink5 = vf32(A1, 10240, 1024)
    p5stg = [vbf(A1, 12288, 2048), vbf(A1, 14336, 2048)]
    i16P = [vi16(A1, 16384, 2048), vi16(A1, 18432, 2048)]

    def p5_a(nch):
        par5 = nch % 2
        obase = par5 * 12288
        for grp in range(IC // 4):
            slot = grp % 2
            def f_lp(h, nch=nch, grp=grp, slot=slot):
                return h.dma_start(
                    out=p5stg[slot],
                    in_=s_p[grp * 4:(grp + 1) * 4, nch].transpose([1, 0, 2]))
            p.dma("sync", f"p5i{slot}", f_lp,
                  reads=[f"p_{grp * 4 + j}_{nch // 4}" for j in range(4)],
                  writes=[f"p5stg{slot}"])
            def f_oq(h, slot=slot):
                return h.activation(out=i16P[slot], in_=p5stg[slot],
                                    func=AF.Identity, bias=0.0,
                                    scale=scales[:, 7:8])
            p.op("scalar", f_oq, reads=[f"p5stg{slot}", "scl7"],
                 writes=[f"i16P{slot}"])
            def f_ot(h, grp=grp, obase=obase, slot=slot):
                return h.tensor_copy(
                    out=A2[:, obase + grp * 2048: obase + (grp + 1) * 2048],
                    in_=i16P[slot])
            p.op("vector", f_ot, reads=[f"i16P{slot}"],
                 writes=[f"ot{par5}_{grp}"])

    def p5_b(nch):
        par5 = nch % 2
        obase = par5 * 12288
        otk = [f"ot{par5}_{g}" for g in range(IC // 4)]
        for hc in range(DC):
            bank = hc % 4
            for ic in range(IC):
                def f_dn(h, hc=hc, ic=ic, obase=obase, bank=bank):
                    return h.matmul(
                        pb[bank][:, :],
                        W1[:, ic * 1024 + hc * 128: ic * 1024 + hc * 128 + 128],
                        A2[:, obase + ic * 512: obase + (ic + 1) * 512],
                        start=(ic == 0), stop=(ic == IC - 1))
                p.op("tensor", f_dn, reads=w1k(ic * 1024 + hc * 128) + otk,
                     writes=[PS[bank]], quiet=(ic != IC - 1))
            def f_ev(h, bank=bank, hc=hc):
                return h.activation(out=sstg4[hc % 4][:, :],
                                    in_=pb[bank][:, :], func=AF.Identity,
                                    bias=0.0, scale=scales[:, 8:9])
            p.op("scalar", f_ev, reads=["scl8"],
                 writes=[f"sstg{hc % 4}", PS[bank]])
            def f_so(h, hc=hc, nch=nch):
                return h.dma_start(
                    out=s_o2T[hc * 128:(hc + 1) * 128,
                              nch * 512:(nch + 1) * 512],
                    in_=sstg4[hc % 4][:, :])
            p.dma("gpsimd", f"o2T{hc % 4}", f_so, reads=[f"sstg{hc % 4}"],
                  writes=[f"o2T_{hc}_{nch}"])

    def p5_c(nch):
        for tt in range(4):
            t = nch * 4 + tt
            par = t % 2
            def f_tl(h, t=t, par=par):
                return h.dma_start(out=o2t[par],
                                   in_=s_o2T[:, t * 128:(t + 1) * 128],
                                   transpose=True)
            p.dma("scalar", "xbar", f_tl,
                  reads=[f"o2T_{hc}_{nch}" for hc in range(DC)],
                  writes=[f"o2t{par}"])
            def f_xl(h, t=t, par=par):
                return h.dma_start(out=x1l[par],
                                   in_=s_x1[t * 128:(t + 1) * 128, :])
            p.dma("sync", f"x1l{par}", f_xl, reads=[f"s_x1_t{t}"],
                  writes=[f"x1l{par}"])
            cm, cr = ln_stats(o2t[par], [f"o2t{par}"], par, sink5, "sink5")
            def f_ln(h, par=par, cm=cm, cr=cr):
                return h.tensor_scalar(out=resl[par], in0=o2t[par],
                                       scalar1=stats[:, cm:cm + 1],
                                       scalar2=stats[:, cr:cr + 1],
                                       op0=OP.subtract, op1=OP.mult)
            p.op("vector", f_ln, reads=[f"o2t{par}", f"st{8*par+2}",
                                        f"st{8*par+6}"], writes=[f"resl{par}"])
            def f_res(h, par=par):
                return h.tensor_tensor(out=resl[par], in0=resl[par],
                                       in1=x1l[par], op=OP.add)
            p.op("vector", f_res, reads=[f"resl{par}", f"x1l{par}"],
                 writes=[f"resl{par}"])
            def f_out(h, t=t, par=par):
                return h.dma_start(out=out_h[t * 128:(t + 1) * 128, :],
                                   in_=resl[par])
            p.dma("gpsimd", f"outst{par}", f_out, reads=[f"resl{par}"],
                  writes=[f"out_t{t}"])

    for k in range(NCH + 2):
        if k < NCH:
            p5_a(k)
        if 1 <= k <= NCH:
            p5_b(k - 1)
        if k >= 2:
            p5_c(k - 2)

    if "d_o2T" in dbg:
        p.dma("gpsimd", "dbgo2", lambda h: h.dma_start(
            out=dbg["d_o2T"], in_=s_o2T),
            reads=[f"o2T_{hc}_{nch}" for hc in range(DC) for nch in range(NCH)],
            writes=["dbgo2w"])
    if "d_scal" in dbg:
        p.dma("gpsimd", "dbgend", lambda h: h.dma_start(
            out=dbg["d_scal"], in_=sc11[0:1, :]),
            reads=["v_c54", "v_s4", "v_is4", "v_s1", "v_s2", "v_s3"],
            writes=["dbgs1"])
    p.barrier()
    p.replay()
    return nc, dbg


def _dumps_A(p, dbg, pb, s_ln1q, s_qT, s_wu, s_wd, debug):
    if "ln1q" in debug:
        p.dma("gpsimd", "dbg", lambda h: h.dma_start(
            out=dbg["d_ln1q"], in_=s_ln1q),
            reads=[f"s_ln1q_t{t}" for t in range(NT)], writes=["dbg1"])
    if "qT" in debug:
        p.dma("gpsimd", "dbg", lambda h: h.dma_start(out=dbg["d_qT"], in_=s_qT),
              reads=[f"s_qT_{c}_{n}" for c in range(DC) for n in range(NCH)],
              writes=["dbg2"])
    if "kv" in debug:
        def f1(h):
            return h.tensor_copy(out=dbg.setdefault("_kvsb", None) or None, in_=None)
    if "wdq" in debug:
        p.dma("gpsimd", "dbg", lambda h: h.dma_start(out=dbg["d_wu"], in_=s_wu),
              reads=[f"s_wu_{it}_0" for it in range(IF // 128)], writes=["dbg3"])
        p.dma("gpsimd", "dbg2", lambda h: h.dma_start(out=dbg["d_wd"], in_=s_wd),
              reads=[f"s_wd_{it}_{pc}" for it in range(DC) for pc in range(3)],
              writes=["dbg4"])


def _dumps_B(p, dbg, s_x1, s_ln2, s_qT, debug, A2, mroll, sc11):
    if "x1" in debug:
        p.dma("gpsimd", "dbg", lambda h: h.dma_start(out=dbg["d_x1"], in_=s_x1),
              reads=[f"s_x1_t{t}" for t in range(NT)], writes=["dbg5"])
    if "ln2" in debug:
        p.dma("gpsimd", "dbg2", lambda h: h.dma_start(out=dbg["d_ln2"], in_=s_ln2),
              reads=[f"s_ln2_t{t}" for t in range(NT)], writes=["dbg6"])


def _dumps_P2(p, dbg, s_up, s_gate, debug):
    if "upg" in debug:
        p.dma("gpsimd", "dbg", lambda h: h.dma_start(out=dbg["d_up"], in_=s_up),
              reads=[f"up_{i}_{n}" for i in range(IC) for n in range(NCH)],
              writes=["dbg7"])
        p.dma("gpsimd", "dbg2", lambda h: h.dma_start(
            out=dbg["d_gate"], in_=s_gate),
            reads=[f"gate_{i}_{n}" for i in range(IC) for n in range(NCH)],
            writes=["dbg8"])


def _dumps_P3(p, dbg, s_g3, s_p, debug):
    if "g3p" in debug:
        p.dma("gpsimd", "dbg", lambda h: h.dma_start(out=dbg["d_g3"], in_=s_g3),
              reads=[f"g3_{i}_{n}" for i in range(IC) for n in range(2)],
              writes=["dbg9"])


_NC_CACHE = {}


def kernel(**inputs):
    x = np.ascontiguousarray(np.asarray(inputs["L0_attn_x"], dtype=np.float32))
    per_core_common = {
        "wq": np.ascontiguousarray(inputs["L0_attn_Wq"], dtype=np.float32),
        "wk": np.ascontiguousarray(inputs["L0_attn_Wk"], dtype=np.float32),
        "wv": np.ascontiguousarray(inputs["L0_attn_Wv"], dtype=np.float32),
        "wo": np.ascontiguousarray(inputs["L0_attn_Wo"], dtype=np.float32),
        "bq": np.ascontiguousarray(inputs["L0_attn_bq"], dtype=np.float32),
        "bk": np.ascontiguousarray(inputs["L0_attn_bk"], dtype=np.float32),
        "bv": np.ascontiguousarray(inputs["L0_attn_bv"], dtype=np.float32),
        "bo": np.ascontiguousarray(inputs["L0_attn_bo"], dtype=np.float32),
        "stdp": np.ascontiguousarray(inputs["L0_attn_stdp"], dtype=np.float32),
        "wg": np.ascontiguousarray(inputs["L0_ffn_w_gate"], dtype=np.float32),
        "wu": np.ascontiguousarray(inputs["L0_ffn_w_up"], dtype=np.float32),
        "wd": np.ascontiguousarray(inputs["L0_ffn_w_down"], dtype=np.float32),
    }
    if "nc" not in _NC_CACHE:
        _NC_CACHE["nc"] = build_nc(stage="full")[0]
    nc = _NC_CACHE["nc"]
    in_maps = []
    for i in range(N_CORES):
        m = dict(per_core_common)
        m["x"] = np.ascontiguousarray(x[i])
        in_maps.append(m)
    res = run_bass_kernel_spmd(nc, in_maps, core_ids=list(range(N_CORES)))
    out = np.stack([np.asarray(res.results[i]["out"]) for i in range(N_CORES)],
                   axis=0)
    return out.astype(np.float32)


# version 24
# speedup vs baseline: 1.1497x; 1.1497x over previous
import numpy as np
from contextlib import ExitStack

import concourse.bass as bass
import concourse.mybir as mybir
from concourse.bass_utils import run_bass_kernel_spmd

F32 = mybir.dt.float32
BF16 = mybir.dt.bfloat16
I16 = mybir.dt.int16
U16 = mybir.dt.uint16
OP = mybir.AluOpType
AF = mybir.ActivationFunctionType
AX = mybir.AxisListType

N_CORES = 8
SEQ = 4096
D = 1024
H = 16
IF = 3072
NT = SEQ // 128
NCH = SEQ // 512
DC = D // 128
IC = IF // 128
QMAX = 32767.0
EPS = 1e-5

ENGS = ("tensor", "vector", "scalar", "gpsimd", "sync")


class Prog:

    def __init__(self, nc, es):
        self.nc = nc
        self.es = es
        self.eng = {n: dict(sem=es.enter_context(nc.semaphore(f"es_{n}")),
                            cnt=0, q=[]) for n in ENGS}
        self.know = {e: {} for e in ENGS}
        self.buf = {}
        self.streams = {}

    def _sem(self, kind, name):
        return self.eng[name]["sem"] if kind == "eng" else self.streams[name]["sem"]

    def _wait(self, consumer, kind, name, val):
        if kind == "eng" and name == consumer and val > self.eng[name]["cnt"]:
            return
        k = self.know[consumer]
        if k.get((kind, name), 0) >= val:
            return
        k[(kind, name)] = val
        self.eng[consumer]["q"].append(("wait", self._sem(kind, name), val))

    def _deps(self, consumer, reads, writes):
        for key in reads:
            b = self.buf.get(key)
            if b:
                for (kind, name), cnt in b["w"].items():
                    self._wait(consumer, kind, name, cnt)
        for key in writes:
            b = self.buf.get(key)
            if b:
                for (kind, name), cnt in b["w"].items():
                    self._wait(consumer, kind, name, cnt)
                for (kind, name), cnt in b["r"].items():
                    self._wait(consumer, kind, name, cnt)

    def _record(self, kind, pname, reads, writes, cnt):
        for key in writes:
            self.buf[key] = dict(w={(kind, pname): cnt}, r={})
        for key in reads:
            b = self.buf.setdefault(key, dict(w={}, r={}))
            b["r"][(kind, pname)] = cnt

    def op(self, ename, fn, reads=(), writes=(), quiet=False):
        e = self.eng[ename]
        self._deps(ename, reads, writes)
        if quiet:
            e["q"].append(("raw", lambda h, fn=fn: fn(h)))
            e.setdefault("pending", False)
            e["pending"] = True
            self._record("eng", ename, reads, writes, e["cnt"] + 1)
            return
        e["cnt"] += 1
        e["pending"] = False
        e["q"].append(("op", fn, e["sem"], 1))
        self._record("eng", ename, reads, writes, e["cnt"])

    def flush_quiet(self):
        for ename, e in self.eng.items():
            if e.get("pending"):
                e["cnt"] += 1
                e["pending"] = False
                e["q"].append(("op", lambda h: h.nop(), e["sem"], 1))

    def _stream(self, name):
        if name not in self.streams:
            self.streams[name] = dict(
                sem=self.es.enter_context(self.nc.semaphore(f"ds_{name}")), cnt=0)
        return self.streams[name]

    def dma(self, qname, stream, fn, reads=(), writes=()):
        s = self._stream(stream)
        if s["cnt"] > 0:
            self._wait(qname, "stream", stream, s["cnt"])
        self._deps(qname, reads, writes)
        s["cnt"] += 16
        self.eng[qname]["q"].append(("op", fn, s["sem"], 16))
        self._record("stream", stream, reads, writes, s["cnt"])

    def cc(self, stream, fn, reads=(), writes=()):
        s = self._stream(stream)
        if s["cnt"] > 0:
            self._wait("gpsimd", "stream", stream, s["cnt"])
        self._deps("gpsimd", reads, writes)
        s["cnt"] += 1
        self.eng["gpsimd"]["q"].append(("op", fn, s["sem"], 1))
        self._record("stream", stream, reads, writes, s["cnt"])

    def barrier(self, exclude=()):
        self.flush_quiet()
        targets = [("eng", n, self.eng[n]["cnt"]) for n in ENGS
                   if self.eng[n]["cnt"] > 0]
        targets += [("stream", sn, s["cnt"]) for sn, s in self.streams.items()
                    if s["cnt"] > 0 and not any(sn.startswith(e) for e in exclude)]
        for consumer in ENGS:
            for kind, name, cnt in targets:
                if kind == "eng" and name == consumer:
                    continue
                self._wait(consumer, kind, name, cnt)

    def replay(self):
        with self.nc.Block() as block:
            def mk(name):
                q = self.eng[name]["q"]
                def body(h):
                    for it in q:
                        if it[0] == "wait":
                            h.wait_ge(it[1], it[2])
                        elif it[0] == "raw":
                            it[1](h)
                        else:
                            it[1](h).then_inc(it[2], it[3])
                return body
            block.tensor(mk("tensor"))
            block.vector(mk("vector"))
            block.scalar(mk("scalar"))
            block.gpsimd(mk("gpsimd"))
            block.sync(mk("sync"))


def _keys(prefix, off, length, gran):
    return [f"{prefix}{i}" for i in range(off // gran, (off + length - 1) // gran + 1)]


def build_nc(stage="full", debug=()):
    nc = bass.Bass("TRN2", target_bir_lowering=False)
    es = ExitStack()

    x_h = nc.dram_tensor("x", [SEQ, D], F32, kind="ExternalInput").ap()
    w_attn_h = [nc.dram_tensor(n, [D, D], F32, kind="ExternalInput").ap()
                for n in ("wq", "wk", "wv", "wo")]
    b_attn_h = [nc.dram_tensor(n, [D], F32, kind="ExternalInput").ap()
                for n in ("bq", "bk", "bv", "bo")]
    stdp_h = nc.dram_tensor("stdp", [H], F32, kind="ExternalInput").ap()
    wg_h = nc.dram_tensor("wg", [IF, D], F32, kind="ExternalInput").ap()
    wu_h = nc.dram_tensor("wu", [IF, D], F32, kind="ExternalInput").ap()
    wd_h = nc.dram_tensor("wd", [D, IF], F32, kind="ExternalInput").ap()
    out_h = nc.dram_tensor("out", [SEQ, D], F32, kind="ExternalOutput").ap()

    s_wattn = nc.dram_tensor("s_wattn", [4 * D, D], BF16).ap()
    s_ln1q = nc.dram_tensor("s_ln1q", [SEQ, D], BF16).ap()
    s_qT = nc.dram_tensor("s_qT", [D, SEQ], BF16).ap()
    s_ln2 = nc.dram_tensor("s_ln2", [SEQ, D], BF16).ap()
    s_ln2T = nc.dram_tensor("s_ln2T", [D, SEQ], BF16).ap()
    s_x1 = nc.dram_tensor("s_x1", [SEQ, D], F32).ap()
    s_wu = nc.dram_tensor("s_wu", [IF, D], BF16).ap()
    s_wg = nc.dram_tensor("s_wg", [IF, D], BF16).ap()
    s_wd = nc.dram_tensor("s_wd", [D, IF], BF16).ap()
    s_up = nc.dram_tensor("s_up", [IC, NCH, 128, 512], BF16).ap()
    s_gate = nc.dram_tensor("s_gate", [IC, NCH, 128, 512], BF16).ap()
    s_g3 = nc.dram_tensor("s_g3", [IC, NCH, 128, 512], BF16).ap()
    s_p = nc.dram_tensor("s_p", [IC, NCH, 128, 512], BF16).ap()
    s_o2T = nc.dram_tensor("s_o2T", [D, SEQ], BF16).ap()
    s_part = [nc.dram_tensor(f"s_part{j}", [128, 1], F32).ap() for j in range(2)]
    cc_in = nc.dram_tensor("cc_in", [1, 4], F32).ap()
    cc_out = nc.dram_tensor("cc_out", [1, 4], F32).ap()

    dbg = {}
    def dbg_decl(nm, shp, dt=BF16):
        dbg[nm] = nc.dram_tensor(nm, shp, dt, kind="ExternalOutput").ap()
    if "ln1q" in debug: dbg_decl("d_ln1q", [SEQ, D])
    if "qT" in debug: dbg_decl("d_qT", [D, SEQ])
    if "xqT" in debug:
        dbg_decl("d_xqT", [128, SEQ])
        dbg_decl("d_wupT", [128, 3072])
    if "z0" in debug: dbg_decl("d_z0", [128, 512], F32)
    if "kv" in debug: dbg_decl("d_kv", [128, 512], F32)
    if "ksum" in debug: dbg_decl("d_ksum", [128, 8], F32)
    if "x1" in debug: dbg_decl("d_x1", [SEQ, D], F32)
    if "ln2" in debug: dbg_decl("d_ln2", [SEQ, D])
    if "attnT" in debug: dbg_decl("d_attnT", [D, SEQ])
    if "scal" in debug:
        dbg_decl("d_scal", [1, 24], F32)
        dbg_decl("d_scal2", [1, 12], F32)
    if "upg" in debug:
        dbg_decl("d_up", [IC, NCH, 128, 512])
        dbg_decl("d_gate", [IC, NCH, 128, 512])
    if "g3p" in debug:
        dbg_decl("d_g3", [IC, NCH, 128, 512])
        dbg_decl("d_p", [IC, NCH, 128, 512])
    if "wdq" in debug:
        dbg_decl("d_wu", [IF, D])
        dbg_decl("d_wd", [D, IF])
    if "o2T" in debug: dbg_decl("d_o2T", [D, SEQ])

    sb = lambda name, shape, dt: es.enter_context(nc.sbuf_tensor(name, shape, dt))

    A1 = sb("A1", [128, 8 * SEQ], BF16)
    A2 = sb("A2", [128, 8 * SEQ], BF16)
    W1 = sb("W1", [128, 24 * 1024], BF16)
    stats = sb("stats", [128, 16], F32)
    consts = sb("consts", [128, 2], F32)
    scales = sb("scales", [128, 12], F32)
    mroll = sb("mroll", [128, 8], F32)
    scr32 = sb("scr32", [128, 32], F32)
    t128 = sb("t128", [128, 2], F32)
    small = sb("small", [128, 8], F32)
    kv_sb = sb("kv_sb", [128, 512], BF16)
    ksumbd = sb("ksumbd", [128, 8 * 16], BF16)
    e16f = sb("e16f", [16, 8 * 128], BF16)
    rden = sb("rden", [16, 512], F32)
    rdenb = sb("rdenb", [16, 512], BF16)
    onesE = sb("onesE", [128, 512], BF16)
    onesf = sb("onesf", [1, 128], F32)
    brow4 = sb("brow4", [128, D], BF16)
    stdp_s = sb("stdp_s", [16, 1], F32)
    stdp_sig = sb("stdp_sig", [16, 1], F32)
    sc11 = sb("sc11", [1, 24], F32)
    mxcol = sb("mxcol", [128, 420], F32)
    prow = sb("prow", [1, 256], F32)
    idf = sb("idf", [128, 128], F32)
    W1x = sb("W1x", [128, 8768], BF16)
    i16c = [W1x[:, 0:2048].bitcast(I16), W1x[:, 2048:4096].bitcast(I16)]
    sstg4 = [W1x[:, 4096 + 512 * i:4096 + 512 * (i + 1)] for i in range(4)]

    pb = [es.enter_context(nc.psum_tensor(f"pb{i}", [128, 512], F32))
          for i in range(8)]
    PS = [f"ps{i}" for i in range(8)]

    p = Prog(nc, es)

    def vf32(arena, off, n):
        return arena[:, off:off + 2 * n].bitcast(F32)
    def vbf(arena, off, n):
        return arena[:, off:off + n]
    def vi16(arena, off, n):
        return arena[:, off:off + n].bitcast(I16)
    def vu16(arena, off, n):
        return arena[:, off:off + n].bitcast(U16)

    def w1k(off, length=128):
        return _keys("W1_", off, length, 1024)
    def a1k(c):
        return [f"A1c{c}"]
    def a2k(off, length):
        return _keys("A2c", off, length, 4096)

    for _t in range(2):
        def f_pf(h, _t=_t):
            return h.dma_start(out=vf32(A2, _t * 2048, 1024),
                               in_=x_h[_t * 128:(_t + 1) * 128, :])
        p.dma("sync", f"xin{_t}", f_pf, writes=[f"xin{_t}"])
    p.op("vector", lambda h: h.memset(consts[:, 0:1], EPS), writes=["c_eps"])
    p.op("vector", lambda h: h.memset(sc11[:, :], 0.0), writes=["sc11z"])
    p.op("vector", lambda h: h.memset(onesE[:, :], 1.0), writes=["onesE"])
    p.op("vector", lambda h: h.memset(onesf[:, :], 1.0), writes=["onesf"])
    p.op("vector", lambda h: h.memset(brow4[:, :], 0.0), writes=["brow"])
    e16i = A1[0:16, 0:2048].bitcast(mybir.dt.int32)
    def f_idi(h):
        return h.iota(out=idf[:, :].bitcast(mybir.dt.int32),
                      pattern=[[1, 128]], base=0, channel_multiplier=-1)
    p.op("gpsimd", f_idi, writes=["idfi"])
    p.op("vector", lambda h: h.tensor_scalar(
        out=idf[:, :], in0=idf[:, :].bitcast(mybir.dt.int32), scalar1=0.0,
        scalar2=None, op0=OP.is_equal), reads=["idfi"], writes=["idf"])
    def f_ei(h):
        return h.iota(out=e16i, pattern=[[2, 8], [1, 2], [0, 64]],
                      base=0, channel_multiplier=-1)
    p.op("gpsimd", f_ei, writes=["e16i", "A1c0"])
    p.op("vector", lambda h: h.tensor_scalar(
        out=e16f[:, :], in0=e16i, scalar1=0.0,
        scalar2=None, op0=OP.is_equal), reads=["e16i", "A1c0"],
        writes=["e16", "A1c0"])
    for i in range(4):
        def f_wc(h, i=i):
            return h.dma_start(out=s_wattn[i * D:(i + 1) * D, :], in_=w_attn_h[i])
        p.dma("gpsimd", "wcast", f_wc, writes=[f"s_wattn{i}"])
        def f_bc(h, i=i):
            return h.dma_start(out=brow4[32 * i:32 * i + 1, :],
                               in_=b_attn_h[i].rearrange("(a d) -> a d", a=1))
        p.dma("gpsimd", "bload", f_bc, reads=["brow"], writes=["brow"])
    p.dma("gpsimd", "bload", lambda h: h.dma_start(
        out=stdp_s[:, :], in_=stdp_h.rearrange("(h a) -> h a", a=1)), writes=["stdp_s"])
    p.op("scalar", lambda h: h.activation(out=small[0:16, 0:1], in_=stdp_s[:, :],
                                          func=AF.Exp, bias=0.0, scale=-1.0),
         reads=["stdp_s"], writes=["sp_e"])
    p.op("vector", lambda h: h.tensor_scalar(
        out=small[0:16, 1:2], in0=small[0:16, 0:1], scalar1=1.0, scalar2=None,
        op0=OP.add), reads=["sp_e"], writes=["sp_e1"])
    p.op("vector", lambda h: h.reciprocal(out=stdp_sig[:, :], in_=small[0:16, 1:2]),
         reads=["sp_e1"], writes=["stdp_sig"])
    for wi in range(3):
        for c in range(DC):
            def f_wT(h, wi=wi, c=c):
                return h.dma_start(
                    out=W1[:, wi * 8192 + c * 1024: wi * 8192 + (c + 1) * 1024],
                    in_=s_wattn[wi * D:(wi + 1) * D, c * 128:(c + 1) * 128],
                    transpose=True)
            p.dma("scalar", "xbar", f_wT, reads=[f"s_wattn{wi}"],
                  writes=w1k(wi * 8192 + c * 1024, 1024))

    xin = [vf32(A2, 0, 1024), vf32(A2, 2048, 1024)]
    wrk = [vf32(A2, 4096, 1024), vf32(A2, 6144, 1024)]
    wrk2 = [vf32(A2, 8192 + 1024 * i, 512) for i in range(4)]
    bstg = [vbf(A2, 12288, 1024), vbf(A2, 13312, 1024)]
    ktile = [vbf(A2, 14336, 1024), vbf(A2, 15360, 1024)]
    vtile = [vbf(A2, 16384, 1024), vbf(A2, 17408, 1024)]
    qstg = [vbf(A2, 18432, 512), vbf(A2, 18944, 512)]
    wprep = [vf32(A2, 19456, 3072), vf32(A2, 25600, 3072)]
    wdqv = [vbf(W1x, 0, 3072), vbf(W1x, 0, 3072)]
    wi16v = vi16(W1x, 3072, 3072)
    ewrk = [vf32(W1x, 6720, 512), vf32(W1x, 7744, 512)]
    blkmx = vf32(W1x, 6144, 96)
    blks = vf32(W1x, 6336, 96)
    blki = vf32(W1x, 6528, 96)

    wsteps = []
    def mk_wstep(src, dst, dkey, it0, nrt, ncol, sid, buf=None, bkey="wprep"):
        if buf is None:
            buf = wprep
        def step():
            W = nrt * ncol
            nblk = W // 32
            for j in range(nrt):
                def f_ld(h, j=j):
                    return h.dma_start(
                        out=buf[sid][:, j * ncol:(j + 1) * ncol],
                        in_=src[(it0 + j) * 128:(it0 + j + 1) * 128, 0:ncol])
                p.dma("sync", f"wpl{sid}{j}", f_ld, writes=[f"{bkey}{sid}"])
            def f_red(h):
                return h.tensor_reduce(
                    out=blkmx[:, 0:nblk],
                    in_=buf[sid][:, 0:W].rearrange("p (b i) -> p b i", i=32),
                    axis=AX.X, op=OP.max, apply_absolute_value=True)
            p.op("vector", f_red, reads=[f"{bkey}{sid}"], writes=["blkmx"])
            def f_s(h):
                return h.tensor_scalar(
                    out=blks[:, 0:nblk], in0=blkmx[:, 0:nblk], scalar1=1.0 / 7.0,
                    scalar2=1e-12, op0=OP.mult, op1=OP.max)
            p.op("vector", f_s, reads=["blkmx"], writes=["blks"])
            p.op("vector", lambda h: h.reciprocal(out=blki[:, 0:nblk],
                                                  in_=blks[:, 0:nblk]),
                 reads=["blks"], writes=["blki"])
            def f_rnd(h):
                return h.tensor_tensor(
                    out=wi16v[:, 0:W].rearrange("p (b i) -> p b i", i=32),
                    in0=buf[sid][:, 0:W].rearrange("p (b i) -> p b i", i=32),
                    in1=blki[:, 0:nblk].unsqueeze(2).broadcast_to([128, nblk, 32]),
                    op=OP.mult)
            p.op("vector", f_rnd, reads=[f"{bkey}{sid}", "blki"], writes=["wi16"])
            def f_deq(h):
                return h.tensor_tensor(
                    out=wdqv[sid][:, 0:W].rearrange("p (b i) -> p b i", i=32),
                    in0=wi16v[:, 0:W].rearrange("p (b i) -> p b i", i=32),
                    in1=blks[:, 0:nblk].unsqueeze(2).broadcast_to([128, nblk, 32]),
                    op=OP.mult)
            p.op("vector", f_deq, reads=["wi16", "blks"], writes=["wdq"])
            for j in range(nrt):
                def f_st(h, j=j):
                    return h.dma_start(
                        out=dst[(it0 + j) * 128:(it0 + j + 1) * 128, 0:ncol],
                        in_=wdqv[sid][:, j * ncol:(j + 1) * ncol])
                p.dma("scalar", f"wps{sid}{j}", f_st, reads=["wdq"],
                      writes=[f"{dkey}_{it0 + j}_0"])
        return step
    ws_u = [mk_wstep(wu_h, s_wu, "s_wu", 2 * i, 2, D, i % 2)
            for i in range(12)]
    ws_g = [mk_wstep(wg_h, s_wg, "s_wg", 2 * i, 2, D, i % 2)
            for i in range(12)]
    ws_d = [mk_wstep(wd_h, s_wd, "s_wd", i, 1, IF, i % 2)
            for i in range(8)]
    wi_ = {"u": 0, "g": 0, "d": 0, "a": 0}
    def mk_drip(fam, lst):
        def drip(n):
            for _ in range(n):
                if wi_[fam] < len(lst):
                    lst[wi_[fam]]()
                    wi_[fam] += 1
        return drip
    drip_u = mk_drip("u", ws_u)
    drip_g = mk_drip("g", ws_g)
    drip_d = mk_drip("d", ws_d)
    drip_a = mk_drip("a", ws_u + ws_g + ws_d)

    def ln_stats(src_ap, src_keys, par, sink_ap, sink_key):
        c0, c1, c2, c3, c4, c5, c6 = (8 * par + j for j in range(7))
        def f_s0(h):
            return h.activation(out=sink_ap, in_=src_ap, func=AF.Identity,
                                bias=0.0, scale=1.0,
                                accum_out=stats[:, c0:c0 + 1])
        p.op("scalar", f_s0, reads=src_keys, writes=[f"st{c0}", sink_key])
        def f_s1(h):
            return h.activation(out=sink_ap, in_=src_ap, func=AF.Square,
                                bias=0.0, scale=1.0,
                                accum_out=stats[:, c1:c1 + 1])
        p.op("scalar", f_s1, reads=src_keys, writes=[f"st{c1}", sink_key])
        def f_m(h):
            return h.tensor_scalar(out=stats[:, c2:c2 + 1],
                                   in0=stats[:, c0:c0 + 1], scalar1=1.0 / D,
                                   scalar2=None, op0=OP.mult)
        p.op("vector", f_m, reads=[f"st{c0}"], writes=[f"st{c2}"])
        def f_m2(h):
            return h.tensor_tensor(out=stats[:, c3:c3 + 1],
                                   in0=stats[:, c2:c2 + 1],
                                   in1=stats[:, c2:c2 + 1], op=OP.mult)
        p.op("vector", f_m2, reads=[f"st{c2}"], writes=[f"st{c3}"])
        def f_v(h):
            return h.scalar_tensor_tensor(
                out=stats[:, c4:c4 + 1], in0=stats[:, c1:c1 + 1],
                scalar=1.0 / D, in1=stats[:, c3:c3 + 1],
                op0=OP.mult, op1=OP.subtract)
        p.op("vector", f_v, reads=[f"st{c1}", f"st{c3}"], writes=[f"st{c4}"])
        def f_l(h):
            return h.activation(out=stats[:, c5:c5 + 1], in_=stats[:, c4:c4 + 1],
                                func=AF.Ln, bias=consts[:, 0:1], scale=1.0)
        p.op("scalar", f_l, reads=[f"st{c4}", "c_eps"], writes=[f"st{c5}"])
        def f_e(h):
            return h.activation(out=stats[:, c6:c6 + 1], in_=stats[:, c5:c5 + 1],
                                func=AF.Exp, bias=0.0, scale=-0.5)
        p.op("scalar", f_e, reads=[f"st{c5}"], writes=[f"st{c6}"])
        return c2, c6

    for t in range(NT):
        par = t % 2
        if t >= 2:
            def f_x(h, t=t, par=par):
                return h.dma_start(out=xin[par], in_=x_h[t * 128:(t + 1) * 128, :])
            p.dma("sync", f"xin{par}", f_x, writes=[f"xin{par}"])
        cm, cr = ln_stats(xin[par], [f"xin{par}"], par, wrk[par], f"wrk{par}")
        def f_ln(h, par=par, cm=cm, cr=cr):
            return h.tensor_scalar(out=bstg[par], in0=xin[par],
                                   scalar1=stats[:, cm:cm + 1],
                                   scalar2=stats[:, cr:cr + 1],
                                   op0=OP.subtract, op1=OP.mult)
        p.op("vector", f_ln, reads=[f"xin{par}", f"st{8*par+2}", f"st{8*par+6}"],
             writes=[f"bstg{par}"])
        def f_l1(h, t=t, par=par):
            return h.dma_start(out=s_ln1q[t * 128:(t + 1) * 128, :],
                               in_=bstg[par])
        p.dma("gpsimd", f"ln1st{par}", f_l1, reads=[f"bstg{par}"],
              writes=[f"s_ln1q_t{t}"])
        drip_a(1)

    ln1_done = [f"s_ln1q_t{t}" for t in range(NT)]
    for c in range(DC):
        def f_lT(h, c=c):
            return h.dma_start(out=A1[:, c * SEQ:(c + 1) * SEQ],
                               in_=s_ln1q[:, c * 128:(c + 1) * 128],
                               transpose=True)
        p.dma("scalar", "xbar", f_lT, reads=ln1_done, writes=[f"A1c{c}"])

    def elu1p_evac(bank, out_ap, out_keys, wb):
        def f_r(h, bank=bank, wb=wb):
            return h.activation(out=wrk2[wb], in_=pb[bank][:, :], func=AF.Relu,
                                bias=0.0, scale=1.0)
        p.op("scalar", f_r, reads=[], writes=[f"er{wb}", PS[bank]])
        def f_m(h, bank=bank, wb=wb):
            return h.tensor_tensor(out=wrk2[(wb + 2) % 4], in0=pb[bank][:, :],
                                   in1=wrk2[wb], op=OP.subtract)
        p.op("vector", f_m, reads=[f"er{wb}"], writes=[f"em{wb}", PS[bank]])
        def f_e(h, wb=wb):
            return h.activation(out=ewrk[wb % 2], in_=wrk2[(wb + 2) % 4],
                                func=AF.Exp, bias=0.0, scale=1.0)
        p.op("scalar", f_e, reads=[f"em{wb}"], writes=[f"ee{wb}"])
        def f_o(h, wb=wb, out_ap=out_ap):
            return h.tensor_tensor(out=out_ap, in0=ewrk[wb % 2],
                                   in1=wrk2[wb], op=OP.add)
        p.op("vector", f_o, reads=[f"ee{wb}", f"er{wb}"], writes=out_keys)

    for co in range(DC):
        for nch in range(NCH):
            bank = (co * NCH + nch) % 2
            for dc in range(DC):
                def f_mm(h, co=co, nch=nch, dc=dc, bank=bank):
                    return h.matmul(
                        pb[bank][:, :],
                        W1[:, dc * 1024 + co * 128: dc * 1024 + co * 128 + 128],
                        A1[:, dc * SEQ + nch * 512: dc * SEQ + (nch + 1) * 512],
                        start=(dc == 0), stop=False)
                p.op("tensor", f_mm, reads=w1k(dc * 1024 + co * 128) + a1k(dc),
                     writes=[PS[bank]], quiet=True)
            def f_bq(h, co=co, bank=bank):
                return h.matmul(pb[bank][:, :],
                                brow4[0:1, co * 128: co * 128 + 128],
                                onesE[0:1, :], start=False, stop=True)
            p.op("tensor", f_bq, reads=["brow", "onesE"], writes=[PS[bank]])
            if co == 0 and nch == 0 and "d_z0" in dbg:
                zdump = vf32(A2, 26816, 512)
                def f_zd(h, bank=bank, zdump=zdump):
                    return h.activation(out=zdump, in_=pb[bank][:, :],
                                        func=AF.Identity, bias=0.0, scale=1.0)
                p.op("scalar", f_zd, reads=[], writes=["zdump", PS[bank]])
                p.dma("gpsimd", "dbgz", (lambda zdump=zdump: lambda h: h.dma_start(
                    out=dbg["d_z0"], in_=zdump))(), reads=["zdump"], writes=["dz0"])
            elu1p_evac(bank, qstg[bank], [f"qstg{bank}"], bank)
            def f_qs(h, co=co, nch=nch, bank=bank):
                return h.dma_start(
                    out=s_qT[co * 128:(co + 1) * 128, nch * 512:(nch + 1) * 512],
                    in_=qstg[bank])
            p.dma("sync", f"qsp{bank}", f_qs, reads=[f"qstg{bank}"],
                  writes=[f"s_qT_{co}_{nch}"])

    first_kv = [True, True]

    def emit_kvsum(t):
        par = t % 2
        for hh in range(H):
            rlo = 64 * (hh % 2)
            clo = (hh // 2) * 64
            st = (t == 0 and hh < 2)
            def f_kv(h, par=par, hh=hh, rlo=rlo, clo=clo, st=st,
                     last=(t == NT - 1 and hh == H - 1)):
                return h.matmul(pb[6][rlo:rlo + 64, clo:clo + 64],
                                ktile[par][:, hh * 64:(hh + 1) * 64],
                                vtile[par][:, hh * 64:(hh + 1) * 64],
                                start=st, stop=last, skip_group_check=True)
            p.op("tensor", f_kv, reads=[f"ktile{par}", f"vtile{par}"],
                 writes=["ps6"], quiet=(hh != H - 1))
        for pr in range(8):
            st = (t == 0 and pr == 0)
            def f_ks(h, par=par, pr=pr, st=st, last=(t == NT - 1 and pr == 7)):
                return h.matmul(pb[7][:, pr:pr + 1],
                                ktile[par][:, pr * 128:(pr + 1) * 128],
                                onesE[:, 0:1], start=st, stop=last,
                                skip_group_check=True)
            p.op("tensor", f_ks, reads=[f"ktile{par}", "onesE"], writes=["ps7"],
                 quiet=(pr != 7))

    for t in range(NT):
        par = t % 2
        for wsel, lab in ((1, "k"), (2, "v")):
            for oc in range(2):
                bank = 2 + (wsel - 1) * 2 + oc
                for dc in range(DC):
                    def f_mm(h, wsel=wsel, t=t, oc=oc, dc=dc, bank=bank):
                        return h.matmul(
                            pb[bank][:, :],
                            A1[:, dc * SEQ + t * 128: dc * SEQ + t * 128 + 128],
                            W1[:, wsel * 8192 + dc * 1024 + oc * 512:
                                  wsel * 8192 + dc * 1024 + (oc + 1) * 512],
                            start=(dc == 0), stop=False)
                    p.op("tensor", f_mm,
                         reads=a1k(dc) + w1k(wsel * 8192 + dc * 1024, 1024),
                         writes=[PS[bank]], quiet=True)
                def f_bb(h, wsel=wsel, oc=oc, bank=bank):
                    return h.matmul(
                        pb[bank][:, :], onesE[32 * wsel:32 * wsel + 1, 0:128],
                        brow4[32 * wsel:32 * wsel + 1, oc * 512:(oc + 1) * 512],
                        start=False, stop=True)
                p.op("tensor", f_bb, reads=["onesE", "brow"], writes=[PS[bank]])
                if lab == "k":
                    elu1p_evac(bank, ktile[par][:, oc * 512:(oc + 1) * 512],
                               [f"ktile{par}"], oc)
                else:
                    def f_v(h, par=par, oc=oc, bank=bank):
                        return h.activation(
                            out=vtile[par][:, oc * 512:(oc + 1) * 512],
                            in_=pb[bank][:, :], func=AF.Identity, bias=0.0,
                            scale=1.0)
                    p.op("scalar", f_v, reads=[], writes=[f"vtile{par}", PS[bank]])
        if t >= 1:
            emit_kvsum(t - 1)
    emit_kvsum(NT - 1)
    drip_a(99)

    if stage == "A":
        if "ln1q" in dbg or True:
            pass
        _dumps_A(p, dbg, pb, s_ln1q, s_qT, s_wu, s_wd, debug)
        p.barrier()
        p.replay()
        return nc, dbg

    p.op("vector", lambda h: h.tensor_copy(out=kv_sb[:, :], in_=pb[6][:, :]),
         reads=[], writes=["kv_sb", "ps6"])
    p.op("vector", lambda h: h.memset(ksumbd[:, :], 0.0), writes=["ksumbd"])
    for c in range(DC):
        def f_k0(h, c=c):
            return h.tensor_copy(
                out=ksumbd[0:64, c * 16 + 2 * c: c * 16 + 2 * c + 1],
                in_=pb[7][0:64, c:c + 1])
        p.op("vector", f_k0, reads=["ksumbd"], writes=["ksumbd", "ps7"])
        def f_k1(h, c=c):
            return h.tensor_copy(
                out=ksumbd[64:128, c * 16 + 2 * c + 1: c * 16 + 2 * c + 2],
                in_=pb[7][64:128, c:c + 1])
        p.op("vector", f_k1, reads=["ksumbd"], writes=["ksumbd", "ps7"])

    p.barrier()

    zB = [vf32(A1, 0, 1024), vf32(A1, 2048, 1024)]
    x1B = [vf32(A1, 4096, 1024), vf32(A1, 6144, 1024)]
    xinB = [vf32(A1, 8192, 1024), vf32(A1, 10240, 1024)]
    bccB = [vf32(A1, 12288, 512), vf32(A1, 19456, 512)]
    sinkB = [vf32(A1, 13312, 1024), vf32(A1, 15360, 1024)]
    bstgB = [vbf(A1, 17408, 1024), vbf(A1, 18432, 1024)]

    for c in range(DC):
        q = "sync" if c % 2 == 0 else "scalar"
        def f_ql(h, c=c):
            return h.dma_start(out=A2[:, c * SEQ:(c + 1) * SEQ],
                               in_=s_qT[c * 128:(c + 1) * 128, :])
        p.dma(q, f"qld{c % 2}", f_ql,
              reads=[f"s_qT_{c}_{n}" for n in range(NCH)], writes=[f"A2c{c}"])
    for c in range(DC):
        def f_wo(h, c=c):
            return h.dma_start(out=W1[:, c * 1024:(c + 1) * 1024],
                               in_=s_wattn[3 * D:4 * D, c * 128:(c + 1) * 128],
                               transpose=True)
        p.dma("scalar", "xbar", f_wo, reads=["s_wattn3"],
              writes=w1k(c * 1024, 1024))

    for c in range(3, DC):
        def f_u(h, c=c):
            return h.dma_start(out=W1[:, c * 3072:(c + 1) * 3072],
                               in_=s_wu[:, c * 128:(c + 1) * 128],
                               transpose=True)
        p.dma("scalar", "xbar", f_u,
              reads=[f"s_wu_{it}_0" for it in range(IF // 128)],
              writes=w1k(c * 3072, 3072))

    for nch in range(NCH):
        dbank = 4 + nch % 2
        for c in range(DC):
            def f_dn(h, c=c, nch=nch, dbank=dbank):
                return h.matmul(
                    pb[dbank][0:16, :], ksumbd[:, c * 16:(c + 1) * 16],
                    A2[:, c * SEQ + nch * 512: c * SEQ + (nch + 1) * 512],
                    start=(c == 0), stop=(c == DC - 1))
            p.op("tensor", f_dn, reads=["ksumbd", f"A2c{c}"], writes=[PS[dbank]],
                 quiet=(c != DC - 1))
        for pr in range(8):
            bank = 2 + pr % 2
            for rlo in (0, 64):
                def f_nm(h, pr=pr, rlo=rlo, nch=nch, bank=bank):
                    return h.matmul(
                        pb[bank][rlo:rlo + 64, :],
                        kv_sb[rlo:rlo + 64, pr * 64:(pr + 1) * 64],
                        A2[rlo:rlo + 64,
                           pr * SEQ + nch * 512: pr * SEQ + (nch + 1) * 512],
                        start=True, stop=True, skip_group_check=True)
                p.op("tensor", f_nm, reads=["kv_sb", f"A2c{pr}"],
                     writes=[PS[bank]], quiet=(rlo == 0))
            if pr == 0:
                def f_r1(h, dbank=dbank):
                    return h.tensor_scalar(
                        out=rden[:, :], in0=pb[dbank][0:16, :], scalar1=1e-6,
                        scalar2=None, op0=OP.add)
                p.op("vector", f_r1, reads=[], writes=["rden", PS[dbank]])
                p.op("vector", lambda h: h.reciprocal(out=rden[:, :],
                                                      in_=rden[:, :]),
                     reads=["rden"], writes=["rden"])
                def f_r3(h):
                    return h.tensor_scalar(out=rdenb[:, :], in0=rden[:, :],
                                           scalar1=stdp_sig[:, :], scalar2=None,
                                           op0=OP.mult)
                p.op("vector", f_r3, reads=["rden", "stdp_sig"],
                     writes=["rdenb"])
            bbank = 6 + pr % 2
            def f_bc(h, pr=pr, bbank=bbank):
                return h.matmul(pb[bbank][:, :], e16f[:, pr * 128:(pr + 1) * 128],
                                rdenb[:, :], start=True, stop=True)
            p.op("tensor", f_bc, reads=["e16", "rdenb"], writes=[PS[bbank]])
            def f_bcc(h, pr=pr, bbank=bbank):
                return h.activation(out=bccB[pr % 2], in_=pb[bbank][:, :],
                                    func=AF.Identity, bias=0.0, scale=1.0)
            p.op("scalar", f_bcc, reads=[], writes=[f"bcc{pr % 2}", PS[bbank]])
            def f_at(h, pr=pr, nch=nch, bank=bank):
                return h.tensor_tensor(
                    out=A2[:, pr * SEQ + nch * 512: pr * SEQ + (nch + 1) * 512],
                    in0=pb[bank][:, :], in1=bccB[pr % 2], op=OP.mult)
            p.op("vector", f_at, reads=[f"bcc{pr % 2}"],
                 writes=[f"A2c{pr}", PS[bank]])

    p.op("vector", lambda h: h.memset(mroll[:, 0:1], 0.0), writes=["m0roll"])

    def b_stage1(t):
        par = t % 2
        for oc in range(2):
            bank = (t * 2 + oc) % 4
            for fc in range(DC):
                def f_wo(h, t=t, oc=oc, fc=fc, bank=bank):
                    return h.matmul(
                        pb[bank][:, :],
                        A2[:, fc * SEQ + t * 128: fc * SEQ + t * 128 + 128],
                        W1[:, fc * 1024 + oc * 512: fc * 1024 + (oc + 1) * 512],
                        start=(fc == 0), stop=False)
                p.op("tensor", f_wo, reads=[f"A2c{fc}"] + w1k(fc * 1024, 1024),
                     writes=[PS[bank]], quiet=True)
            def f_bo(h, oc=oc, bank=bank):
                return h.matmul(pb[bank][:, :], onesE[96:97, 0:128],
                                brow4[96:97, oc * 512:(oc + 1) * 512],
                                start=False, stop=True, tile_position=(96, 0))
            p.op("tensor", f_bo, reads=["onesE", "brow"], writes=[PS[bank]])
            def f_z(h, par=par, oc=oc, bank=bank):
                return h.activation(out=zB[par][:, oc * 512:(oc + 1) * 512],
                                    in_=pb[bank][:, :], func=AF.Identity,
                                    bias=0.0, scale=1.0)
            p.op("scalar", f_z, reads=[], writes=[f"z{par}", PS[bank]])
        def f_x(h, t=t, par=par):
            return h.dma_start(out=xinB[par], in_=x_h[t * 128:(t + 1) * 128, :])
        p.dma("sync", f"xin{par}", f_x, writes=[f"xinB{par}"])

    def b_stage2(t):
        par = t % 2
        cm, cr = ln_stats(zB[par], [f"z{par}"], par, sinkB[par], f"sink{par}")
        def f_lna(h, par=par, cm=cm, cr=cr):
            return h.tensor_scalar(out=x1B[par], in0=zB[par],
                                   scalar1=stats[:, cm:cm + 1],
                                   scalar2=stats[:, cr:cr + 1],
                                   op0=OP.subtract, op1=OP.mult)
        p.op("vector", f_lna, reads=[f"z{par}", f"st{8*par+2}", f"st{8*par+6}"],
             writes=[f"x1w{par}"])
        def f_x1(h, par=par):
            return h.tensor_tensor(out=x1B[par], in0=x1B[par], in1=xinB[par],
                                   op=OP.add)
        p.op("vector", f_x1, reads=[f"x1w{par}", f"xinB{par}"],
             writes=[f"x1w{par}"])
        def f_xs(h, t=t, par=par):
            return h.dma_start(out=s_x1[t * 128:(t + 1) * 128, :], in_=x1B[par])
        p.dma("gpsimd", f"x1st{par}", f_xs, reads=[f"x1w{par}"],
              writes=[f"s_x1_t{t}"])
        cm2, cr2 = ln_stats(x1B[par], [f"x1w{par}"], par, sinkB[par],
                            f"sink{par}")
        def f_l2(h, par=par, cm2=cm2, cr2=cr2):
            return h.tensor_scalar(out=xinB[par], in0=x1B[par],
                                   scalar1=stats[:, cm2:cm2 + 1],
                                   scalar2=stats[:, cr2:cr2 + 1],
                                   op0=OP.subtract, op1=OP.mult)
        p.op("vector", f_l2, reads=[f"x1w{par}", f"st{8*par+2}", f"st{8*par+6}"],
             writes=[f"ln2f{par}", f"xinB{par}"])
        def f_m0(h, t=t, par=par):
            return h.tensor_reduce(out=mxcol[:, 384 + t:385 + t],
                                   in_=xinB[par], axis=AX.X,
                                   op=OP.max, apply_absolute_value=True)
        p.op("vector", f_m0, reads=[f"ln2f{par}", f"xinB{par}"],
             writes=[f"m0c{t}"])
        def f_l2s(h, t=t, par=par):
            return h.dma_start(out=s_ln2[t * 128:(t + 1) * 128, :],
                               in_=xinB[par])
        p.dma("gpsimd", f"ln2st{par}", f_l2s, reads=[f"ln2f{par}", f"xinB{par}"],
              writes=[f"s_ln2_t{t}"])

    for t in range(NT + 1):
        if t < NT:
            b_stage1(t)
        if t >= 1:
            b_stage2(t - 1)
    def f_m0f(h):
        return h.tensor_reduce(out=mroll[:, 0:1], in_=mxcol[:, 384:416],
                               axis=AX.X, op=OP.max)
    p.op("vector", f_m0f, reads=[f"m0c{i}" for i in range(NT)],
         writes=["m0roll"])

    _dumps_B(p, dbg, s_x1, s_ln2, s_qT, debug, A2, mroll, sc11)
    if stage == "B":
        p.barrier()
        p.replay()
        return nc, dbg

    def cross_core_max(cols, rkeys, tag):
        n = len(cols)
        for j, col in enumerate(cols):
            def f_z(h):
                return h.memset(scr32[:, :], 0.0)
            p.op("vector", f_z, writes=["scr32"])
            def f_cp(h, col=col):
                return h.tensor_copy(out=scr32[:, 0:1], in_=mroll[:, col:col + 1])
            p.op("vector", f_cp, reads=rkeys + ["scr32"], writes=["scr32"])
            def f_tr(h):
                return h.tensor_reduce(out=t128[:, 0:1], in_=scr32[:, :],
                                       axis=AX.X, op=OP.max,
                                       apply_transpose=True)
            p.op("vector", f_tr, reads=["scr32"], writes=["t128a"])
            for b in range(4):
                def f_gb(h, j=j, b=b):
                    return h.dma_start(out=prow[0:1, j * 4 + b:j * 4 + b + 1],
                                       in_=t128[32 * b:32 * b + 1, 0:1])
                p.dma("sync", "ccg", f_gb, reads=["t128a"],
                      writes=[f"prow{j}"])
            def f_g(h, j=j):
                return h.tensor_reduce(out=sc11[0:1, j:j + 1],
                                       in_=prow[0:1, j * 4:j * 4 + 4],
                                       axis=AX.X, op=OP.max)
            p.op("vector", f_g, reads=[f"prow{j}"], writes=[f"sc11_{j}"])
        def f_s(h, n=n):
            return h.dma_start(out=cc_in[0:1, 0:n], in_=sc11[0:1, 0:n])
        p.dma("gpsimd", "ccio", f_s, reads=[f"sc11_{j}" for j in range(n)],
              writes=["cc_in"])
        def f_c(h, n=n):
            return h.collective_compute("AllReduce", OP.max,
                                        replica_groups=[list(range(N_CORES))],
                                        ins=[cc_in[0:1, 0:n]],
                                        outs=[cc_out[0:1, 0:n]])
        p.cc(f"cc_{tag}", f_c, reads=["cc_in"], writes=["cc_out"])
        def f_l(h, n=n):
            return h.dma_start(out=sc11[0:1, 8:8 + n], in_=cc_out[0:1, 0:n])
        p.dma("gpsimd", "ccio", f_l, reads=["cc_out"], writes=["sc11g"])

    def bcast_scales(srcs, dsts, reads):
        for c, dpos in zip(srcs, dsts):
            def f_b(h, c=c):
                return h.matmul(pb[5][:, 0:1], onesf[0:1, :],
                                sc11[0:1, c:c + 1], start=True, stop=True)
            p.op("tensor", f_b, reads=list(reads) + ["onesf"], writes=["ps5"])
            def f_cp(h, dpos=dpos):
                return h.tensor_copy(out=scales[:, dpos:dpos + 1],
                                     in_=pb[5][:, 0:1])
            p.op("vector", f_cp, reads=[], writes=[f"scl{dpos}", "ps5"])

    if stage == "P2a0":
        p.barrier(); p.replay(); return nc, dbg
    cross_core_max([0], ["m0roll"], "ar0")
    if stage == "P2a":
        p.barrier(); p.replay(); return nc, dbg

    p.barrier(exclude=("cc_", "ccio", "ccp"))
    p.op("vector", lambda h: h.tensor_scalar(
        out=sc11[0:1, 12:13], in0=sc11[0:1, 8:9], scalar1=1.0 / QMAX,
        scalar2=1e-12, op0=OP.mult, op1=OP.max), reads=["sc11g"],
        writes=["v_s0"])
    p.op("vector", lambda h: h.reciprocal(out=sc11[0:1, 13:14],
                                          in_=sc11[0:1, 12:13]),
         reads=["v_s0"], writes=["v_is0"])
    bcast_scales([12, 13], [0, 1], ["v_s0", "v_is0"])
    wu_done = [f"s_wu_{it}_0" for it in range(IF // 128)]
    wg_done = [f"s_wg_{it}_0" for it in range(IF // 128)]
    for c in range(3):
        def f_u(h, c=c):
            return h.dma_start(out=W1[:, c * 3072:(c + 1) * 3072],
                               in_=s_wu[:, c * 128:(c + 1) * 128],
                               transpose=True)
        p.dma("scalar", "xbar", f_u, reads=wu_done,
              writes=w1k(c * 3072, 3072))

    ln2_done = [f"s_ln2_t{t}" for t in range(NT)]
    STG = 24576
    for c in range(DC):
        par = c % 2
        def f_t(h, c=c, par=par):
            return h.dma_start(
                out=A2[:, STG + par * 4096: STG + (par + 1) * 4096],
                in_=s_ln2[:, c * 128:(c + 1) * 128], transpose=True)
        p.dma("scalar", "xbar", f_t, reads=ln2_done,
              writes=[f"A2c{6 + par}"])
        for piece in range(2):
            def f_r(h, par=par, piece=piece):
                return h.tensor_scalar(
                    out=i16c[par][:, :],
                    in0=A2[:, STG + par * 4096 + piece * 2048:
                              STG + par * 4096 + (piece + 1) * 2048],
                    scalar1=scales[:, 1:2], scalar2=None, op0=OP.mult)
            p.op("vector", f_r, reads=[f"A2c{6 + par}", "scl1"],
                 writes=[f"i16c{par}"])
            def f_q(h, c=c, par=par, piece=piece):
                return h.tensor_scalar(
                    out=A1[:, c * SEQ + piece * 2048:
                              c * SEQ + (piece + 1) * 2048],
                    in0=i16c[par][:, :], scalar1=scales[:, 0:1], scalar2=None,
                    op0=OP.mult)
            p.op("vector", f_q, reads=[f"i16c{par}", "scl0"], writes=[f"A1c{c}"])

    for c in range(DC):
        def f_g(h, c=c):
            return h.dma_start(out=A2[:, c * 3072:(c + 1) * 3072],
                               in_=s_wg[:, c * 128:(c + 1) * 128],
                               transpose=True)
        p.dma("scalar", "xbar", f_g, reads=wg_done, writes=a2k(c * 3072, 3072))

    if stage == "P2b":
        p.barrier(); p.replay(); return nc, dbg
    IC_LIM = 2 if stage in ("P2d", "P2e", "P2f", "P2g") else IC
    for wsel in range(2):
        for ich in range(IC_LIM):
            for nch in range(NCH):
                bank = (ich * NCH + nch + wsel * 2) % 4
                for dc in range(DC):
                    if wsel == 0:
                        lhs = W1[:, dc * 3072 + ich * 128:
                                    dc * 3072 + ich * 128 + 128]
                        lk = w1k(dc * 3072 + ich * 128)
                    else:
                        lhs = A2[:, dc * 3072 + ich * 128:
                                    dc * 3072 + ich * 128 + 128]
                        lk = a2k(dc * 3072 + ich * 128, 128)
                    def f_mm(h, lhs=lhs, dc=dc, nch=nch, bank=bank):
                        return h.matmul(
                            pb[bank][:, :], lhs,
                            A1[:, dc * SEQ + nch * 512:
                                  dc * SEQ + (nch + 1) * 512],
                            start=(dc == 0), stop=(dc == DC - 1))
                    p.op("tensor", f_mm, reads=lk + a1k(dc), writes=[PS[bank]],
                         quiet=(dc != DC - 1))
                gidx = wsel * 192 + ich * NCH + nch
                if stage not in ("P2e", "P2g"):
                    def f_mx(h, bank=bank, gidx=gidx):
                        return h.tensor_reduce(out=mxcol[:, gidx:gidx + 1],
                                               in_=pb[bank][:, :], axis=AX.X,
                                               op=OP.max, apply_absolute_value=True)
                    p.op("vector", f_mx, reads=[], writes=[f"mxc{gidx}", PS[bank]])
                sidx = (ich * NCH + nch + wsel) % 4
                if stage not in ("P2e", "P2f"):
                    def f_c(h, bank=bank, sidx=sidx):
                        return h.activation(out=sstg4[sidx][:, :],
                                            in_=pb[bank][:, :], func=AF.Identity,
                                            bias=0.0, scale=1.0)
                    p.op("scalar", f_c, reads=[], writes=[f"sstg{sidx}", PS[bank]])
                dst = s_up if wsel == 0 else s_gate
                dk = "up" if wsel == 0 else "gate"
                if stage not in ("P2c", "P2d", "P2e", "P2f", "P2g"):
                    def f_sp(h, dst=dst, ich=ich, nch=nch, sidx=sidx):
                        return h.dma_start(out=dst[ich, nch], in_=sstg4[sidx][:, :])
                    p.dma("sync", f"sp{sidx}", f_sp, reads=[f"sstg{sidx}"],
                          writes=[f"{dk}_{ich}_{nch}"])

    def f_m1f(h):
        return h.tensor_reduce(out=mroll[:, 1:2], in_=mxcol[:, 0:192], axis=AX.X,
                               op=OP.max)
    p.op("vector", f_m1f, reads=[f"mxc{i}" for i in range(192)],
         writes=["m12roll"])
    def f_m2f(h):
        return h.tensor_reduce(out=mroll[:, 2:3], in_=mxcol[:, 192:384], axis=AX.X,
                               op=OP.max)
    p.op("vector", f_m2f, reads=[f"mxc{i}" for i in range(192, 384)],
         writes=["m12roll"])
    if stage in ("P2c", "P2d", "P2e", "P2f", "P2g"):
        p.barrier(); p.replay(); return nc, dbg
    _dumps_P2(p, dbg, s_up, s_gate, debug)
    if "xqT" in debug:
        p.dma("gpsimd", "dbgx", lambda h: h.dma_start(
            out=dbg["d_xqT"], in_=A1[:, 0:SEQ]), reads=["A1c0"],
            writes=["dbgx1"])
        p.dma("gpsimd", "dbgw", lambda h: h.dma_start(
            out=dbg["d_wupT"], in_=W1[:, 0:3072]),
            reads=[f"W1_{i}" for i in range(3)], writes=["dbgw1"])
    if "scal" in debug and stage == "P2":
        p.dma("gpsimd", "dbgs", lambda h: h.dma_start(
            out=dbg["d_scal"], in_=sc11[0:1, 0:24]), reads=["v_is0"],
            writes=["dbgs1"])
        p.dma("gpsimd", "dbgs2", lambda h: h.dma_start(
            out=dbg["d_scal2"][0:1, 0:2], in_=scales[0:1, 0:2]), reads=["scl0", "scl1"],
            writes=["dbgs2"])
    if stage == "P2":
        p.barrier()
        p.replay()
        return nc, dbg

    cross_core_max([1, 2], ["m12roll"], "ar1")
    p.barrier(exclude=("cc_", "ccio", "ccp"))
    p.op("vector", lambda h: h.tensor_scalar(
        out=sc11[0:1, 14:15], in0=sc11[0:1, 8:9], scalar1=1.0 / QMAX,
        scalar2=1e-12, op0=OP.mult, op1=OP.max), reads=["sc11g"],
        writes=["v_s1"])
    p.op("vector", lambda h: h.reciprocal(out=sc11[0:1, 15:16],
                                          in_=sc11[0:1, 14:15]),
         reads=["v_s1"], writes=["v_is1"])
    p.op("vector", lambda h: h.tensor_scalar(
        out=sc11[0:1, 16:17], in0=sc11[0:1, 9:10], scalar1=1.0 / QMAX,
        scalar2=1e-12, op0=OP.mult, op1=OP.max), reads=["sc11g"],
        writes=["v_s2"])
    p.op("vector", lambda h: h.reciprocal(out=sc11[0:1, 17:18],
                                          in_=sc11[0:1, 16:17]),
         reads=["v_s2"], writes=["v_is2"])
    bcast_scales([14, 15, 16, 17], [2, 3, 4, 5],
                 ["v_s1", "v_is1", "v_s2", "v_is2"])

    g_in = [vbf(A1, 0, 2048), vbf(A1, 2048, 2048),
            vbf(A2, 0, 2048), vbf(A2, 2048, 2048)]
    g_out = [vbf(A1, 4096, 2048), vbf(A1, 6144, 2048),
             vbf(A2, 4096, 2048), vbf(A2, 6144, 2048)]
    sigfv = [vf32(A1, 8192, 2048), vf32(A1, 12288, 2048),
             vf32(A2, 8192, 2048), vf32(A2, 12288, 2048)]
    u_in = [vbf(A1, 16384, 2048), vbf(A1, 18432, 2048),
            vbf(A2, 16384, 2048), vbf(A2, 18432, 2048)]
    i16A = [vi16(A1, 20480, 2048), vi16(A1, 22528, 2048),
            vi16(A2, 20480, 2048), vi16(A2, 22528, 2048)]
    i16B = [vi16(A1, 24576, 2048), vi16(A1, 26624, 2048),
            vi16(A2, 24576, 2048), vi16(A2, 26624, 2048)]
    u16A = [vu16(A1, 28672, 2048), vu16(A1, 30720, 2048),
            vu16(A2, 28672, 2048), vu16(A2, 30720, 2048)]

    wd_done = [f"s_wd_{it}_0" for it in range(DC)]
    for c in range(IC):
        def f_d(h, c=c):
            return h.dma_start(out=W1[:, c * 1024:(c + 1) * 1024],
                               in_=s_wd[:, c * 128:(c + 1) * 128],
                               transpose=True)
        p.dma("scalar", "xbar", f_d, reads=wd_done, writes=w1k(c * 1024, 1024))

    p.op("vector", lambda h: h.memset(mroll[:, 3:4], 0.0), writes=["m3roll"])
    MT = (IC * NCH) // 4
    for mt in range(MT):
        par = mt % 4
        ich, nch4 = divmod(mt, 2)
        def f_ld(h, ich=ich, nch4=nch4, par=par):
            return h.dma_start(out=g_in[par],
                               in_=s_gate[ich, nch4 * 4:(nch4 + 1) * 4].transpose([1, 0, 2]))
        p.dma("sync", f"p3i{par}", f_ld,
              reads=[f"gate_{ich}_{nch4 * 4 + j}" for j in range(4)],
              writes=[f"g_in{par}"])
        def f_r2(h, par=par):
            return h.activation(out=i16A[par], in_=g_in[par], func=AF.Identity,
                                bias=0.0, scale=scales[:, 5:6])
        p.op("scalar", f_r2, reads=[f"g_in{par}", "scl5"], writes=[f"i16A{par}"])
        def f_sg(h, par=par):
            return h.activation(out=sigfv[par], in_=i16A[par], func=AF.Sigmoid,
                                bias=0.0, scale=scales[:, 4:5])
        p.op("scalar", f_sg, reads=[f"i16A{par}", "scl4"], writes=[f"sigf{par}"])
        def f_sq(h, par=par):
            return h.tensor_scalar(out=u16A[par], in0=sigfv[par],
                                   scalar1=65536.0, scalar2=None, op0=OP.mult)
        p.op("vector", f_sq, reads=[f"sigf{par}"], writes=[f"u16A{par}"])
        def f_g3(h, par=par):
            return h.tensor_tensor(out=g_out[par], in0=i16A[par],
                                   in1=u16A[par], op=OP.mult)
        p.op("vector", f_g3, reads=[f"i16A{par}", f"u16A{par}"],
             writes=[f"g_out{par}"])
        def f_mx(h, par=par):
            return h.tensor_reduce(out=small[:, 5:6], in_=g_out[par], axis=AX.X,
                                   op=OP.max, apply_absolute_value=True)
        p.op("vector", f_mx, reads=[f"g_out{par}"], writes=["m3t"])
        p.op("vector", lambda h: h.tensor_tensor(
            out=mroll[:, 3:4], in0=mroll[:, 3:4], in1=small[:, 5:6], op=OP.max),
            reads=["m3t", "m3roll"], writes=["m3roll"])
        def f_st(h, ich=ich, nch4=nch4, par=par):
            return h.dma_start(out=s_g3[ich, nch4 * 4:(nch4 + 1) * 4].transpose([1, 0, 2]),
                               in_=g_out[par])
        p.dma("gpsimd", f"p3o{par}", f_st, reads=[f"g_out{par}"],
              writes=[f"g3_{ich}_{nch4}"])

    _dumps_P3(p, dbg, s_g3, s_p, debug)
    if stage == "P3":
        p.barrier()
        p.replay()
        return nc, dbg

    cross_core_max([3], ["m3roll"], "ar2")
    p.op("vector", lambda h: h.scalar_tensor_tensor(
        out=sc11[0:1, 18:19], in0=sc11[0:1, 8:9], scalar=1.0 / 65536.0,
        in1=sc11[0:1, 16:17], op0=OP.mult, op1=OP.mult),
        reads=["sc11g", "v_s2"], writes=["v_m3t"])
    p.op("vector", lambda h: h.tensor_scalar(
        out=sc11[0:1, 19:20], in0=sc11[0:1, 18:19], scalar1=1.0 / QMAX,
        scalar2=1e-12, op0=OP.mult, op1=OP.max), reads=["v_m3t"],
        writes=["v_s3"])
    p.op("vector", lambda h: h.reciprocal(out=sc11[0:1, 20:21],
                                          in_=sc11[0:1, 19:20]),
         reads=["v_s3"], writes=["v_is3"])
    p.op("vector", lambda h: h.scalar_tensor_tensor(
        out=sc11[0:1, 21:22], in0=sc11[0:1, 20:21], scalar=1.0 / 65536.0,
        in1=sc11[0:1, 16:17], op0=OP.mult, op1=OP.mult),
        reads=["v_is3", "v_s2"], writes=["v_c43"])
    bcast_scales([21], [6], ["v_c43"])

    p.op("vector", lambda h: h.memset(mroll[:, 4:5], 0.0), writes=["m4roll"])
    for mt in range(MT):
        par = mt % 4
        ich, nch4 = divmod(mt, 2)
        def f_lg(h, ich=ich, nch4=nch4, par=par):
            return h.dma_start(out=g_in[par],
                               in_=s_g3[ich, nch4 * 4:(nch4 + 1) * 4].transpose([1, 0, 2]))
        p.dma("sync", f"p4g{par}", f_lg, reads=[f"g3_{ich}_{nch4}"],
              writes=[f"g_in{par}"])
        def f_lu(h, ich=ich, nch4=nch4, par=par):
            return h.dma_start(out=u_in[par],
                               in_=s_up[ich, nch4 * 4:(nch4 + 1) * 4].transpose([1, 0, 2]))
        p.dma("scalar", f"p4u{par}", f_lu,
              reads=[f"up_{ich}_{nch4 * 4 + j}" for j in range(4)],
              writes=[f"u_in{par}"])
        def f_a(h, par=par):
            return h.activation(out=i16A[par], in_=g_in[par], func=AF.Identity,
                                bias=0.0, scale=scales[:, 6:7])
        p.op("scalar", f_a, reads=[f"g_in{par}", "scl6"], writes=[f"i16A{par}"])
        def f_b(h, par=par):
            return h.activation(out=i16B[par], in_=u_in[par], func=AF.Identity,
                                bias=0.0, scale=scales[:, 3:4])
        p.op("scalar", f_b, reads=[f"u_in{par}", "scl3"], writes=[f"i16B{par}"])
        def f_p(h, par=par):
            return h.tensor_tensor(out=g_out[par], in0=i16A[par],
                                   in1=i16B[par], op=OP.mult)
        p.op("vector", f_p, reads=[f"i16A{par}", f"i16B{par}"],
             writes=[f"g_out{par}"])
        def f_mx(h, par=par):
            return h.tensor_reduce(out=small[:, 6:7], in_=g_out[par], axis=AX.X,
                                   op=OP.max, apply_absolute_value=True)
        p.op("vector", f_mx, reads=[f"g_out{par}"], writes=["m4t"])
        p.op("vector", lambda h: h.tensor_tensor(
            out=mroll[:, 4:5], in0=mroll[:, 4:5], in1=small[:, 6:7], op=OP.max),
            reads=["m4t", "m4roll"], writes=["m4roll"])
        def f_st(h, ich=ich, nch4=nch4, par=par):
            return h.dma_start(out=s_p[ich, nch4 * 4:(nch4 + 1) * 4].transpose([1, 0, 2]),
                               in_=g_out[par])
        p.dma("gpsimd", f"p4o{par}", f_st, reads=[f"g_out{par}"],
              writes=[f"p_{ich}_{nch4}"])

    if "g3p" in debug:
        p.dma("gpsimd", "dbgp", lambda h: h.dma_start(out=dbg["d_p"], in_=s_p),
              reads=[f"p_{i}_{n}" for i in range(IC) for n in range(2)],
              writes=["dbg10"])
    if stage == "P4":
        p.barrier()
        p.replay()
        return nc, dbg

    cross_core_max([4], ["m4roll"], "ar3")
    p.op("vector", lambda h: h.tensor_scalar(
        out=sc11[0:1, 22:23], in0=sc11[0:1, 19:20], scalar1=sc11[0:1, 14:15],
        scalar2=None, op0=OP.mult), reads=["v_s3", "v_s1"], writes=["v_sg4"])
    p.op("vector", lambda h: h.tensor_scalar(
        out=sc11[0:1, 10:11], in0=sc11[0:1, 8:9], scalar1=sc11[0:1, 22:23],
        scalar2=None, op0=OP.mult), reads=["sc11g", "v_sg4"], writes=["v_m4t"])
    p.op("vector", lambda h: h.tensor_scalar(
        out=sc11[0:1, 11:12], in0=sc11[0:1, 10:11], scalar1=1.0 / QMAX,
        scalar2=1e-12, op0=OP.mult, op1=OP.max), reads=["v_m4t"],
        writes=["v_s4"])
    p.op("vector", lambda h: h.reciprocal(out=sc11[0:1, 12:13],
                                          in_=sc11[0:1, 11:12]),
         reads=["v_s4"], writes=["v_is4"])
    p.op("vector", lambda h: h.tensor_scalar(
        out=sc11[0:1, 13:14], in0=sc11[0:1, 22:23], scalar1=sc11[0:1, 12:13],
        scalar2=None, op0=OP.mult), reads=["v_sg4", "v_is4"], writes=["v_c54"])
    bcast_scales([13, 11], [7, 8], ["v_c54", "v_s4"])

    p.barrier(exclude=("cc_", "ccio", "ccp"))
    o2t = [vbf(A1, 0, 1024), vbf(A1, 1024, 1024)]
    x1l = [vf32(A1, 2048, 1024), vf32(A1, 4096, 1024)]
    resl = [vf32(A1, 6144, 1024), vf32(A1, 8192, 1024)]
    sink5 = vf32(A1, 10240, 1024)
    p5stg = [vbf(A1, 12288, 2048), vbf(A1, 14336, 2048)]
    i16P = [vi16(A1, 16384, 2048), vi16(A1, 18432, 2048)]

    def p5_a(nch):
        par5 = nch % 2
        obase = par5 * 12288
        for grp in range(IC // 4):
            slot = grp % 2
            def f_lp(h, nch=nch, grp=grp, slot=slot):
                return h.dma_start(
                    out=p5stg[slot],
                    in_=s_p[grp * 4:(grp + 1) * 4, nch].transpose([1, 0, 2]))
            p.dma("sync", f"p5i{slot}", f_lp,
                  reads=[f"p_{grp * 4 + j}_{nch // 4}" for j in range(4)],
                  writes=[f"p5stg{slot}"])
            def f_oq(h, slot=slot):
                return h.activation(out=i16P[slot], in_=p5stg[slot],
                                    func=AF.Identity, bias=0.0,
                                    scale=scales[:, 7:8])
            p.op("scalar", f_oq, reads=[f"p5stg{slot}", "scl7"],
                 writes=[f"i16P{slot}"])
            def f_ot(h, grp=grp, obase=obase, slot=slot):
                return h.tensor_copy(
                    out=A2[:, obase + grp * 2048: obase + (grp + 1) * 2048],
                    in_=i16P[slot])
            p.op("vector", f_ot, reads=[f"i16P{slot}"],
                 writes=[f"ot{par5}_{grp}"])

    def p5_b(nch):
        par5 = nch % 2
        obase = par5 * 12288
        otk = [f"ot{par5}_{g}" for g in range(IC // 4)]
        for hc in range(DC):
            bank = hc % 4
            for ic in range(IC):
                def f_dn(h, hc=hc, ic=ic, obase=obase, bank=bank):
                    return h.matmul(
                        pb[bank][:, :],
                        W1[:, ic * 1024 + hc * 128: ic * 1024 + hc * 128 + 128],
                        A2[:, obase + ic * 512: obase + (ic + 1) * 512],
                        start=(ic == 0), stop=(ic == IC - 1))
                p.op("tensor", f_dn, reads=w1k(ic * 1024 + hc * 128) + otk,
                     writes=[PS[bank]], quiet=(ic != IC - 1))
            def f_ev(h, bank=bank, hc=hc):
                return h.activation(out=sstg4[hc % 4][:, :],
                                    in_=pb[bank][:, :], func=AF.Identity,
                                    bias=0.0, scale=scales[:, 8:9])
            p.op("scalar", f_ev, reads=["scl8"],
                 writes=[f"sstg{hc % 4}", PS[bank]])
            def f_so(h, hc=hc, nch=nch):
                return h.dma_start(
                    out=s_o2T[hc * 128:(hc + 1) * 128,
                              nch * 512:(nch + 1) * 512],
                    in_=sstg4[hc % 4][:, :])
            p.dma("gpsimd", f"o2T{hc % 4}", f_so, reads=[f"sstg{hc % 4}"],
                  writes=[f"o2T_{hc}_{nch}"])

    def p5_c(nch):
        for tt in range(4):
            t = nch * 4 + tt
            par = t % 2
            def f_tl(h, t=t, par=par):
                return h.dma_start(out=o2t[par],
                                   in_=s_o2T[:, t * 128:(t + 1) * 128],
                                   transpose=True)
            p.dma("scalar", "xbar", f_tl,
                  reads=[f"o2T_{hc}_{nch}" for hc in range(DC)],
                  writes=[f"o2t{par}"])
            def f_xl(h, t=t, par=par):
                return h.dma_start(out=x1l[par],
                                   in_=s_x1[t * 128:(t + 1) * 128, :])
            p.dma("sync", f"x1l{par}", f_xl, reads=[f"s_x1_t{t}"],
                  writes=[f"x1l{par}"])
            cm, cr = ln_stats(o2t[par], [f"o2t{par}"], par, sink5, "sink5")
            def f_ln(h, par=par, cm=cm, cr=cr):
                return h.tensor_scalar(out=resl[par], in0=o2t[par],
                                       scalar1=stats[:, cm:cm + 1],
                                       scalar2=stats[:, cr:cr + 1],
                                       op0=OP.subtract, op1=OP.mult)
            p.op("vector", f_ln, reads=[f"o2t{par}", f"st{8*par+2}",
                                        f"st{8*par+6}"], writes=[f"resl{par}"])
            def f_res(h, par=par):
                return h.tensor_tensor(out=resl[par], in0=resl[par],
                                       in1=x1l[par], op=OP.add)
            p.op("vector", f_res, reads=[f"resl{par}", f"x1l{par}"],
                 writes=[f"resl{par}"])
            def f_out(h, t=t, par=par):
                return h.dma_start(out=out_h[t * 128:(t + 1) * 128, :],
                                   in_=resl[par])
            p.dma("gpsimd", f"outst{par}", f_out, reads=[f"resl{par}"],
                  writes=[f"out_t{t}"])

    for k in range(NCH + 2):
        if k < NCH:
            p5_a(k)
        if 1 <= k <= NCH:
            p5_b(k - 1)
        if k >= 2:
            p5_c(k - 2)

    if "d_o2T" in dbg:
        p.dma("gpsimd", "dbgo2", lambda h: h.dma_start(
            out=dbg["d_o2T"], in_=s_o2T),
            reads=[f"o2T_{hc}_{nch}" for hc in range(DC) for nch in range(NCH)],
            writes=["dbgo2w"])
    if "d_scal" in dbg:
        p.dma("gpsimd", "dbgend", lambda h: h.dma_start(
            out=dbg["d_scal"], in_=sc11[0:1, :]),
            reads=["v_c54", "v_s4", "v_is4", "v_s1", "v_s2", "v_s3"],
            writes=["dbgs1"])
    p.barrier()
    p.replay()
    return nc, dbg


def _dumps_A(p, dbg, pb, s_ln1q, s_qT, s_wu, s_wd, debug):
    if "ln1q" in debug:
        p.dma("gpsimd", "dbg", lambda h: h.dma_start(
            out=dbg["d_ln1q"], in_=s_ln1q),
            reads=[f"s_ln1q_t{t}" for t in range(NT)], writes=["dbg1"])
    if "qT" in debug:
        p.dma("gpsimd", "dbg", lambda h: h.dma_start(out=dbg["d_qT"], in_=s_qT),
              reads=[f"s_qT_{c}_{n}" for c in range(DC) for n in range(NCH)],
              writes=["dbg2"])
    if "kv" in debug:
        def f1(h):
            return h.tensor_copy(out=dbg.setdefault("_kvsb", None) or None, in_=None)
    if "wdq" in debug:
        p.dma("gpsimd", "dbg", lambda h: h.dma_start(out=dbg["d_wu"], in_=s_wu),
              reads=[f"s_wu_{it}_0" for it in range(IF // 128)], writes=["dbg3"])
        p.dma("gpsimd", "dbg2", lambda h: h.dma_start(out=dbg["d_wd"], in_=s_wd),
              reads=[f"s_wd_{it}_{pc}" for it in range(DC) for pc in range(3)],
              writes=["dbg4"])


def _dumps_B(p, dbg, s_x1, s_ln2, s_qT, debug, A2, mroll, sc11):
    if "x1" in debug:
        p.dma("gpsimd", "dbg", lambda h: h.dma_start(out=dbg["d_x1"], in_=s_x1),
              reads=[f"s_x1_t{t}" for t in range(NT)], writes=["dbg5"])
    if "ln2" in debug:
        p.dma("gpsimd", "dbg2", lambda h: h.dma_start(out=dbg["d_ln2"], in_=s_ln2),
              reads=[f"s_ln2_t{t}" for t in range(NT)], writes=["dbg6"])


def _dumps_P2(p, dbg, s_up, s_gate, debug):
    if "upg" in debug:
        p.dma("gpsimd", "dbg", lambda h: h.dma_start(out=dbg["d_up"], in_=s_up),
              reads=[f"up_{i}_{n}" for i in range(IC) for n in range(NCH)],
              writes=["dbg7"])
        p.dma("gpsimd", "dbg2", lambda h: h.dma_start(
            out=dbg["d_gate"], in_=s_gate),
            reads=[f"gate_{i}_{n}" for i in range(IC) for n in range(NCH)],
            writes=["dbg8"])


def _dumps_P3(p, dbg, s_g3, s_p, debug):
    if "g3p" in debug:
        p.dma("gpsimd", "dbg", lambda h: h.dma_start(out=dbg["d_g3"], in_=s_g3),
              reads=[f"g3_{i}_{n}" for i in range(IC) for n in range(2)],
              writes=["dbg9"])


_NC_CACHE = {}


def kernel(**inputs):
    x = np.ascontiguousarray(np.asarray(inputs["L0_attn_x"], dtype=np.float32))
    per_core_common = {
        "wq": np.ascontiguousarray(inputs["L0_attn_Wq"], dtype=np.float32),
        "wk": np.ascontiguousarray(inputs["L0_attn_Wk"], dtype=np.float32),
        "wv": np.ascontiguousarray(inputs["L0_attn_Wv"], dtype=np.float32),
        "wo": np.ascontiguousarray(inputs["L0_attn_Wo"], dtype=np.float32),
        "bq": np.ascontiguousarray(inputs["L0_attn_bq"], dtype=np.float32),
        "bk": np.ascontiguousarray(inputs["L0_attn_bk"], dtype=np.float32),
        "bv": np.ascontiguousarray(inputs["L0_attn_bv"], dtype=np.float32),
        "bo": np.ascontiguousarray(inputs["L0_attn_bo"], dtype=np.float32),
        "stdp": np.ascontiguousarray(inputs["L0_attn_stdp"], dtype=np.float32),
        "wg": np.ascontiguousarray(inputs["L0_ffn_w_gate"], dtype=np.float32),
        "wu": np.ascontiguousarray(inputs["L0_ffn_w_up"], dtype=np.float32),
        "wd": np.ascontiguousarray(inputs["L0_ffn_w_down"], dtype=np.float32),
    }
    if "nc" not in _NC_CACHE:
        _NC_CACHE["nc"] = build_nc(stage="full")[0]
    nc = _NC_CACHE["nc"]
    in_maps = []
    for i in range(N_CORES):
        m = dict(per_core_common)
        m["x"] = np.ascontiguousarray(x[i])
        in_maps.append(m)
    res = run_bass_kernel_spmd(nc, in_maps, core_ids=list(range(N_CORES)))
    out = np.stack([np.asarray(res.results[i]["out"]) for i in range(N_CORES)],
                   axis=0)
    return out.astype(np.float32)


# version 25
# speedup vs baseline: 1.1677x; 1.0156x over previous
import numpy as np
from contextlib import ExitStack

import concourse.bass as bass
import concourse.mybir as mybir
from concourse.bass_utils import run_bass_kernel_spmd

F32 = mybir.dt.float32
BF16 = mybir.dt.bfloat16
I16 = mybir.dt.int16
U16 = mybir.dt.uint16
OP = mybir.AluOpType
AF = mybir.ActivationFunctionType
AX = mybir.AxisListType

N_CORES = 8
SEQ = 4096
D = 1024
H = 16
IF = 3072
NT = SEQ // 128
NCH = SEQ // 512
DC = D // 128
IC = IF // 128
QMAX = 32767.0
EPS = 1e-5

ENGS = ("tensor", "vector", "scalar", "gpsimd", "sync")


class Prog:

    def __init__(self, nc, es):
        self.nc = nc
        self.es = es
        self.eng = {n: dict(sem=es.enter_context(nc.semaphore(f"es_{n}")),
                            cnt=0, q=[]) for n in ENGS}
        self.know = {e: {} for e in ENGS}
        self.buf = {}
        self.streams = {}

    def _sem(self, kind, name):
        return self.eng[name]["sem"] if kind == "eng" else self.streams[name]["sem"]

    def _wait(self, consumer, kind, name, val):
        if kind == "eng" and name == consumer and val > self.eng[name]["cnt"]:
            return
        k = self.know[consumer]
        if k.get((kind, name), 0) >= val:
            return
        k[(kind, name)] = val
        self.eng[consumer]["q"].append(("wait", self._sem(kind, name), val))

    def _deps(self, consumer, reads, writes):
        for key in reads:
            b = self.buf.get(key)
            if b:
                for (kind, name), cnt in b["w"].items():
                    self._wait(consumer, kind, name, cnt)
        for key in writes:
            b = self.buf.get(key)
            if b:
                for (kind, name), cnt in b["w"].items():
                    self._wait(consumer, kind, name, cnt)
                for (kind, name), cnt in b["r"].items():
                    self._wait(consumer, kind, name, cnt)

    def _record(self, kind, pname, reads, writes, cnt):
        for key in writes:
            self.buf[key] = dict(w={(kind, pname): cnt}, r={})
        for key in reads:
            b = self.buf.setdefault(key, dict(w={}, r={}))
            b["r"][(kind, pname)] = cnt

    def op(self, ename, fn, reads=(), writes=(), quiet=False):
        e = self.eng[ename]
        self._deps(ename, reads, writes)
        if quiet:
            e["q"].append(("raw", lambda h, fn=fn: fn(h)))
            e.setdefault("pending", False)
            e["pending"] = True
            self._record("eng", ename, reads, writes, e["cnt"] + 1)
            return
        e["cnt"] += 1
        e["pending"] = False
        e["q"].append(("op", fn, e["sem"], 1))
        self._record("eng", ename, reads, writes, e["cnt"])

    def flush_quiet(self):
        for ename, e in self.eng.items():
            if e.get("pending"):
                e["cnt"] += 1
                e["pending"] = False
                e["q"].append(("op", lambda h: h.nop(), e["sem"], 1))

    def _stream(self, name):
        if name not in self.streams:
            self.streams[name] = dict(
                sem=self.es.enter_context(self.nc.semaphore(f"ds_{name}")), cnt=0)
        return self.streams[name]

    def dma(self, qname, stream, fn, reads=(), writes=()):
        s = self._stream(stream)
        if s["cnt"] > 0:
            self._wait(qname, "stream", stream, s["cnt"])
        self._deps(qname, reads, writes)
        s["cnt"] += 16
        self.eng[qname]["q"].append(("op", fn, s["sem"], 16))
        self._record("stream", stream, reads, writes, s["cnt"])

    def cc(self, stream, fn, reads=(), writes=()):
        s = self._stream(stream)
        if s["cnt"] > 0:
            self._wait("gpsimd", "stream", stream, s["cnt"])
        self._deps("gpsimd", reads, writes)
        s["cnt"] += 1
        self.eng["gpsimd"]["q"].append(("op", fn, s["sem"], 1))
        self._record("stream", stream, reads, writes, s["cnt"])

    def barrier(self, exclude=()):
        self.flush_quiet()
        targets = [("eng", n, self.eng[n]["cnt"]) for n in ENGS
                   if self.eng[n]["cnt"] > 0]
        targets += [("stream", sn, s["cnt"]) for sn, s in self.streams.items()
                    if s["cnt"] > 0 and not any(sn.startswith(e) for e in exclude)]
        for consumer in ENGS:
            for kind, name, cnt in targets:
                if kind == "eng" and name == consumer:
                    continue
                self._wait(consumer, kind, name, cnt)

    def replay(self):
        with self.nc.Block() as block:
            def mk(name):
                q = self.eng[name]["q"]
                def body(h):
                    for it in q:
                        if it[0] == "wait":
                            h.wait_ge(it[1], it[2])
                        elif it[0] == "raw":
                            it[1](h)
                        else:
                            it[1](h).then_inc(it[2], it[3])
                return body
            block.tensor(mk("tensor"))
            block.vector(mk("vector"))
            block.scalar(mk("scalar"))
            block.gpsimd(mk("gpsimd"))
            block.sync(mk("sync"))


def _keys(prefix, off, length, gran):
    return [f"{prefix}{i}" for i in range(off // gran, (off + length - 1) // gran + 1)]


def build_nc(stage="full", debug=()):
    nc = bass.Bass("TRN2", target_bir_lowering=False)
    es = ExitStack()

    x_h = nc.dram_tensor("x", [SEQ, D], F32, kind="ExternalInput").ap()
    w_attn_h = [nc.dram_tensor(n, [D, D], F32, kind="ExternalInput").ap()
                for n in ("wq", "wk", "wv", "wo")]
    b_attn_h = [nc.dram_tensor(n, [D], F32, kind="ExternalInput").ap()
                for n in ("bq", "bk", "bv", "bo")]
    stdp_h = nc.dram_tensor("stdp", [H], F32, kind="ExternalInput").ap()
    wg_h = nc.dram_tensor("wg", [IF, D], F32, kind="ExternalInput").ap()
    wu_h = nc.dram_tensor("wu", [IF, D], F32, kind="ExternalInput").ap()
    wd_h = nc.dram_tensor("wd", [D, IF], F32, kind="ExternalInput").ap()
    out_h = nc.dram_tensor("out", [SEQ, D], F32, kind="ExternalOutput").ap()

    s_wattn = nc.dram_tensor("s_wattn", [4 * D, D], BF16).ap()
    s_ln1q = nc.dram_tensor("s_ln1q", [SEQ, D], BF16).ap()
    s_qT = nc.dram_tensor("s_qT", [D, SEQ], BF16).ap()
    s_ln2 = nc.dram_tensor("s_ln2", [SEQ, D], BF16).ap()
    s_ln2T = nc.dram_tensor("s_ln2T", [D, SEQ], BF16).ap()
    s_x1 = nc.dram_tensor("s_x1", [SEQ, D], F32).ap()
    s_wu = nc.dram_tensor("s_wu", [IF, D], BF16).ap()
    s_wg = nc.dram_tensor("s_wg", [IF, D], BF16).ap()
    s_wd = nc.dram_tensor("s_wd", [D, IF], BF16).ap()
    s_up = nc.dram_tensor("s_up", [IC, NCH, 128, 512], BF16).ap()
    s_gate = nc.dram_tensor("s_gate", [IC, NCH, 128, 512], BF16).ap()
    s_g3 = nc.dram_tensor("s_g3", [IC, NCH, 128, 512], BF16).ap()
    s_p = nc.dram_tensor("s_p", [IC, NCH, 128, 512], BF16).ap()
    s_o2T = nc.dram_tensor("s_o2T", [D, SEQ], BF16).ap()
    s_part = [nc.dram_tensor(f"s_part{j}", [128, 1], F32).ap() for j in range(2)]
    cc_in = nc.dram_tensor("cc_in", [1, 4], F32).ap()
    cc_out = nc.dram_tensor("cc_out", [1, 4], F32).ap()

    dbg = {}
    def dbg_decl(nm, shp, dt=BF16):
        dbg[nm] = nc.dram_tensor(nm, shp, dt, kind="ExternalOutput").ap()
    if "ln1q" in debug: dbg_decl("d_ln1q", [SEQ, D])
    if "qT" in debug: dbg_decl("d_qT", [D, SEQ])
    if "xqT" in debug:
        dbg_decl("d_xqT", [128, SEQ])
        dbg_decl("d_wupT", [128, 3072])
    if "z0" in debug: dbg_decl("d_z0", [128, 512], F32)
    if "kv" in debug: dbg_decl("d_kv", [128, 512], F32)
    if "ksum" in debug: dbg_decl("d_ksum", [128, 8], F32)
    if "x1" in debug: dbg_decl("d_x1", [SEQ, D], F32)
    if "ln2" in debug: dbg_decl("d_ln2", [SEQ, D])
    if "attnT" in debug: dbg_decl("d_attnT", [D, SEQ])
    if "scal" in debug:
        dbg_decl("d_scal", [1, 24], F32)
        dbg_decl("d_scal2", [1, 12], F32)
    if "upg" in debug:
        dbg_decl("d_up", [IC, NCH, 128, 512])
        dbg_decl("d_gate", [IC, NCH, 128, 512])
    if "g3p" in debug:
        dbg_decl("d_g3", [IC, NCH, 128, 512])
        dbg_decl("d_p", [IC, NCH, 128, 512])
    if "wdq" in debug:
        dbg_decl("d_wu", [IF, D])
        dbg_decl("d_wd", [D, IF])
    if "o2T" in debug: dbg_decl("d_o2T", [D, SEQ])

    sb = lambda name, shape, dt: es.enter_context(nc.sbuf_tensor(name, shape, dt))

    A1 = sb("A1", [128, 8 * SEQ], BF16)
    A2 = sb("A2", [128, 8 * SEQ], BF16)
    W1 = sb("W1", [128, 24 * 1024], BF16)
    stats = sb("stats", [128, 32], F32)
    consts = sb("consts", [128, 2], F32)
    scales = sb("scales", [128, 12], F32)
    mroll = sb("mroll", [128, 8], F32)
    small = sb("small", [128, 8], F32)
    kv_sb = sb("kv_sb", [128, 512], BF16)
    ksumbd = sb("ksumbd", [128, 8 * 16], BF16)
    e16f = sb("e16f", [16, 8 * 128], BF16)
    rden = sb("rden", [16, 512], F32)
    rdenb = sb("rdenb", [16, 512], BF16)
    onesE = sb("onesE", [128, 512], BF16)
    onesf = sb("onesf", [1, 128], F32)
    brow4 = sb("brow4", [128, D], BF16)
    stdp_s = sb("stdp_s", [16, 1], F32)
    stdp_sig = sb("stdp_sig", [16, 1], F32)
    sc11 = sb("sc11", [1, 24], F32)
    mxcol = sb("mxcol", [128, 420], F32)
    prow = sb("prow", [1, 256], F32)
    idf = sb("idf", [128, 128], F32)
    W1x = sb("W1x", [128, 8768], BF16)
    i16c = [W1x[:, 0:2048].bitcast(I16), W1x[:, 2048:4096].bitcast(I16)]
    sstg4 = [W1x[:, 4096 + 512 * i:4096 + 512 * (i + 1)] for i in range(4)]

    pb = [es.enter_context(nc.psum_tensor(f"pb{i}", [128, 512], F32))
          for i in range(8)]
    PS = [f"ps{i}" for i in range(8)]

    p = Prog(nc, es)

    def vf32(arena, off, n):
        return arena[:, off:off + 2 * n].bitcast(F32)
    def vbf(arena, off, n):
        return arena[:, off:off + n]
    def vi16(arena, off, n):
        return arena[:, off:off + n].bitcast(I16)
    def vu16(arena, off, n):
        return arena[:, off:off + n].bitcast(U16)

    def w1k(off, length=128):
        return _keys("W1_", off, length, 1024)
    def a1k(c):
        return [f"A1c{c}"]
    def a2k(off, length):
        return _keys("A2c", off, length, 4096)

    p.op("vector", lambda h: h.memset(consts[:, 0:1], EPS), writes=["c_eps"])
    p.op("vector", lambda h: h.memset(sc11[:, :], 0.0), writes=["sc11z"])
    p.op("vector", lambda h: h.memset(onesE[:, :], 1.0), writes=["onesE"])
    p.op("vector", lambda h: h.memset(onesf[:, :], 1.0), writes=["onesf"])
    p.op("vector", lambda h: h.memset(brow4[:, :], 0.0), writes=["brow"])
    e16i = A1[0:16, 0:2048].bitcast(mybir.dt.int32)
    def f_idi(h):
        return h.iota(out=idf[:, :].bitcast(mybir.dt.int32),
                      pattern=[[1, 128]], base=0, channel_multiplier=-1)
    p.op("gpsimd", f_idi, writes=["idfi"])
    p.op("vector", lambda h: h.tensor_scalar(
        out=idf[:, :], in0=idf[:, :].bitcast(mybir.dt.int32), scalar1=0.0,
        scalar2=None, op0=OP.is_equal), reads=["idfi"], writes=["idf"])
    def f_ei(h):
        return h.iota(out=e16i, pattern=[[2, 8], [1, 2], [0, 64]],
                      base=0, channel_multiplier=-1)
    p.op("gpsimd", f_ei, writes=["e16i", "A1c0"])
    p.op("vector", lambda h: h.tensor_scalar(
        out=e16f[:, :], in0=e16i, scalar1=0.0,
        scalar2=None, op0=OP.is_equal), reads=["e16i", "A1c0"],
        writes=["e16", "A1c0"])
    for i in range(4):
        def f_wc(h, i=i):
            return h.dma_start(out=s_wattn[i * D:(i + 1) * D, :], in_=w_attn_h[i])
        p.dma("gpsimd", "wcast", f_wc, writes=[f"s_wattn{i}"])
        def f_bc(h, i=i):
            return h.dma_start(out=brow4[32 * i:32 * i + 1, :],
                               in_=b_attn_h[i].rearrange("(a d) -> a d", a=1))
        p.dma("gpsimd", "bload", f_bc, reads=["brow"], writes=["brow"])
    p.dma("gpsimd", "bload", lambda h: h.dma_start(
        out=stdp_s[:, :], in_=stdp_h.rearrange("(h a) -> h a", a=1)), writes=["stdp_s"])
    p.op("scalar", lambda h: h.activation(out=small[0:16, 0:1], in_=stdp_s[:, :],
                                          func=AF.Exp, bias=0.0, scale=-1.0),
         reads=["stdp_s"], writes=["sp_e"])
    p.op("vector", lambda h: h.tensor_scalar(
        out=small[0:16, 1:2], in0=small[0:16, 0:1], scalar1=1.0, scalar2=None,
        op0=OP.add), reads=["sp_e"], writes=["sp_e1"])
    p.op("vector", lambda h: h.reciprocal(out=stdp_sig[:, :], in_=small[0:16, 1:2]),
         reads=["sp_e1"], writes=["stdp_sig"])
    for wi in range(3):
        for c in range(DC):
            def f_wT(h, wi=wi, c=c):
                return h.dma_start(
                    out=W1[:, wi * 8192 + c * 1024: wi * 8192 + (c + 1) * 1024],
                    in_=s_wattn[wi * D:(wi + 1) * D, c * 128:(c + 1) * 128],
                    transpose=True)
            p.dma("scalar", "xbar", f_wT, reads=[f"s_wattn{wi}"],
                  writes=w1k(wi * 8192 + c * 1024, 1024))

    xin = [vf32(A2, 0, 1024), vf32(A2, 2048, 1024)]
    wrk = [vf32(A2, 4096, 1024), vf32(A2, 6144, 1024)]
    wrk2 = [vf32(A2, 8192 + 1024 * i, 512) for i in range(4)]
    bstg = [vbf(A2, 12288, 1024), vbf(A2, 13312, 1024)]
    ktile = [vbf(A2, 14336, 1024), vbf(A2, 15360, 1024)]
    vtile = [vbf(A2, 16384, 1024), vbf(A2, 17408, 1024)]
    qstg = [vbf(A2, 18432, 512), vbf(A2, 18944, 512)]
    wprep = [vf32(A2, 19456, 3072), vf32(A2, 25600, 3072)]
    wdqv = [vbf(W1x, 0, 3072), vbf(W1x, 0, 3072)]
    wi16v = vi16(W1x, 3072, 3072)
    ewrk = [vf32(W1x, 6720, 512), vf32(W1x, 7744, 512)]
    blkmx = vf32(W1x, 6144, 96)
    blks = vf32(W1x, 6336, 96)
    blki = vf32(W1x, 6528, 96)

    wsteps = []
    def mk_wstep(src, dst, dkey, it0, nrt, ncol, sid):
        def step():
            W = nrt * ncol
            nblk = W // 32
            for j in range(nrt):
                def f_ld(h, j=j):
                    return h.dma_start(
                        out=wprep[sid][:, j * ncol:(j + 1) * ncol],
                        in_=src[(it0 + j) * 128:(it0 + j + 1) * 128, 0:ncol])
                p.dma("sync", f"wpl{sid}{j}", f_ld, writes=[f"wprep{sid}"])
            def f_red(h):
                return h.tensor_reduce(
                    out=blkmx[:, 0:nblk],
                    in_=wprep[sid][:, 0:W].rearrange("p (b i) -> p b i", i=32),
                    axis=AX.X, op=OP.max, apply_absolute_value=True)
            p.op("vector", f_red, reads=[f"wprep{sid}"], writes=["blkmx"])
            def f_s(h):
                return h.tensor_scalar(
                    out=blks[:, 0:nblk], in0=blkmx[:, 0:nblk], scalar1=1.0 / 7.0,
                    scalar2=1e-12, op0=OP.mult, op1=OP.max)
            p.op("vector", f_s, reads=["blkmx"], writes=["blks"])
            p.op("vector", lambda h: h.reciprocal(out=blki[:, 0:nblk],
                                                  in_=blks[:, 0:nblk]),
                 reads=["blks"], writes=["blki"])
            def f_rnd(h):
                return h.tensor_tensor(
                    out=wi16v[:, 0:W].rearrange("p (b i) -> p b i", i=32),
                    in0=wprep[sid][:, 0:W].rearrange("p (b i) -> p b i", i=32),
                    in1=blki[:, 0:nblk].unsqueeze(2).broadcast_to([128, nblk, 32]),
                    op=OP.mult)
            p.op("vector", f_rnd, reads=[f"wprep{sid}", "blki"], writes=["wi16"])
            def f_deq(h):
                return h.tensor_tensor(
                    out=wdqv[sid][:, 0:W].rearrange("p (b i) -> p b i", i=32),
                    in0=wi16v[:, 0:W].rearrange("p (b i) -> p b i", i=32),
                    in1=blks[:, 0:nblk].unsqueeze(2).broadcast_to([128, nblk, 32]),
                    op=OP.mult)
            p.op("vector", f_deq, reads=["wi16", "blks"], writes=["wdq"])
            for j in range(nrt):
                def f_st(h, j=j):
                    return h.dma_start(
                        out=dst[(it0 + j) * 128:(it0 + j + 1) * 128, 0:ncol],
                        in_=wdqv[sid][:, j * ncol:(j + 1) * ncol])
                p.dma("scalar", f"wps{sid}{j}", f_st, reads=["wdq"],
                      writes=[f"{dkey}_{it0 + j}_0"])
        return step
    for i in range(12):
        wsteps.append(mk_wstep(wu_h, s_wu, "s_wu", 2 * i, 2, D, i % 2))
    for i in range(12):
        wsteps.append(mk_wstep(wg_h, s_wg, "s_wg", 2 * i, 2, D, i % 2))
    for i in range(8):
        wsteps.append(mk_wstep(wd_h, s_wd, "s_wd", i, 1, IF, i % 2))
    wstep_i = [0]
    def drip(n):
        for _ in range(n):
            if wstep_i[0] < len(wsteps):
                wsteps[wstep_i[0]]()
                wstep_i[0] += 1

    def ln_stats(src_ap, src_keys, par, sink_ap, sink_key, pre_sum=False):
        c0, c1, c2, c3, c4, c5, c6 = (8 * par + j for j in range(7))
        if not pre_sum:
            def f_s0(h):
                return h.activation(out=sink_ap, in_=src_ap, func=AF.Identity,
                                    bias=0.0, scale=1.0,
                                    accum_out=stats[:, c0:c0 + 1])
            p.op("scalar", f_s0, reads=src_keys, writes=[f"st{c0}", sink_key])
        def f_s1(h):
            return h.activation(out=sink_ap, in_=src_ap, func=AF.Square,
                                bias=0.0, scale=1.0,
                                accum_out=stats[:, c1:c1 + 1])
        p.op("scalar", f_s1, reads=src_keys, writes=[f"st{c1}", sink_key])
        def f_m(h):
            return h.tensor_scalar(out=stats[:, c2:c2 + 1],
                                   in0=stats[:, c0:c0 + 1], scalar1=1.0 / D,
                                   scalar2=None, op0=OP.mult)
        p.op("vector", f_m, reads=[f"st{c0}"], writes=[f"st{c2}"])
        def f_m2(h):
            return h.tensor_tensor(out=stats[:, c3:c3 + 1],
                                   in0=stats[:, c2:c2 + 1],
                                   in1=stats[:, c2:c2 + 1], op=OP.mult)
        p.op("vector", f_m2, reads=[f"st{c2}"], writes=[f"st{c3}"])
        def f_v(h):
            return h.scalar_tensor_tensor(
                out=stats[:, c4:c4 + 1], in0=stats[:, c1:c1 + 1],
                scalar=1.0 / D, in1=stats[:, c3:c3 + 1],
                op0=OP.mult, op1=OP.subtract)
        p.op("vector", f_v, reads=[f"st{c1}", f"st{c3}"], writes=[f"st{c4}"])
        def f_l(h):
            return h.activation(out=stats[:, c5:c5 + 1], in_=stats[:, c4:c4 + 1],
                                func=AF.Ln, bias=consts[:, 0:1], scale=1.0)
        p.op("scalar", f_l, reads=[f"st{c4}", "c_eps"], writes=[f"st{c5}"])
        def f_e(h):
            return h.activation(out=stats[:, c6:c6 + 1], in_=stats[:, c5:c5 + 1],
                                func=AF.Exp, bias=0.0, scale=-0.5)
        p.op("scalar", f_e, reads=[f"st{c5}"], writes=[f"st{c6}"])
        return c2, c6

    for t in range(NT):
        par = t % 2
        def f_x(h, t=t, par=par):
            return h.dma_start(out=xin[par], in_=x_h[t * 128:(t + 1) * 128, :])
        p.dma("sync", f"xin{par}", f_x, writes=[f"xin{par}"])
        cm, cr = ln_stats(xin[par], [f"xin{par}"], par, wrk[par], f"wrk{par}")
        def f_ln(h, par=par, cm=cm, cr=cr):
            return h.tensor_scalar(out=bstg[par], in0=xin[par],
                                   scalar1=stats[:, cm:cm + 1],
                                   scalar2=stats[:, cr:cr + 1],
                                   op0=OP.subtract, op1=OP.mult)
        p.op("vector", f_ln, reads=[f"xin{par}", f"st{8*par+2}", f"st{8*par+6}"],
             writes=[f"bstg{par}"])
        def f_l1(h, t=t, par=par):
            return h.dma_start(out=s_ln1q[t * 128:(t + 1) * 128, :],
                               in_=bstg[par])
        p.dma("gpsimd", f"ln1st{par}", f_l1, reads=[f"bstg{par}"],
              writes=[f"s_ln1q_t{t}"])
        drip(2)

    ln1_done = [f"s_ln1q_t{t}" for t in range(NT)]
    for c in range(DC):
        def f_lT(h, c=c):
            return h.dma_start(out=A1[:, c * SEQ:(c + 1) * SEQ],
                               in_=s_ln1q[:, c * 128:(c + 1) * 128],
                               transpose=True)
        p.dma("scalar", "xbar", f_lT, reads=ln1_done, writes=[f"A1c{c}"])

    def elu1p_evac(bank, out_ap, out_keys, wb):
        def f_r(h, bank=bank, wb=wb):
            return h.activation(out=wrk2[wb], in_=pb[bank][:, :], func=AF.Relu,
                                bias=0.0, scale=1.0)
        p.op("scalar", f_r, reads=[], writes=[f"er{wb}", PS[bank]])
        def f_m(h, bank=bank, wb=wb):
            return h.tensor_tensor(out=wrk2[(wb + 2) % 4], in0=pb[bank][:, :],
                                   in1=wrk2[wb], op=OP.subtract)
        p.op("vector", f_m, reads=[f"er{wb}"], writes=[f"em{wb}", PS[bank]])
        def f_e(h, wb=wb):
            return h.activation(out=ewrk[wb % 2], in_=wrk2[(wb + 2) % 4],
                                func=AF.Exp, bias=0.0, scale=1.0)
        p.op("scalar", f_e, reads=[f"em{wb}"], writes=[f"ee{wb}"])
        def f_o(h, wb=wb, out_ap=out_ap):
            return h.tensor_tensor(out=out_ap, in0=ewrk[wb % 2],
                                   in1=wrk2[wb], op=OP.add)
        p.op("vector", f_o, reads=[f"ee{wb}", f"er{wb}"], writes=out_keys)

    for co in range(DC):
        for nch in range(NCH):
            bank = (co * NCH + nch) % 2
            for dc in range(DC):
                def f_mm(h, co=co, nch=nch, dc=dc, bank=bank):
                    return h.matmul(
                        pb[bank][:, :],
                        W1[:, dc * 1024 + co * 128: dc * 1024 + co * 128 + 128],
                        A1[:, dc * SEQ + nch * 512: dc * SEQ + (nch + 1) * 512],
                        start=(dc == 0), stop=False)
                p.op("tensor", f_mm, reads=w1k(dc * 1024 + co * 128) + a1k(dc),
                     writes=[PS[bank]], quiet=True)
            def f_bq(h, co=co, bank=bank):
                return h.matmul(pb[bank][:, :],
                                brow4[0:1, co * 128: co * 128 + 128],
                                onesE[0:1, :], start=False, stop=True)
            p.op("tensor", f_bq, reads=["brow", "onesE"], writes=[PS[bank]])
            if co == 0 and nch == 0 and "d_z0" in dbg:
                zdump = vf32(A2, 26816, 512)
                def f_zd(h, bank=bank, zdump=zdump):
                    return h.activation(out=zdump, in_=pb[bank][:, :],
                                        func=AF.Identity, bias=0.0, scale=1.0)
                p.op("scalar", f_zd, reads=[], writes=["zdump", PS[bank]])
                p.dma("gpsimd", "dbgz", (lambda zdump=zdump: lambda h: h.dma_start(
                    out=dbg["d_z0"], in_=zdump))(), reads=["zdump"], writes=["dz0"])
            elu1p_evac(bank, qstg[bank], [f"qstg{bank}"], bank)
            def f_qs(h, co=co, nch=nch, bank=bank):
                return h.dma_start(
                    out=s_qT[co * 128:(co + 1) * 128, nch * 512:(nch + 1) * 512],
                    in_=qstg[bank])
            p.dma("sync", f"qsp{bank}", f_qs, reads=[f"qstg{bank}"],
                  writes=[f"s_qT_{co}_{nch}"])
            drip(2)

    first_kv = [True, True]

    def emit_kvsum(t):
        par = t % 2
        for hh in range(H):
            rlo = 64 * (hh % 2)
            clo = (hh // 2) * 64
            st = (t == 0 and hh < 2)
            def f_kv(h, par=par, hh=hh, rlo=rlo, clo=clo, st=st,
                     last=(t == NT - 1 and hh == H - 1)):
                return h.matmul(pb[6][rlo:rlo + 64, clo:clo + 64],
                                ktile[par][:, hh * 64:(hh + 1) * 64],
                                vtile[par][:, hh * 64:(hh + 1) * 64],
                                start=st, stop=last, skip_group_check=True)
            p.op("tensor", f_kv, reads=[f"ktile{par}", f"vtile{par}"],
                 writes=["ps6"], quiet=(hh != H - 1))
        for pr in range(8):
            st = (t == 0 and pr == 0)
            def f_ks(h, par=par, pr=pr, st=st, last=(t == NT - 1 and pr == 7)):
                return h.matmul(pb[7][:, pr:pr + 1],
                                ktile[par][:, pr * 128:(pr + 1) * 128],
                                onesE[:, 0:1], start=st, stop=last,
                                skip_group_check=True)
            p.op("tensor", f_ks, reads=[f"ktile{par}", "onesE"], writes=["ps7"],
                 quiet=(pr != 7))

    for t in range(NT):
        par = t % 2
        for wsel, lab in ((1, "k"), (2, "v")):
            for oc in range(2):
                bank = 2 + (wsel - 1) * 2 + oc
                for dc in range(DC):
                    def f_mm(h, wsel=wsel, t=t, oc=oc, dc=dc, bank=bank):
                        return h.matmul(
                            pb[bank][:, :],
                            A1[:, dc * SEQ + t * 128: dc * SEQ + t * 128 + 128],
                            W1[:, wsel * 8192 + dc * 1024 + oc * 512:
                                  wsel * 8192 + dc * 1024 + (oc + 1) * 512],
                            start=(dc == 0), stop=False)
                    p.op("tensor", f_mm,
                         reads=a1k(dc) + w1k(wsel * 8192 + dc * 1024, 1024),
                         writes=[PS[bank]], quiet=True)
                def f_bb(h, wsel=wsel, oc=oc, bank=bank):
                    return h.matmul(
                        pb[bank][:, :], onesE[32 * wsel:32 * wsel + 1, 0:128],
                        brow4[32 * wsel:32 * wsel + 1, oc * 512:(oc + 1) * 512],
                        start=False, stop=True)
                p.op("tensor", f_bb, reads=["onesE", "brow"], writes=[PS[bank]])
                if lab == "k":
                    elu1p_evac(bank, ktile[par][:, oc * 512:(oc + 1) * 512],
                               [f"ktile{par}"], oc)
                else:
                    def f_v(h, par=par, oc=oc, bank=bank):
                        return h.activation(
                            out=vtile[par][:, oc * 512:(oc + 1) * 512],
                            in_=pb[bank][:, :], func=AF.Identity, bias=0.0,
                            scale=1.0)
                    p.op("scalar", f_v, reads=[], writes=[f"vtile{par}", PS[bank]])
        if t >= 1:
            emit_kvsum(t - 1)
        drip(3)
    emit_kvsum(NT - 1)
    drip(len(wsteps))

    if stage == "A":
        if "ln1q" in dbg or True:
            pass
        _dumps_A(p, dbg, pb, s_ln1q, s_qT, s_wu, s_wd, debug)
        p.barrier()
        p.replay()
        return nc, dbg

    p.op("vector", lambda h: h.tensor_copy(out=kv_sb[:, :], in_=pb[6][:, :]),
         reads=[], writes=["kv_sb", "ps6"])
    p.op("vector", lambda h: h.memset(ksumbd[:, :], 0.0), writes=["ksumbd"])
    for c in range(DC):
        def f_k0(h, c=c):
            return h.tensor_copy(
                out=ksumbd[0:64, c * 16 + 2 * c: c * 16 + 2 * c + 1],
                in_=pb[7][0:64, c:c + 1])
        p.op("vector", f_k0, reads=["ksumbd"], writes=["ksumbd", "ps7"])
        def f_k1(h, c=c):
            return h.tensor_copy(
                out=ksumbd[64:128, c * 16 + 2 * c + 1: c * 16 + 2 * c + 2],
                in_=pb[7][64:128, c:c + 1])
        p.op("vector", f_k1, reads=["ksumbd"], writes=["ksumbd", "ps7"])

    p.barrier()

    zB = [vf32(A1, 0, 1024), vf32(A1, 2048, 1024)]
    x1B = [vf32(A1, 4096, 1024), vf32(A1, 6144, 1024)]
    xinB = [vf32(A1, 8192, 1024), vf32(A1, 10240, 1024)]
    bccB = [vf32(A1, 12288, 512), vf32(A1, 19456, 512)]
    sinkB = [vf32(A1, 13312, 1024), vf32(A1, 15360, 1024)]
    bstgB = [vbf(A1, 17408, 1024), vbf(A1, 18432, 1024)]

    for c in range(DC):
        q = "sync" if c % 2 == 0 else "scalar"
        def f_ql(h, c=c):
            return h.dma_start(out=A2[:, c * SEQ:(c + 1) * SEQ],
                               in_=s_qT[c * 128:(c + 1) * 128, :])
        p.dma(q, f"qld{c % 2}", f_ql,
              reads=[f"s_qT_{c}_{n}" for n in range(NCH)], writes=[f"A2c{c}"])
    for c in range(DC):
        def f_wo(h, c=c):
            return h.dma_start(out=W1[:, c * 1024:(c + 1) * 1024],
                               in_=s_wattn[3 * D:4 * D, c * 128:(c + 1) * 128],
                               transpose=True)
        p.dma("scalar", "xbar", f_wo, reads=["s_wattn3"],
              writes=w1k(c * 1024, 1024))

    for c in range(3, DC):
        def f_u(h, c=c):
            return h.dma_start(out=W1[:, c * 3072:(c + 1) * 3072],
                               in_=s_wu[:, c * 128:(c + 1) * 128],
                               transpose=True)
        p.dma("scalar", "xbar", f_u,
              reads=[f"s_wu_{it}_0" for it in range(IF // 128)],
              writes=w1k(c * 3072, 3072))

    for nch in range(NCH):
        dbank = 4 + nch % 2
        for c in range(DC):
            def f_dn(h, c=c, nch=nch, dbank=dbank):
                return h.matmul(
                    pb[dbank][0:16, :], ksumbd[:, c * 16:(c + 1) * 16],
                    A2[:, c * SEQ + nch * 512: c * SEQ + (nch + 1) * 512],
                    start=(c == 0), stop=(c == DC - 1))
            p.op("tensor", f_dn, reads=["ksumbd", f"A2c{c}"], writes=[PS[dbank]],
                 quiet=(c != DC - 1))
        for pr in range(8):
            bank = 2 + pr % 2
            for rlo in (0, 64):
                def f_nm(h, pr=pr, rlo=rlo, nch=nch, bank=bank):
                    return h.matmul(
                        pb[bank][rlo:rlo + 64, :],
                        kv_sb[rlo:rlo + 64, pr * 64:(pr + 1) * 64],
                        A2[rlo:rlo + 64,
                           pr * SEQ + nch * 512: pr * SEQ + (nch + 1) * 512],
                        start=True, stop=True, skip_group_check=True)
                p.op("tensor", f_nm, reads=["kv_sb", f"A2c{pr}"],
                     writes=[PS[bank]], quiet=(rlo == 0))
            if pr == 0:
                def f_r1(h, dbank=dbank):
                    return h.tensor_scalar(
                        out=rden[:, :], in0=pb[dbank][0:16, :], scalar1=1e-6,
                        scalar2=None, op0=OP.add)
                p.op("vector", f_r1, reads=[], writes=["rden", PS[dbank]])
                p.op("vector", lambda h: h.reciprocal(out=rden[:, :],
                                                      in_=rden[:, :]),
                     reads=["rden"], writes=["rden"])
                def f_r3(h):
                    return h.tensor_scalar(out=rdenb[:, :], in0=rden[:, :],
                                           scalar1=stdp_sig[:, :], scalar2=None,
                                           op0=OP.mult)
                p.op("vector", f_r3, reads=["rden", "stdp_sig"],
                     writes=["rdenb"])
            bbank = 6 + pr % 2
            def f_bc(h, pr=pr, bbank=bbank):
                return h.matmul(pb[bbank][:, :], e16f[:, pr * 128:(pr + 1) * 128],
                                rdenb[:, :], start=True, stop=True)
            p.op("tensor", f_bc, reads=["e16", "rdenb"], writes=[PS[bbank]])
            def f_bcc(h, pr=pr, bbank=bbank):
                return h.activation(out=bccB[pr % 2], in_=pb[bbank][:, :],
                                    func=AF.Identity, bias=0.0, scale=1.0)
            p.op("scalar", f_bcc, reads=[], writes=[f"bcc{pr % 2}", PS[bbank]])
            def f_at(h, pr=pr, nch=nch, bank=bank):
                return h.tensor_tensor(
                    out=A2[:, pr * SEQ + nch * 512: pr * SEQ + (nch + 1) * 512],
                    in0=pb[bank][:, :], in1=bccB[pr % 2], op=OP.mult)
            p.op("vector", f_at, reads=[f"bcc{pr % 2}"],
                 writes=[f"A2c{pr}", PS[bank]])

    p.op("vector", lambda h: h.memset(mroll[:, 0:1], 0.0), writes=["m0roll"])

    def b_stage1(t):
        par = t % 2
        for oc in range(2):
            bank = (t * 2 + oc) % 4
            for fc in range(DC):
                def f_wo(h, t=t, oc=oc, fc=fc, bank=bank):
                    return h.matmul(
                        pb[bank][:, :],
                        A2[:, fc * SEQ + t * 128: fc * SEQ + t * 128 + 128],
                        W1[:, fc * 1024 + oc * 512: fc * 1024 + (oc + 1) * 512],
                        start=(fc == 0), stop=False)
                p.op("tensor", f_wo, reads=[f"A2c{fc}"] + w1k(fc * 1024, 1024),
                     writes=[PS[bank]], quiet=True)
            def f_bo(h, oc=oc, bank=bank):
                return h.matmul(pb[bank][:, :], onesE[96:97, 0:128],
                                brow4[96:97, oc * 512:(oc + 1) * 512],
                                start=False, stop=True, tile_position=(96, 0))
            p.op("tensor", f_bo, reads=["onesE", "brow"], writes=[PS[bank]])
            def f_z(h, par=par, oc=oc, bank=bank):
                return h.activation(out=zB[par][:, oc * 512:(oc + 1) * 512],
                                    in_=pb[bank][:, :], func=AF.Identity,
                                    bias=0.0, scale=1.0,
                                    accum_out=stats[:, 16 + 2 * par + oc:
                                                    17 + 2 * par + oc])
            p.op("scalar", f_z, reads=[], writes=[f"z{par}", PS[bank],
                                                  f"stZ{par}{oc}"])
        def f_x(h, t=t, par=par):
            return h.dma_start(out=xinB[par], in_=x_h[t * 128:(t + 1) * 128, :])
        p.dma("sync", f"xin{par}", f_x, writes=[f"xinB{par}"])

    def b_stage2(t):
        par = t % 2
        def f_sz(h, par=par):
            return h.tensor_tensor(out=stats[:, 8 * par:8 * par + 1],
                                   in0=stats[:, 16 + 2 * par:17 + 2 * par],
                                   in1=stats[:, 17 + 2 * par:18 + 2 * par],
                                   op=OP.add)
        p.op("vector", f_sz, reads=[f"stZ{par}0", f"stZ{par}1"],
             writes=[f"st{8 * par}"])
        cm, cr = ln_stats(zB[par], [f"z{par}"], par, sinkB[par], f"sink{par}",
                          pre_sum=True)
        def f_lna(h, par=par, cm=cm, cr=cr):
            return h.tensor_scalar(out=x1B[par], in0=zB[par],
                                   scalar1=stats[:, cm:cm + 1],
                                   scalar2=stats[:, cr:cr + 1],
                                   op0=OP.subtract, op1=OP.mult)
        p.op("vector", f_lna, reads=[f"z{par}", f"st{8*par+2}", f"st{8*par+6}"],
             writes=[f"x1w{par}"])
        def f_x1(h, par=par):
            return h.scalar_tensor_tensor(
                out=x1B[par], in0=x1B[par], scalar=1.0, in1=xinB[par],
                op0=OP.mult, op1=OP.add,
                accum_out=stats[:, 8 * par:8 * par + 1])
        p.op("vector", f_x1, reads=[f"x1w{par}", f"xinB{par}"],
             writes=[f"x1w{par}", f"st{8 * par}"])
        def f_xs(h, t=t, par=par):
            return h.dma_start(out=s_x1[t * 128:(t + 1) * 128, :], in_=x1B[par])
        p.dma("gpsimd", f"x1st{par}", f_xs, reads=[f"x1w{par}"],
              writes=[f"s_x1_t{t}"])
        cm2, cr2 = ln_stats(x1B[par], [f"x1w{par}"], par, sinkB[par],
                            f"sink{par}", pre_sum=True)
        def f_l2(h, par=par, cm2=cm2, cr2=cr2):
            return h.tensor_scalar(out=xinB[par], in0=x1B[par],
                                   scalar1=stats[:, cm2:cm2 + 1],
                                   scalar2=stats[:, cr2:cr2 + 1],
                                   op0=OP.subtract, op1=OP.mult)
        p.op("vector", f_l2, reads=[f"x1w{par}", f"st{8*par+2}", f"st{8*par+6}"],
             writes=[f"ln2f{par}", f"xinB{par}"])
        def f_m0(h, t=t, par=par):
            return h.tensor_reduce(out=mxcol[:, 384 + t:385 + t],
                                   in_=xinB[par], axis=AX.X,
                                   op=OP.max, apply_absolute_value=True)
        p.op("vector", f_m0, reads=[f"ln2f{par}", f"xinB{par}"],
             writes=[f"m0c{t}"])
        def f_l2s(h, t=t, par=par):
            return h.dma_start(out=s_ln2[t * 128:(t + 1) * 128, :],
                               in_=xinB[par])
        p.dma("gpsimd", f"ln2st{par}", f_l2s, reads=[f"ln2f{par}", f"xinB{par}"],
              writes=[f"s_ln2_t{t}"])

    for t in range(NT + 1):
        if t < NT:
            b_stage1(t)
        if t >= 1:
            b_stage2(t - 1)
    def f_m0f(h):
        return h.tensor_reduce(out=mroll[:, 0:1], in_=mxcol[:, 384:416],
                               axis=AX.X, op=OP.max)
    p.op("vector", f_m0f, reads=[f"m0c{i}" for i in range(NT)],
         writes=["m0roll"])

    _dumps_B(p, dbg, s_x1, s_ln2, s_qT, debug, A2, mroll, sc11)
    if stage == "B":
        p.barrier()
        p.replay()
        return nc, dbg

    def cross_core_max(cols, rkeys, tag):
        n = len(cols)
        for j, col in enumerate(cols):
            def f_d1(h, j=j, col=col):
                return h.dma_start(out=s_part[j], in_=mroll[:, col:col + 1])
            p.dma("gpsimd", f"ccp{j}", f_d1, reads=rkeys, writes=[f"s_part{j}"])
            def f_d2(h, j=j):
                return h.dma_start(out=prow[0:1, j * 128:(j + 1) * 128],
                                   in_=s_part[j].rearrange("(p a) b -> a (p b)", a=1))
            p.dma("gpsimd", f"ccp{j}", f_d2, reads=[f"s_part{j}"],
                  writes=[f"prow{j}"])
            def f_g(h, j=j):
                return h.tensor_reduce(out=sc11[0:1, j:j + 1],
                                       in_=prow[0:1, j * 128:(j + 1) * 128],
                                       axis=AX.X, op=OP.max)
            p.op("vector", f_g, reads=[f"prow{j}"], writes=[f"sc11_{j}"])
        def f_s(h, n=n):
            return h.dma_start(out=cc_in[0:1, 0:n], in_=sc11[0:1, 0:n])
        p.dma("gpsimd", "ccio", f_s, reads=[f"sc11_{j}" for j in range(n)],
              writes=["cc_in"])
        def f_c(h, n=n):
            return h.collective_compute("AllReduce", OP.max,
                                        replica_groups=[list(range(N_CORES))],
                                        ins=[cc_in[0:1, 0:n]],
                                        outs=[cc_out[0:1, 0:n]])
        p.cc(f"cc_{tag}", f_c, reads=["cc_in"], writes=["cc_out"])
        def f_l(h, n=n):
            return h.dma_start(out=sc11[0:1, 8:8 + n], in_=cc_out[0:1, 0:n])
        p.dma("gpsimd", "ccio", f_l, reads=["cc_out"], writes=["sc11g"])

    def bcast_scales(srcs, dsts, reads):
        for c, dpos in zip(srcs, dsts):
            def f_b(h, c=c):
                return h.matmul(pb[5][:, 0:1], onesf[0:1, :],
                                sc11[0:1, c:c + 1], start=True, stop=True)
            p.op("tensor", f_b, reads=list(reads) + ["onesf"], writes=["ps5"])
            def f_cp(h, dpos=dpos):
                return h.tensor_copy(out=scales[:, dpos:dpos + 1],
                                     in_=pb[5][:, 0:1])
            p.op("vector", f_cp, reads=[], writes=[f"scl{dpos}", "ps5"])

    if stage == "P2a0":
        p.barrier(); p.replay(); return nc, dbg
    cross_core_max([0], ["m0roll"], "ar0")
    if stage == "P2a":
        p.barrier(); p.replay(); return nc, dbg

    p.barrier(exclude=("cc_", "ccio", "ccp"))
    p.op("vector", lambda h: h.tensor_scalar(
        out=sc11[0:1, 12:13], in0=sc11[0:1, 8:9], scalar1=1.0 / QMAX,
        scalar2=1e-12, op0=OP.mult, op1=OP.max), reads=["sc11g"],
        writes=["v_s0"])
    p.op("vector", lambda h: h.reciprocal(out=sc11[0:1, 13:14],
                                          in_=sc11[0:1, 12:13]),
         reads=["v_s0"], writes=["v_is0"])
    bcast_scales([12, 13], [0, 1], ["v_s0", "v_is0"])
    wu_done = [f"s_wu_{it}_0" for it in range(IF // 128)]
    wg_done = [f"s_wg_{it}_0" for it in range(IF // 128)]
    for c in range(3):
        def f_u(h, c=c):
            return h.dma_start(out=W1[:, c * 3072:(c + 1) * 3072],
                               in_=s_wu[:, c * 128:(c + 1) * 128],
                               transpose=True)
        p.dma("scalar", "xbar", f_u, reads=wu_done,
              writes=w1k(c * 3072, 3072))

    ln2_done = [f"s_ln2_t{t}" for t in range(NT)]
    STG = 24576
    for c in range(DC):
        par = c % 2
        def f_t(h, c=c, par=par):
            return h.dma_start(
                out=A2[:, STG + par * 4096: STG + (par + 1) * 4096],
                in_=s_ln2[:, c * 128:(c + 1) * 128], transpose=True)
        p.dma("scalar", "xbar", f_t, reads=ln2_done,
              writes=[f"A2c{6 + par}"])
        for piece in range(2):
            def f_r(h, par=par, piece=piece):
                return h.tensor_scalar(
                    out=i16c[par][:, :],
                    in0=A2[:, STG + par * 4096 + piece * 2048:
                              STG + par * 4096 + (piece + 1) * 2048],
                    scalar1=scales[:, 1:2], scalar2=None, op0=OP.mult)
            p.op("vector", f_r, reads=[f"A2c{6 + par}", "scl1"],
                 writes=[f"i16c{par}"])
            def f_q(h, c=c, par=par, piece=piece):
                return h.tensor_scalar(
                    out=A1[:, c * SEQ + piece * 2048:
                              c * SEQ + (piece + 1) * 2048],
                    in0=i16c[par][:, :], scalar1=scales[:, 0:1], scalar2=None,
                    op0=OP.mult)
            p.op("vector", f_q, reads=[f"i16c{par}", "scl0"], writes=[f"A1c{c}"])

    for c in range(DC):
        def f_g(h, c=c):
            return h.dma_start(out=A2[:, c * 3072:(c + 1) * 3072],
                               in_=s_wg[:, c * 128:(c + 1) * 128],
                               transpose=True)
        p.dma("scalar", "xbar", f_g, reads=wg_done, writes=a2k(c * 3072, 3072))

    if stage == "P2b":
        p.barrier(); p.replay(); return nc, dbg
    IC_LIM = 2 if stage in ("P2d", "P2e", "P2f", "P2g") else IC
    for wsel in range(2):
        for ich in range(IC_LIM):
            for nch in range(NCH):
                bank = (ich * NCH + nch + wsel * 2) % 4
                for dc in range(DC):
                    if wsel == 0:
                        lhs = W1[:, dc * 3072 + ich * 128:
                                    dc * 3072 + ich * 128 + 128]
                        lk = w1k(dc * 3072 + ich * 128)
                    else:
                        lhs = A2[:, dc * 3072 + ich * 128:
                                    dc * 3072 + ich * 128 + 128]
                        lk = a2k(dc * 3072 + ich * 128, 128)
                    def f_mm(h, lhs=lhs, dc=dc, nch=nch, bank=bank):
                        return h.matmul(
                            pb[bank][:, :], lhs,
                            A1[:, dc * SEQ + nch * 512:
                                  dc * SEQ + (nch + 1) * 512],
                            start=(dc == 0), stop=(dc == DC - 1))
                    p.op("tensor", f_mm, reads=lk + a1k(dc), writes=[PS[bank]],
                         quiet=(dc != DC - 1))
                gidx = wsel * 192 + ich * NCH + nch
                if stage not in ("P2e", "P2g"):
                    def f_mx(h, bank=bank, gidx=gidx):
                        return h.tensor_reduce(out=mxcol[:, gidx:gidx + 1],
                                               in_=pb[bank][:, :], axis=AX.X,
                                               op=OP.max, apply_absolute_value=True)
                    p.op("vector", f_mx, reads=[], writes=[f"mxc{gidx}", PS[bank]])
                sidx = (ich * NCH + nch + wsel) % 4
                if stage not in ("P2e", "P2f"):
                    def f_c(h, bank=bank, sidx=sidx):
                        return h.activation(out=sstg4[sidx][:, :],
                                            in_=pb[bank][:, :], func=AF.Identity,
                                            bias=0.0, scale=1.0)
                    p.op("scalar", f_c, reads=[], writes=[f"sstg{sidx}", PS[bank]])
                dst = s_up if wsel == 0 else s_gate
                dk = "up" if wsel == 0 else "gate"
                if stage not in ("P2c", "P2d", "P2e", "P2f", "P2g"):
                    def f_sp(h, dst=dst, ich=ich, nch=nch, sidx=sidx):
                        return h.dma_start(out=dst[ich, nch], in_=sstg4[sidx][:, :])
                    p.dma("sync", f"sp{sidx}", f_sp, reads=[f"sstg{sidx}"],
                          writes=[f"{dk}_{ich}_{nch}"])

    def f_m1f(h):
        return h.tensor_reduce(out=mroll[:, 1:2], in_=mxcol[:, 0:192], axis=AX.X,
                               op=OP.max)
    p.op("vector", f_m1f, reads=[f"mxc{i}" for i in range(192)],
         writes=["m12roll"])
    def f_m2f(h):
        return h.tensor_reduce(out=mroll[:, 2:3], in_=mxcol[:, 192:384], axis=AX.X,
                               op=OP.max)
    p.op("vector", f_m2f, reads=[f"mxc{i}" for i in range(192, 384)],
         writes=["m12roll"])
    if stage in ("P2c", "P2d", "P2e", "P2f", "P2g"):
        p.barrier(); p.replay(); return nc, dbg
    _dumps_P2(p, dbg, s_up, s_gate, debug)
    if "xqT" in debug:
        p.dma("gpsimd", "dbgx", lambda h: h.dma_start(
            out=dbg["d_xqT"], in_=A1[:, 0:SEQ]), reads=["A1c0"],
            writes=["dbgx1"])
        p.dma("gpsimd", "dbgw", lambda h: h.dma_start(
            out=dbg["d_wupT"], in_=W1[:, 0:3072]),
            reads=[f"W1_{i}" for i in range(3)], writes=["dbgw1"])
    if "scal" in debug and stage == "P2":
        p.dma("gpsimd", "dbgs", lambda h: h.dma_start(
            out=dbg["d_scal"], in_=sc11[0:1, 0:24]), reads=["v_is0"],
            writes=["dbgs1"])
        p.dma("gpsimd", "dbgs2", lambda h: h.dma_start(
            out=dbg["d_scal2"][0:1, 0:2], in_=scales[0:1, 0:2]), reads=["scl0", "scl1"],
            writes=["dbgs2"])
    if stage == "P2":
        p.barrier()
        p.replay()
        return nc, dbg

    cross_core_max([1, 2], ["m12roll"], "ar1")
    p.barrier(exclude=("cc_", "ccio", "ccp"))
    p.op("vector", lambda h: h.tensor_scalar(
        out=sc11[0:1, 14:15], in0=sc11[0:1, 8:9], scalar1=1.0 / QMAX,
        scalar2=1e-12, op0=OP.mult, op1=OP.max), reads=["sc11g"],
        writes=["v_s1"])
    p.op("vector", lambda h: h.reciprocal(out=sc11[0:1, 15:16],
                                          in_=sc11[0:1, 14:15]),
         reads=["v_s1"], writes=["v_is1"])
    p.op("vector", lambda h: h.tensor_scalar(
        out=sc11[0:1, 16:17], in0=sc11[0:1, 9:10], scalar1=1.0 / QMAX,
        scalar2=1e-12, op0=OP.mult, op1=OP.max), reads=["sc11g"],
        writes=["v_s2"])
    p.op("vector", lambda h: h.reciprocal(out=sc11[0:1, 17:18],
                                          in_=sc11[0:1, 16:17]),
         reads=["v_s2"], writes=["v_is2"])
    bcast_scales([14, 15, 16, 17], [2, 3, 4, 5],
                 ["v_s1", "v_is1", "v_s2", "v_is2"])

    g_in = [vbf(A1, 0, 2048), vbf(A1, 2048, 2048),
            vbf(A2, 0, 2048), vbf(A2, 2048, 2048)]
    g_out = [vbf(A1, 4096, 2048), vbf(A1, 6144, 2048),
             vbf(A2, 4096, 2048), vbf(A2, 6144, 2048)]
    sigfv = [vf32(A1, 8192, 2048), vf32(A1, 12288, 2048),
             vf32(A2, 8192, 2048), vf32(A2, 12288, 2048)]
    u_in = [vbf(A1, 16384, 2048), vbf(A1, 18432, 2048),
            vbf(A2, 16384, 2048), vbf(A2, 18432, 2048)]
    i16A = [vi16(A1, 20480, 2048), vi16(A1, 22528, 2048),
            vi16(A2, 20480, 2048), vi16(A2, 22528, 2048)]
    i16B = [vi16(A1, 24576, 2048), vi16(A1, 26624, 2048),
            vi16(A2, 24576, 2048), vi16(A2, 26624, 2048)]
    u16A = [vu16(A1, 28672, 2048), vu16(A1, 30720, 2048),
            vu16(A2, 28672, 2048), vu16(A2, 30720, 2048)]

    wd_done = [f"s_wd_{it}_0" for it in range(DC)]
    for c in range(IC):
        def f_d(h, c=c):
            return h.dma_start(out=W1[:, c * 1024:(c + 1) * 1024],
                               in_=s_wd[:, c * 128:(c + 1) * 128],
                               transpose=True)
        p.dma("scalar", "xbar", f_d, reads=wd_done, writes=w1k(c * 1024, 1024))

    p.op("vector", lambda h: h.memset(mroll[:, 3:4], 0.0), writes=["m3roll"])
    MT = (IC * NCH) // 4
    for mt in range(MT):
        par = mt % 4
        ich, nch4 = divmod(mt, 2)
        def f_ld(h, ich=ich, nch4=nch4, par=par):
            return h.dma_start(out=g_in[par],
                               in_=s_gate[ich, nch4 * 4:(nch4 + 1) * 4].transpose([1, 0, 2]))
        p.dma("sync", f"p3i{par}", f_ld,
              reads=[f"gate_{ich}_{nch4 * 4 + j}" for j in range(4)],
              writes=[f"g_in{par}"])
        def f_r2(h, par=par):
            return h.activation(out=i16A[par], in_=g_in[par], func=AF.Identity,
                                bias=0.0, scale=scales[:, 5:6])
        p.op("scalar", f_r2, reads=[f"g_in{par}", "scl5"], writes=[f"i16A{par}"])
        def f_sg(h, par=par):
            return h.activation(out=sigfv[par], in_=i16A[par], func=AF.Sigmoid,
                                bias=0.0, scale=scales[:, 4:5])
        p.op("scalar", f_sg, reads=[f"i16A{par}", "scl4"], writes=[f"sigf{par}"])
        def f_sq(h, par=par):
            return h.tensor_scalar(out=u16A[par], in0=sigfv[par],
                                   scalar1=65536.0, scalar2=None, op0=OP.mult)
        p.op("vector", f_sq, reads=[f"sigf{par}"], writes=[f"u16A{par}"])
        def f_g3(h, par=par):
            return h.tensor_tensor(out=g_out[par], in0=i16A[par],
                                   in1=u16A[par], op=OP.mult)
        p.op("vector", f_g3, reads=[f"i16A{par}", f"u16A{par}"],
             writes=[f"g_out{par}"])
        def f_mx(h, par=par):
            return h.tensor_reduce(out=small[:, 5:6], in_=g_out[par], axis=AX.X,
                                   op=OP.max, apply_absolute_value=True)
        p.op("vector", f_mx, reads=[f"g_out{par}"], writes=["m3t"])
        p.op("vector", lambda h: h.tensor_tensor(
            out=mroll[:, 3:4], in0=mroll[:, 3:4], in1=small[:, 5:6], op=OP.max),
            reads=["m3t", "m3roll"], writes=["m3roll"])
        def f_st(h, ich=ich, nch4=nch4, par=par):
            return h.dma_start(out=s_g3[ich, nch4 * 4:(nch4 + 1) * 4].transpose([1, 0, 2]),
                               in_=g_out[par])
        p.dma("gpsimd", f"p3o{par}", f_st, reads=[f"g_out{par}"],
              writes=[f"g3_{ich}_{nch4}"])

    _dumps_P3(p, dbg, s_g3, s_p, debug)
    if stage == "P3":
        p.barrier()
        p.replay()
        return nc, dbg

    cross_core_max([3], ["m3roll"], "ar2")
    p.op("vector", lambda h: h.scalar_tensor_tensor(
        out=sc11[0:1, 18:19], in0=sc11[0:1, 8:9], scalar=1.0 / 65536.0,
        in1=sc11[0:1, 16:17], op0=OP.mult, op1=OP.mult),
        reads=["sc11g", "v_s2"], writes=["v_m3t"])
    p.op("vector", lambda h: h.tensor_scalar(
        out=sc11[0:1, 19:20], in0=sc11[0:1, 18:19], scalar1=1.0 / QMAX,
        scalar2=1e-12, op0=OP.mult, op1=OP.max), reads=["v_m3t"],
        writes=["v_s3"])
    p.op("vector", lambda h: h.reciprocal(out=sc11[0:1, 20:21],
                                          in_=sc11[0:1, 19:20]),
         reads=["v_s3"], writes=["v_is3"])
    p.op("vector", lambda h: h.scalar_tensor_tensor(
        out=sc11[0:1, 21:22], in0=sc11[0:1, 20:21], scalar=1.0 / 65536.0,
        in1=sc11[0:1, 16:17], op0=OP.mult, op1=OP.mult),
        reads=["v_is3", "v_s2"], writes=["v_c43"])
    bcast_scales([21], [6], ["v_c43"])

    p.op("vector", lambda h: h.memset(mroll[:, 4:5], 0.0), writes=["m4roll"])
    for mt in range(MT):
        par = mt % 4
        ich, nch4 = divmod(mt, 2)
        def f_lg(h, ich=ich, nch4=nch4, par=par):
            return h.dma_start(out=g_in[par],
                               in_=s_g3[ich, nch4 * 4:(nch4 + 1) * 4].transpose([1, 0, 2]))
        p.dma("sync", f"p4g{par}", f_lg, reads=[f"g3_{ich}_{nch4}"],
              writes=[f"g_in{par}"])
        def f_lu(h, ich=ich, nch4=nch4, par=par):
            return h.dma_start(out=u_in[par],
                               in_=s_up[ich, nch4 * 4:(nch4 + 1) * 4].transpose([1, 0, 2]))
        p.dma("scalar", f"p4u{par}", f_lu,
              reads=[f"up_{ich}_{nch4 * 4 + j}" for j in range(4)],
              writes=[f"u_in{par}"])
        def f_a(h, par=par):
            return h.activation(out=i16A[par], in_=g_in[par], func=AF.Identity,
                                bias=0.0, scale=scales[:, 6:7])
        p.op("scalar", f_a, reads=[f"g_in{par}", "scl6"], writes=[f"i16A{par}"])
        def f_b(h, par=par):
            return h.activation(out=i16B[par], in_=u_in[par], func=AF.Identity,
                                bias=0.0, scale=scales[:, 3:4])
        p.op("scalar", f_b, reads=[f"u_in{par}", "scl3"], writes=[f"i16B{par}"])
        def f_p(h, par=par):
            return h.tensor_tensor(out=g_out[par], in0=i16A[par],
                                   in1=i16B[par], op=OP.mult)
        p.op("vector", f_p, reads=[f"i16A{par}", f"i16B{par}"],
             writes=[f"g_out{par}"])
        def f_mx(h, par=par):
            return h.tensor_reduce(out=small[:, 6:7], in_=g_out[par], axis=AX.X,
                                   op=OP.max, apply_absolute_value=True)
        p.op("vector", f_mx, reads=[f"g_out{par}"], writes=["m4t"])
        p.op("vector", lambda h: h.tensor_tensor(
            out=mroll[:, 4:5], in0=mroll[:, 4:5], in1=small[:, 6:7], op=OP.max),
            reads=["m4t", "m4roll"], writes=["m4roll"])
        def f_st(h, ich=ich, nch4=nch4, par=par):
            return h.dma_start(out=s_p[ich, nch4 * 4:(nch4 + 1) * 4].transpose([1, 0, 2]),
                               in_=g_out[par])
        p.dma("gpsimd", f"p4o{par}", f_st, reads=[f"g_out{par}"],
              writes=[f"p_{ich}_{nch4}"])

    if "g3p" in debug:
        p.dma("gpsimd", "dbgp", lambda h: h.dma_start(out=dbg["d_p"], in_=s_p),
              reads=[f"p_{i}_{n}" for i in range(IC) for n in range(2)],
              writes=["dbg10"])
    if stage == "P4":
        p.barrier()
        p.replay()
        return nc, dbg

    cross_core_max([4], ["m4roll"], "ar3")
    p.op("vector", lambda h: h.tensor_scalar(
        out=sc11[0:1, 22:23], in0=sc11[0:1, 19:20], scalar1=sc11[0:1, 14:15],
        scalar2=None, op0=OP.mult), reads=["v_s3", "v_s1"], writes=["v_sg4"])
    p.op("vector", lambda h: h.tensor_scalar(
        out=sc11[0:1, 10:11], in0=sc11[0:1, 8:9], scalar1=sc11[0:1, 22:23],
        scalar2=None, op0=OP.mult), reads=["sc11g", "v_sg4"], writes=["v_m4t"])
    p.op("vector", lambda h: h.tensor_scalar(
        out=sc11[0:1, 11:12], in0=sc11[0:1, 10:11], scalar1=1.0 / QMAX,
        scalar2=1e-12, op0=OP.mult, op1=OP.max), reads=["v_m4t"],
        writes=["v_s4"])
    p.op("vector", lambda h: h.reciprocal(out=sc11[0:1, 12:13],
                                          in_=sc11[0:1, 11:12]),
         reads=["v_s4"], writes=["v_is4"])
    p.op("vector", lambda h: h.tensor_scalar(
        out=sc11[0:1, 13:14], in0=sc11[0:1, 22:23], scalar1=sc11[0:1, 12:13],
        scalar2=None, op0=OP.mult), reads=["v_sg4", "v_is4"], writes=["v_c54"])
    bcast_scales([13, 11], [7, 8], ["v_c54", "v_s4"])

    p.barrier(exclude=("cc_", "ccio", "ccp"))
    o2t = [vbf(A1, 0, 1024), vbf(A1, 1024, 1024)]
    x1l = [vf32(A1, 2048, 1024), vf32(A1, 4096, 1024)]
    resl = [vf32(A1, 6144, 1024), vf32(A1, 8192, 1024)]
    sink5 = vf32(A1, 10240, 1024)
    p5stg = [vbf(A1, 12288, 2048), vbf(A1, 14336, 2048)]
    i16P = [vi16(A1, 16384, 2048), vi16(A1, 18432, 2048)]

    def p5_a(nch):
        par5 = nch % 2
        obase = par5 * 12288
        for grp in range(IC // 4):
            slot = grp % 2
            def f_lp(h, nch=nch, grp=grp, slot=slot):
                return h.dma_start(
                    out=p5stg[slot],
                    in_=s_p[grp * 4:(grp + 1) * 4, nch].transpose([1, 0, 2]))
            p.dma("sync", f"p5i{slot}", f_lp,
                  reads=[f"p_{grp * 4 + j}_{nch // 4}" for j in range(4)],
                  writes=[f"p5stg{slot}"])
            def f_oq(h, slot=slot):
                return h.activation(out=i16P[slot], in_=p5stg[slot],
                                    func=AF.Identity, bias=0.0,
                                    scale=scales[:, 7:8])
            p.op("scalar", f_oq, reads=[f"p5stg{slot}", "scl7"],
                 writes=[f"i16P{slot}"])
            def f_ot(h, grp=grp, obase=obase, slot=slot):
                return h.tensor_copy(
                    out=A2[:, obase + grp * 2048: obase + (grp + 1) * 2048],
                    in_=i16P[slot])
            p.op("vector", f_ot, reads=[f"i16P{slot}"],
                 writes=[f"ot{par5}_{grp}"])

    def p5_b(nch):
        par5 = nch % 2
        obase = par5 * 12288
        otk = [f"ot{par5}_{g}" for g in range(IC // 4)]
        for hc in range(DC):
            bank = hc % 4
            for ic in range(IC):
                def f_dn(h, hc=hc, ic=ic, obase=obase, bank=bank):
                    return h.matmul(
                        pb[bank][:, :],
                        W1[:, ic * 1024 + hc * 128: ic * 1024 + hc * 128 + 128],
                        A2[:, obase + ic * 512: obase + (ic + 1) * 512],
                        start=(ic == 0), stop=(ic == IC - 1))
                p.op("tensor", f_dn, reads=w1k(ic * 1024 + hc * 128) + otk,
                     writes=[PS[bank]], quiet=(ic != IC - 1))
            def f_ev(h, bank=bank, hc=hc):
                return h.activation(out=sstg4[hc % 4][:, :],
                                    in_=pb[bank][:, :], func=AF.Identity,
                                    bias=0.0, scale=scales[:, 8:9])
            p.op("scalar", f_ev, reads=["scl8"],
                 writes=[f"sstg{hc % 4}", PS[bank]])
            def f_so(h, hc=hc, nch=nch):
                return h.dma_start(
                    out=s_o2T[hc * 128:(hc + 1) * 128,
                              nch * 512:(nch + 1) * 512],
                    in_=sstg4[hc % 4][:, :])
            p.dma("gpsimd", f"o2T{hc % 4}", f_so, reads=[f"sstg{hc % 4}"],
                  writes=[f"o2T_{hc}_{nch}"])

    def p5_c(nch):
        for tt in range(4):
            t = nch * 4 + tt
            par = t % 2
            def f_tl(h, t=t, par=par):
                return h.dma_start(out=o2t[par],
                                   in_=s_o2T[:, t * 128:(t + 1) * 128],
                                   transpose=True)
            p.dma("scalar", "xbar", f_tl,
                  reads=[f"o2T_{hc}_{nch}" for hc in range(DC)],
                  writes=[f"o2t{par}"])
            def f_xl(h, t=t, par=par):
                return h.dma_start(out=x1l[par],
                                   in_=s_x1[t * 128:(t + 1) * 128, :])
            p.dma("sync", f"x1l{par}", f_xl, reads=[f"s_x1_t{t}"],
                  writes=[f"x1l{par}"])
            cm, cr = ln_stats(o2t[par], [f"o2t{par}"], par, sink5, "sink5")
            def f_ln(h, par=par, cm=cm, cr=cr):
                return h.tensor_scalar(out=resl[par], in0=o2t[par],
                                       scalar1=stats[:, cm:cm + 1],
                                       scalar2=stats[:, cr:cr + 1],
                                       op0=OP.subtract, op1=OP.mult)
            p.op("vector", f_ln, reads=[f"o2t{par}", f"st{8*par+2}",
                                        f"st{8*par+6}"], writes=[f"resl{par}"])
            def f_res(h, par=par):
                return h.tensor_tensor(out=resl[par], in0=resl[par],
                                       in1=x1l[par], op=OP.add)
            p.op("vector", f_res, reads=[f"resl{par}", f"x1l{par}"],
                 writes=[f"resl{par}"])
            def f_out(h, t=t, par=par):
                return h.dma_start(out=out_h[t * 128:(t + 1) * 128, :],
                                   in_=resl[par])
            p.dma("gpsimd", f"outst{par}", f_out, reads=[f"resl{par}"],
                  writes=[f"out_t{t}"])

    for k in range(NCH + 2):
        if k < NCH:
            p5_a(k)
        if 1 <= k <= NCH:
            p5_b(k - 1)
        if k >= 2:
            p5_c(k - 2)

    if "d_o2T" in dbg:
        p.dma("gpsimd", "dbgo2", lambda h: h.dma_start(
            out=dbg["d_o2T"], in_=s_o2T),
            reads=[f"o2T_{hc}_{nch}" for hc in range(DC) for nch in range(NCH)],
            writes=["dbgo2w"])
    if "d_scal" in dbg:
        p.dma("gpsimd", "dbgend", lambda h: h.dma_start(
            out=dbg["d_scal"], in_=sc11[0:1, :]),
            reads=["v_c54", "v_s4", "v_is4", "v_s1", "v_s2", "v_s3"],
            writes=["dbgs1"])
    p.barrier()
    p.replay()
    return nc, dbg


def _dumps_A(p, dbg, pb, s_ln1q, s_qT, s_wu, s_wd, debug):
    if "ln1q" in debug:
        p.dma("gpsimd", "dbg", lambda h: h.dma_start(
            out=dbg["d_ln1q"], in_=s_ln1q),
            reads=[f"s_ln1q_t{t}" for t in range(NT)], writes=["dbg1"])
    if "qT" in debug:
        p.dma("gpsimd", "dbg", lambda h: h.dma_start(out=dbg["d_qT"], in_=s_qT),
              reads=[f"s_qT_{c}_{n}" for c in range(DC) for n in range(NCH)],
              writes=["dbg2"])
    if "kv" in debug:
        def f1(h):
            return h.tensor_copy(out=dbg.setdefault("_kvsb", None) or None, in_=None)
    if "wdq" in debug:
        p.dma("gpsimd", "dbg", lambda h: h.dma_start(out=dbg["d_wu"], in_=s_wu),
              reads=[f"s_wu_{it}_0" for it in range(IF // 128)], writes=["dbg3"])
        p.dma("gpsimd", "dbg2", lambda h: h.dma_start(out=dbg["d_wd"], in_=s_wd),
              reads=[f"s_wd_{it}_{pc}" for it in range(DC) for pc in range(3)],
              writes=["dbg4"])


def _dumps_B(p, dbg, s_x1, s_ln2, s_qT, debug, A2, mroll, sc11):
    if "x1" in debug:
        p.dma("gpsimd", "dbg", lambda h: h.dma_start(out=dbg["d_x1"], in_=s_x1),
              reads=[f"s_x1_t{t}" for t in range(NT)], writes=["dbg5"])
    if "ln2" in debug:
        p.dma("gpsimd", "dbg2", lambda h: h.dma_start(out=dbg["d_ln2"], in_=s_ln2),
              reads=[f"s_ln2_t{t}" for t in range(NT)], writes=["dbg6"])


def _dumps_P2(p, dbg, s_up, s_gate, debug):
    if "upg" in debug:
        p.dma("gpsimd", "dbg", lambda h: h.dma_start(out=dbg["d_up"], in_=s_up),
              reads=[f"up_{i}_{n}" for i in range(IC) for n in range(NCH)],
              writes=["dbg7"])
        p.dma("gpsimd", "dbg2", lambda h: h.dma_start(
            out=dbg["d_gate"], in_=s_gate),
            reads=[f"gate_{i}_{n}" for i in range(IC) for n in range(NCH)],
            writes=["dbg8"])


def _dumps_P3(p, dbg, s_g3, s_p, debug):
    if "g3p" in debug:
        p.dma("gpsimd", "dbg", lambda h: h.dma_start(out=dbg["d_g3"], in_=s_g3),
              reads=[f"g3_{i}_{n}" for i in range(IC) for n in range(2)],
              writes=["dbg9"])


_NC_CACHE = {}


def kernel(**inputs):
    x = np.ascontiguousarray(np.asarray(inputs["L0_attn_x"], dtype=np.float32))
    per_core_common = {
        "wq": np.ascontiguousarray(inputs["L0_attn_Wq"], dtype=np.float32),
        "wk": np.ascontiguousarray(inputs["L0_attn_Wk"], dtype=np.float32),
        "wv": np.ascontiguousarray(inputs["L0_attn_Wv"], dtype=np.float32),
        "wo": np.ascontiguousarray(inputs["L0_attn_Wo"], dtype=np.float32),
        "bq": np.ascontiguousarray(inputs["L0_attn_bq"], dtype=np.float32),
        "bk": np.ascontiguousarray(inputs["L0_attn_bk"], dtype=np.float32),
        "bv": np.ascontiguousarray(inputs["L0_attn_bv"], dtype=np.float32),
        "bo": np.ascontiguousarray(inputs["L0_attn_bo"], dtype=np.float32),
        "stdp": np.ascontiguousarray(inputs["L0_attn_stdp"], dtype=np.float32),
        "wg": np.ascontiguousarray(inputs["L0_ffn_w_gate"], dtype=np.float32),
        "wu": np.ascontiguousarray(inputs["L0_ffn_w_up"], dtype=np.float32),
        "wd": np.ascontiguousarray(inputs["L0_ffn_w_down"], dtype=np.float32),
    }
    if "nc" not in _NC_CACHE:
        _NC_CACHE["nc"] = build_nc(stage="full")[0]
    nc = _NC_CACHE["nc"]
    in_maps = []
    for i in range(N_CORES):
        m = dict(per_core_common)
        m["x"] = np.ascontiguousarray(x[i])
        in_maps.append(m)
    res = run_bass_kernel_spmd(nc, in_maps, core_ids=list(range(N_CORES)))
    out = np.stack([np.asarray(res.results[i]["out"]) for i in range(N_CORES)],
                   axis=0)
    return out.astype(np.float32)
